# version 05
# baseline (speedup 1.0000x reference)
import math
import numpy as np
import ml_dtypes
from contextlib import ExitStack

import concourse.bass as bass
import concourse.mybir as mybir
from concourse.bass_utils import run_bass_kernel_spmd

F32 = mybir.dt.float32
BF16 = mybir.dt.bfloat16
I32 = mybir.dt.int32
AF = mybir.ActivationFunctionType
ALU = mybir.AluOpType
AX = mybir.AxisListType

D = 2048
NH = 16
HID = 1408
NHC = 11
NKC = 16
NE = 34
NR = 32
TOPK = 6
EPS = 1e-5
CAP = 512
ROWW = 2176
NCORES = 8


def default_cfg():
    return dict(n_st=4, n_exp=NE)


class Prog:
    ENGINES = ("sync", "tensor", "scalar", "vector", "gpsimd")

    def __init__(self):
        self.ops = []
        self.cnt = {}

    def n(self, sem):
        return self.cnt.get(sem, 0)

    CHAIN = {"scalar": "ev", "vector": "vv"}

    def op(self, eng, fn=None, waits=(), incs=(), chain=True):
        assert eng in self.ENGINES
        incs = list(incs)
        waits = list(waits)
        csem = self.CHAIN.get(eng)
        if fn is not None and csem is not None and chain:
            waits.append((csem, self.n(csem)))
            if not any(s == csem for s, _ in incs):
                incs.append((csem, 1))
        ws = {}
        for s, v in waits:
            if v > 0:
                assert v <= self.n(s), f"wait {s}>={v} but only {self.n(s)}"
                ws[s] = max(ws.get(s, 0), v)
        if incs:
            assert fn is not None, "incs must ride a real instruction"
        assert len(incs) <= 2
        self.ops.append((eng, fn, sorted(ws.items()), incs))
        for s, k in incs:
            self.cnt[s] = self.n(s) + k

    def barrier(self):
        snap = [(s, v) for s, v in self.cnt.items() if v > 0]
        for eng in self.ENGINES:
            self.op(eng, None, waits=snap)

    def emit(self, nc, sems):
        ops = self.ops

        def mk(eng_name):
            def inner(eng):
                for (e, fn, waits, incs) in ops:
                    if e != eng_name:
                        continue
                    for s, v in waits:
                        eng.wait_ge(sems[s], v)
                    inst = fn(eng) if fn is not None else None
                    for s, k in incs:
                        inst.then_inc(sems[s], k)
            return inner

        with nc.Block() as block:
            block.sync(mk("sync"))
            block.tensor(mk("tensor"))
            block.scalar(mk("scalar"))
            block.vector(mk("vector"))
            block.gpsimd(mk("gpsimd"))


class Slots:

    def __init__(self, n):
        self.free = {i: [] for i in range(n)}

    def wait(self, i):
        return list(self.free[i])

    def set(self, i, *conds):
        self.free[i] = [c for c in conds if c[1] > 0]

    def add(self, i, cond):
        if cond[1] > 0:
            self.free[i].append(cond)


def build_nc(cfg):
    n_st = cfg["n_st"]
    n_exp = cfg["n_exp"]
    nr_red = n_exp - 2
    T = 512 * n_st
    NTC = T // 128

    lam = np.exp(-(2.0 ** (-0.5)) ** (np.arange(1, NH + 1, dtype=np.float64)))

    nc = bass.Bass("TRN2", target_bir_lowering=False)

    def din(name, shape, dt=F32):
        return nc.dram_tensor(name, shape, dt, kind="ExternalInput")

    x_ext = din("x", [T, D])
    wqkvT = din("wqkvT", [D, 3 * D])
    woutT = din("woutT", [D, D])
    w1_all = din("w1_all", [n_exp, D, HID])
    w2_all = din("w2_all", [n_exp, HID, D])
    routerw = din("routerw", [D, NR])
    bq_row = din("bq_row", [1, 3 * D], BF16)
    bqcol_in = din("bqcol", [128, NKC])
    bout_row = din("bout_row", [1, D], BF16)
    routerb_row = din("routerb_row", [1, NR], BF16)
    b2g_in = din("b2g", [n_exp, D], BF16)
    b1cols_in = din("b1cols", [128, n_exp * NHC])
    ident_bf_in = din("ident_bf", [128, 128], BF16)
    dmaskT_in = din("dmaskT", [128, NH * 128])
    decayq_in = din("decayq", [128, NH * 128])
    dkx_in = din("dkx", [128, NH * 4])
    dk1_in = din("dk1", [128, NH])
    sinw_in = din("sinw", [128, NCORES * NH])
    triu_in = din("triu", [128, 128], BF16)
    eCrow_in = din("eCrow", [128, NR])
    trashb_in = din("trashb", [128, NR])
    tokid_in = din("tokid", [128, NTC])

    out_ext = nc.dram_tensor("out", [T, D], F32, kind="ExternalOutput")

    qsc = nc.dram_tensor("qsc", [D, T], BF16)
    ksc = nc.dram_tensor("ksc", [T, D], BF16)
    vsc = nc.dram_tensor("vsc", [T, D], BF16)
    xhsc = nc.dram_tensor("xhsc", [D, T], BF16)
    xg_rows = nr_red * CAP + NTC * NR + 128
    xg_dram = nc.dram_tensor("xg_dram", [xg_rows, ROWW], BF16)
    cc_in = nc.dram_tensor("cc_in", [128, NH * 128], F32)
    from concourse.replica_groups import (
        maybe_share_collective_output_space, ReplicaGroups)
    rg = [list(range(NCORES))]
    aspace = maybe_share_collective_output_space("AllGather", ReplicaGroups(rg))
    cc_out = nc.dram_tensor("cc_out", [NCORES * 128, NH * 128], F32,
                            addr_space=aspace)

    p = Prog()
    es = ExitStack()
    with es:
        def sb(name, shape, dt=F32):
            return es.enter_context(nc.sbuf_tensor(name, shape, dt))

        psf = [es.enter_context(nc.psum_tensor(f"psf{i}", [128, 512], F32))
               for i in range(6)]
        psb = [es.enter_context(nc.psum_tensor(f"psb{i}", [128, 1024], BF16))
               for i in range(2)]
        sem_names = ["dc", "dg", "sx", "sq", "sk", "sv", "sm", "sg2",
                     "sxq0", "sxq1", "dwa", "dwb", "dw1", "dw2",
                     "ds", "dsx", "da", "cc", "pe", "ev", "vv", "gp",
                     "dz", "dsc", "sgx0", "sgx1", "sg2b"]
        sems = {s: es.enter_context(nc.semaphore(s)) for s in sem_names}

        ident_bf = sb("ident_bf_s", [128, 128], BF16)
        ones_bf = sb("ones_bf", [1, 128], BF16)
        rbr = sb("rbr", [1, NR], BF16)
        rw_bf = sb("rw_bf", [128, NKC * NR], BF16)
        gates_fm = sb("gates_fm", [n_exp, T], BF16)
        triu_sb = sb("triu_sb", [128, 128], BF16)
        eCrow = sb("eCrow_s", [128, NR])
        trashb = sb("trashb_s", [128, NR])
        tokid = sb("tokid_s", [128, NTC])
        onesc_bf = sb("onesc_bf", [128, 1], BF16)
        zr = sb("zr", [128, ROWW], BF16)
        offs_full = sb("offs_full", [128, NR])
        gtmp = sb("gtmp", [128, NR])
        ghif = sb("ghif", [128, NR])
        mk_f = sb("mk_f", [128, NR])
        mk_b = sb("mk_b", [128, NR], BF16)
        ov_t = sb("ov_t", [128, NR])
        vd_t = sb("vd_t", [128, NR])
        pe_t = sb("pe_t", [128, NR])
        tr_t = sb("tr_t", [128, NR])
        of_t = sb("of_t", [128, NR])
        cntb = sb("cntb", [1, NR], BF16)
        tidf = sb("tidf", [128, 1])
        stat = sb("stat", [128, 64])
        e_buf = sb("e_buf", [128, NR])
        m8 = sb("m8", [128, 8])
        erem = sb("erem", [128, NR])
        gacc = sb("gacc", [128, NR])
        gbf_tm = sb("gbf_tm", [128, NR], BF16)
        sq_dump = sb("sq_dump", [128, D], BF16)

        fbank = Slots(6)
        bbank = Slots(2)
        tr_par = [0]

        def dma_sync(dst, src, waits=(), sem="dc"):
            w = list(waits) + ([(sem, p.n(sem))] if sem == "dc" else [])
            p.op("sync", lambda e, d=dst, s=src: e.dma_start(out=d, in_=s),
                 waits=w, incs=[(sem, 16)])
            return p.n(sem)

        def dma_gp(dst, src, waits=(), sem="dg"):
            w = list(waits) + ([(sem, p.n(sem))] if sem == "dg" else [])
            p.op("gpsimd", lambda e, d=dst, s=src: e.dma_start(out=d, in_=s),
                 waits=w, incs=[(sem, 16)])
            return p.n(sem)

        dma_sync(eCrow[:, :], eCrow_in[:, :])
        dma_sync(trashb[:, :], trashb_in[:, :])
        dma_sync(tokid[:, :], tokid_in[:, :])
        dma_sync(triu_sb[:, :], triu_in[:, :])
        dma_gp(ident_bf[:, :], ident_bf_in[:, :])
        dma_gp(rbr[:, :], routerb_row[:, :])
        dma_gp(rw_bf[:, :].rearrange("p (kc e) -> p kc e", kc=NKC),
               routerw.rearrange("(kc p) e -> p kc e", p=128))
        p.op("gpsimd", lambda e: e.memset(ones_bf[:, :], 1.0), incs=[("gp", 1)])
        p.op("gpsimd", lambda e: e.memset(onesc_bf[:, :], 1.0), incs=[("gp", 1)])
        p.op("gpsimd", lambda e: e.memset(zr[:, :], 0.0), incs=[("gp", 1)])
        zr_init = p.n("gp")
        p.op("vector", lambda e: e.memset(offs_full[:, :], 0.0), incs=[("vv", 1)])
        for r in range(nr_red * CAP // 128):
            p.op("sync", lambda e, rr=r: e.dma_start(
                out=xg_dram[rr * 128:(rr + 1) * 128, :], in_=zr[:, :]),
                 waits=[("gp", zr_init)] if r == 0 else [], incs=[("dz", 16)])
        dz_all = p.n("dz")
        p.op("vector", lambda e: e.memset(stat[:, 51:52], EPS), incs=[("vv", 1)])
        eps_init = p.n("vv")
        setup_dx = p.n("dc")
        dc_all = [setup_dx]
        setup_dw = p.n("dg")
        setup_gp = p.n("gp")
        dg_all = [setup_dw]

        ev_par = [0]

        def transpose_128(src_ap, dst_ap, extra_waits=()):
            b = tr_par[0]
            tr_par[0] ^= 1
            p.op("tensor",
                 lambda e, s=src_ap, bb=b: e.transpose(
                     psb[bb][:, 0:128], s, ident_bf[:, :]),
                 waits=bbank.wait(b) + list(extra_waits) +
                 [("dg", dg_all[0])], incs=[("pe", 1)])
            mm = p.n("pe")
            ev_par[0] ^= 1
            if ev_par[0]:
                p.op("scalar",
                     lambda e, d=dst_ap, bb=b: e.copy(d, psb[bb][:, 0:128]),
                     waits=[("pe", mm)], incs=[("ev", 1)])
                bbank.set(b, ("ev", p.n("ev")))
            else:
                p.op("vector",
                     lambda e, d=dst_ap, bb=b: e.tensor_copy(
                         d, psb[bb][:, 0:128]),
                     waits=[("pe", mm)], incs=[("vv", 1)])
                bbank.set(b, ("vv", p.n("vv")))

        xhat_tm_ref = [None]

        def ln_tm(src_tile_fn, n_tc, dst_fn, first_waits, ts_dst_fn=None,
                  ts_waits_fn=None):
            xhat_tm = xhat_tm_ref[0]
            c_sum = stat[:, 0:0 + n_tc]
            c_sq = stat[:, 8:8 + n_tc]
            c_m = stat[:, 16:16 + n_tc]
            c_rstd = stat[:, 24:24 + n_tc]
            c_nmr = stat[:, 32:32 + n_tc]
            c_t0 = stat[:, 40:40 + n_tc]
            for tc in range(n_tc):
                src = src_tile_fn(tc)
                w = (list(first_waits) + [("vv", p.n("vv"))]) if tc == 0 else []
                p.op("scalar",
                     lambda e, s=src, c=c_sq[:, tc:tc + 1]: e.activation(
                         sq_dump[:, :], s, AF.Square, accum_out=c),
                     waits=w + [("ev", p.n("ev"))], incs=[("ev", 1)])
                p.op("scalar",
                     lambda e, s=src, c=c_sum[:, tc:tc + 1]: e.activation(
                         sq_dump[:, :], s, AF.Copy, accum_out=c),
                     waits=[("ev", p.n("ev"))], incs=[("ev", 1)])
            act_done = p.n("ev")
            p.op("vector", lambda e: e.tensor_scalar_mul(c_m, c_sum, 1.0 / D),
                 waits=[("ev", act_done)])
            p.op("vector", lambda e: e.tensor_scalar_mul(c_t0, c_sq, 1.0 / D))
            p.op("vector", lambda e: e.tensor_tensor(c_rstd, c_m, c_m, ALU.mult))
            p.op("vector",
                 lambda e: e.tensor_tensor(c_t0, c_t0, c_rstd, ALU.subtract),
                 incs=[("vv", 1)])
            var_done = p.n("vv")
            p.op("scalar", lambda e: e.activation(c_t0, c_t0, AF.Sqrt,
                                                  bias=stat[:, 51:52]),
                 waits=[("vv", var_done)], incs=[("ev", 1)])
            sd_done = p.n("ev")
            p.op("vector", lambda e: e.reciprocal(c_rstd, c_t0),
                 waits=[("ev", sd_done)])
            p.op("vector", lambda e: e.scalar_tensor_tensor(
                c_nmr, c_m, -1.0, c_rstd, ALU.mult, ALU.mult),
                 incs=[("vv", 1)])
            stats_done = p.n("vv")
            for tc in range(n_tc):
                src = src_tile_fn(tc)
                tdst = ts_dst_fn(tc) if ts_dst_fn else xhat_tm[:, :]
                tw = (ts_waits_fn(tc) if ts_waits_fn else [])
                p.op("vector",
                     lambda e, s=src, r=c_rstd[:, tc:tc + 1],
                     nm=c_nmr[:, tc:tc + 1], xt=tdst:
                     e.tensor_scalar(xt, s, r, nm, ALU.mult, ALU.add),
                     waits=[("vv", stats_done), ("pe", p.n("pe"))] + tw,
                     incs=[("vv", 1)])
                xh = p.n("vv")
                for kc in range(NKC):
                    transpose_128(tdst[:, kc * 128:(kc + 1) * 128],
                                  dst_fn(tc, kc), [("vv", xh)])

        with ExitStack() as asco:
            def asb(name, shape, dt=F32):
                return asco.enter_context(nc.sbuf_tensor(name, shape, dt))

            sinw = asb("sinw_s", [128, NCORES * NH])
            S_delta = asb("S_delta", [128, NH * 128])
            S_state = S_delta
            x_buf = [asb("x_buf0", [128, 4 * D])]
            x_buf = [x_buf[0], x_buf[0]]
            fm_buf = asb("fm_buf", [128, NKC * 512], BF16)
            k_tm = asb("k_tm", [128, 4 * D], BF16)
            v_tm = asb("v_tm", [128, 4 * D], BF16)
            wsl = [asb(f"wsl{i}", [128, NKC * 512], BF16) for i in range(2)]
            gat_buf = None
            rt = [asb(f"rt{i}", [128, ROWW], BF16) for i in range(4)]
            oint = [asb(f"oint{i}", [128, NR], I32) for i in range(4)]
            for _i in range(4):
                p.op("vector", lambda e, i=_i: e.memset(
                    rt[i][:, D + 2 * NR + 2:ROWW], 0.0), incs=[("vv", 1)])


            consts_dx = dma_sync(sinw[:, :], sinw_in[:, :])
            dc_all[0] = p.n("dc")
            p.op("vector", lambda e: e.memset(S_delta[:, :], 0.0),
                 incs=[("vv", 1)])
            sdelta_init = p.n("vv")

            xload = {}
            xslot = Slots(1)

            def load_x(st):
                w = xslot.wait(0)
                xload[st] = dma_sync(
                    x_buf[st % 2][:, :].rearrange("p (tc d) -> p tc d", tc=4),
                    x_ext[st * 512:(st + 1) * 512, :].rearrange(
                        "(tc p) d -> p tc d", p=128), waits=w, sem="sx")

            load_x(0)

            wslot = Slots(2)
            kd_slot = Slots(2)
            rt_free = Slots(4)
            oint_free = Slots(4)

            ph1_scope = ExitStack()
            bqr = ph1_scope.enter_context(
                nc.sbuf_tensor("bqr", [1, 3 * D], BF16))
            dkx = ph1_scope.enter_context(
                nc.sbuf_tensor("dkx_s", [128, NH * 4], F32))
            kdb = ph1_scope.enter_context(
                nc.sbuf_tensor("kdb", [128, 2 * 128], BF16))
            qfm_buf = ph1_scope.enter_context(
                nc.sbuf_tensor("qfm_buf", [128, NKC * 512], BF16))
            q_tm = ph1_scope.enter_context(
                nc.sbuf_tensor("q_tm", [128, 4 * D], BF16))
            bqcol = ph1_scope.enter_context(
                nc.sbuf_tensor("bqcol_s", [128, NKC], F32))
            dma_gp(bqr[:, :], bq_row[:, :])
            setup_dw2 = p.n("dg")
            dg_all[0] = p.n("dg")
            dma_sync(dkx[:, :], dkx_in[:, :])
            dma_sync(bqcol[:, :], bqcol_in[:, :])
            dc_all[0] = p.n("dc")
            for st in range(n_st):
                xb = x_buf[st % 2]
                scst_prev = p.n("ds")
                ln_tm(lambda tc, b=xb: b[:, tc * D:(tc + 1) * D], 4,
                      lambda tc, kc: fm_buf[:, kc * 512 + tc * 128:
                                            kc * 512 + (tc + 1) * 128],
                      [("sx", xload[st])],
                      ts_dst_fn=lambda tc: rt[tc][:, 0:D],
                      ts_waits_fn=lambda tc: rt_free.wait(tc))
                for _tc in range(4):
                    rt_free.set(_tc, ("pe", p.n("pe")))
                xhat_done = p.n("ev")
                xhat_vv = p.n("vv")
                xslot.set(0, ("ev", xhat_done))

                first_evac = True
                for pi, (dst, fofs) in enumerate(
                        ((None, 0), (k_tm, D), (v_tm, 2 * D))):
                    for fsl in range(4):
                        slot = (pi * 4 + fsl) % 2
                        wsem = "dwa" if slot == 0 else "dwb"
                        wl = dma_gp(
                            wsl[slot][:, :].rearrange(
                                "p (kc f) -> p kc f", kc=NKC),
                            wqkvT[:, fofs + fsl * 512:
                                  fofs + (fsl + 1) * 512].rearrange(
                                "(kc p) f -> p kc f", p=128),
                            waits=wslot.wait(slot), sem=wsem)
                        if pi == 0:
                            for fo_i in range(4):
                                fc = fsl * 4 + fo_i
                                bank = fo_i % 4
                                bw = fbank.wait(bank)
                                for kc in range(NKC):
                                    p.op("tensor",
                                         lambda e, b=bank, k=kc, s=slot,
                                         fi=fo_i: e.matmul(
                                             psf[b][:, :],
                                             wsl[s][:, k * 512 + fi * 128:
                                                    k * 512 + (fi + 1) * 128],
                                             fm_buf[:, k * 512:(k + 1) * 512],
                                             start=(k == 0),
                                             stop=(k == NKC - 1)),
                                         waits=([("ev", xhat_done),
                                                 ("vv", xhat_vv),
                                                 (wsem, wl)] + bw)
                                         if kc == 0 else [],
                                         incs=([("pe", 1)]
                                               if kc == NKC - 1 else []))
                                grp = p.n("pe")
                                ev_w = [("pe", grp), ("dc", dc_all[0])]
                                if first_evac and st > 0:
                                    ev_w.append(("ds", scst_prev))
                                first_evac = False
                                p.op("scalar",
                                     lambda e, b=bank, c=fc: e.activation(
                                         qfm_buf[:, c * 512:(c + 1) * 512],
                                         psf[b][:, :], AF.Identity,
                                         bias=bqcol[:, c:c + 1]),
                                     waits=ev_w, incs=[("ev", 1)])
                                fbank.set(bank, ("ev", p.n("ev")))
                            wslot.set(slot, ("pe", p.n("pe")))
                            continue
                        for tc in range(4):
                            bank = tc % 4
                            bw = fbank.wait(bank)
                            for kc in range(NKC):
                                p.op("tensor",
                                     lambda e, b=bank, k=kc, t=tc, s=slot:
                                     e.matmul(
                                         psf[b][:, :],
                                         fm_buf[:, k * 512 + t * 128:
                                                k * 512 + (t + 1) * 128],
                                         wsl[s][:, k * 512:(k + 1) * 512],
                                         start=(k == 0), stop=False),
                                     waits=([("ev", xhat_done),
                                             ("vv", xhat_vv),
                                             (wsem, wl)] + bw)
                                     if kc == 0 else [])
                            p.op("tensor",
                                 lambda e, b=bank, fo=fofs, f=fsl: e.matmul(
                                     psf[b][:, :], ones_bf[0:1, 0:128],
                                     bqr[0:1, fo + f * 512:fo + (f + 1) * 512],
                                     start=False, stop=True),
                                 waits=[("dg", dg_all[0]), ("gp", setup_gp)],
                                 incs=[("pe", 1)])
                            grp = p.n("pe")
                            ev_w = [("pe", grp)]
                            if first_evac and st > 0:
                                ev_w.append(("ds", scst_prev))
                            first_evac = False
                            p.op("scalar",
                                 lambda e, d=dst, b=bank, t=tc, f=fsl: e.copy(
                                     d[:, t * D + f * 512:t * D + (f + 1) * 512],
                                     psf[b][:, :]),
                                 waits=ev_w, incs=[("ev", 1)])
                            fbank.set(bank, ("ev", p.n("ev")))
                        wslot.set(slot, ("pe", p.n("pe")))
                qkv_ev = p.n("ev")

                dma_sync(
                    qsc[:, st * 512:(st + 1) * 512].rearrange(
                        "(hc p) t -> p hc t", p=128),
                    qfm_buf[:, :].rearrange("p (hc t) -> p hc t", hc=NKC),
                    waits=[("ev", qkv_ev)], sem="ds")
                for (src, dstt) in ((k_tm, ksc), (v_tm, vsc)):
                    dma_sync(
                        dstt[st * 512:(st + 1) * 512, :].rearrange(
                            "(tc p) d -> p tc d", p=128),
                        src[:, :].rearrange("p (tc d) -> p tc d", tc=4),
                        waits=[("ev", qkv_ev)], sem="ds")

                for h in range(NH):
                    bank = 4 + h % 2
                    bw = fbank.wait(bank)
                    for cc in range(4):
                        sl = (h * 4 + cc) % 2
                        p.op("scalar",
                             lambda e, hh=h, c=cc, s=sl: e.activation(
                                 kdb[:, s * 128:(s + 1) * 128],
                                 k_tm[:, c * D + hh * 128:
                                      c * D + (hh + 1) * 128],
                                 AF.Copy,
                                 scale=dkx[:, hh * 4 + c:hh * 4 + c + 1]),
                             waits=[("ev", qkv_ev), ("dc", dc_all[0])] +
                             kd_slot.wait(sl),
                             incs=[("ev", 1)])
                        kd_ok = p.n("ev")
                        p.op("tensor",
                             lambda e, hh=h, c=cc, b=bank, s=sl: e.matmul(
                                 psf[b][:, 0:128],
                                 kdb[:, s * 128:(s + 1) * 128],
                                 v_tm[:, c * D + hh * 128:
                                      c * D + (hh + 1) * 128],
                                 start=(c == 0), stop=(c == 3)),
                             waits=[("ev", kd_ok)] + (bw if cc == 0 else []),
                             incs=[("pe", 1)])
                        kd_slot.set(sl, ("pe", p.n("pe")))
                    grp = p.n("pe")
                    l512 = float(np.float32(lam[h] ** 512))
                    p.op("vector",
                         lambda e, hh=h, b=bank, l=l512: e.scalar_tensor_tensor(
                             S_delta[:, hh * 128:(hh + 1) * 128],
                             S_delta[:, hh * 128:(hh + 1) * 128], l,
                             psf[b][:, 0:128], ALU.mult, ALU.add),
                         waits=[("pe", grp), ("vv", sdelta_init)],
                         incs=[("vv", 1)])
                    fbank.set(bank, ("vv", p.n("vv")))

                if st + 1 < n_st:
                    load_x(st + 1)

            ph1_vv = p.n("vv")
            dma_sync(cc_in[:, :], S_delta[:, :], waits=[("vv", ph1_vv)],
                     sem="ds")
            sdel_st = p.n("ds")
            p.op("gpsimd", lambda e: e.collective_compute(
                "AllGather", ALU.bypass, replica_groups=rg,
                ins=[cc_in.ap().opt()], outs=[cc_out.ap().opt()]),
                 waits=[("ds", sdel_st)], incs=[("cc", 1)])
            p.op("vector", lambda e: e.memset(S_state[:, :], 0.0),
                 waits=[("ds", sdel_st)], incs=[("vv", 1)])
            gslot = Slots(2)
            gb0 = x_buf[0][:, 0:NH * 128]
            gb1 = x_buf[0][:, NH * 128:2 * NH * 128]
            for r in range(NCORES):
                gb = gb0 if r % 2 == 0 else gb1
                gsm = "sg2" if r % 2 == 0 else "sg2b"
                gw = [("cc", 1)] + gslot.wait(r % 2)
                if r == 0:
                    gw += xslot.wait(0)
                gl = dma_sync(gb, cc_out[r * 128:(r + 1) * 128, :],
                              waits=gw, sem=gsm)
                for h in range(NH):
                    p.op("vector",
                         lambda e, b=gb, hh=h, rr=r:
                         e.scalar_tensor_tensor(
                             S_state[:, hh * 128:(hh + 1) * 128],
                             b[:, hh * 128:(hh + 1) * 128],
                             sinw[:, rr * NH + hh:rr * NH + hh + 1],
                             S_state[:, hh * 128:(hh + 1) * 128],
                             ALU.mult, ALU.add),
                         waits=[(gsm, gl)] if h == 0 else [],
                         incs=([("vv", 1)] if h == NH - 1 else []))
                gslot.set(r % 2, ("vv", p.n("vv")))
            sin_done = p.n("vv")
            xslot.set(0, ("vv", sin_done))

            ph1_scope.close()
            ph2_scope = ExitStack()

            def p2sb(name, shape, dt=BF16):
                return ph2_scope.enter_context(nc.sbuf_tensor(name, shape, dt))

            boutr = p2sb("boutr", [1, D])
            dmaskT = p2sb("dmaskT_s", [128, NH * 128], F32)
            decayq = p2sb("decayq_s", [128, NH * 128], F32)
            dk1 = p2sb("dk1_s", [128, NH], F32)
            qf_st = p2sb("qf_st", [128, NH * 512])
            kf16 = p2sb("kf16", [128, NH * 128])
            smb16 = p2sb("smb16", [128, NH * 128])
            sbf16 = p2sb("sbf16", [128, NH * 128])
            dma_gp(boutr[:, :], bout_row[:, :])
            dw2b = p.n("dg")
            dma_sync(dmaskT[:, :], dmaskT_in[:, :])
            dma_sync(decayq[:, :], decayq_in[:, :])
            ph2_dc = dma_sync(dk1[:, :], dk1_in[:, :])
            qkv_load = {}
            xres_load = {}

            qkv_free = [("pe", 0)]

            def load_qkv(st):
                w0 = [("ds", p.n("ds")), qkv_free[0]]
                vals = [dma_sync(
                    qf_st[:, :].rearrange("p (hc t) -> p hc t", hc=NH),
                    qsc[:, st * 512:(st + 1) * 512].rearrange(
                        "(hc p) t -> p hc t", p=128),
                    waits=w0, sem="sq")]
                w0 = [qkv_free[0]]
                for (srct, dst, sname) in ((ksc, k_tm, "sk"),
                                           (vsc, v_tm, "sv")):
                    vals.append(dma_sync(
                        dst[:, :].rearrange("p (tc d) -> p tc d", tc=4),
                        srct[st * 512:(st + 1) * 512, :].rearrange(
                            "(tc p) d -> p tc d", p=128),
                        waits=w0, sem=sname))
                qkv_load[st] = vals

            def load_xres(st):
                xres_load[st] = dma_sync(
                    x_buf[st % 2][:, :].rearrange("p (tc d) -> p tc d", tc=4),
                    x_ext[st * 512:(st + 1) * 512, :].rearrange(
                        "(tc p) d -> p tc d", p=128),
                    waits=xslot.wait(0), sem="sx")

            qf_free = [[]]
            kf_free = [[]]
            smb_free = [[]]
            supd_prev = [0]
            inter_prev = [0]
            xh_stored = {}
            from concourse.bass_core_rust import IndirectOffsetOnAxis
            p.op("gpsimd", lambda e: e.memset(gates_fm[:, :], 0.5),
                 incs=[("gp", 1)])
            gfm_init = p.n("gp")

            for st in range(n_st):
                if st == 0:
                    load_qkv(0)
                    load_xres(0)
                ldq, ldk, ldv = qkv_load[st]
                first_ch = True
                for cc in range(4):
                    k_ev = {}
                    k_vv = {}
                    for h in range(NH):
                        bb = tr_par[0]; tr_par[0] ^= 1
                        w0 = list(bbank.wait(bb))
                        if h == 0 and cc == 0:
                            w0 += [("sq", ldq), ("sk", ldk), ("sv", ldv)]
                        if h == 0:
                            w0 += kf_free[0]
                        if first_ch:
                            w0 += [("vv", sin_done)]
                            first_ch = False
                        p.op("tensor", lambda e, c=cc, hh=h, b=bb: e.transpose(
                            psb[b][:, 0:128],
                            k_tm[:, c * D + hh * 128:c * D + (hh + 1) * 128],
                            ident_bf[:, :]), waits=w0, incs=[("pe", 1)])
                        if h % 2 == 0:
                            p.op("scalar", lambda e, hh=h, b=bb: e.copy(
                                kf16[:, hh * 128:(hh + 1) * 128],
                                psb[b][:, 0:128]),
                                 waits=[("pe", p.n("pe"))], incs=[("ev", 1)])
                            bbank.set(bb, ("ev", p.n("ev")))
                            k_ev[h] = p.n("ev")
                            k_vv[h] = 0
                        else:
                            p.op("vector", lambda e, hh=h, b=bb: e.tensor_copy(
                                kf16[:, hh * 128:(hh + 1) * 128],
                                psb[b][:, 0:128]),
                                 waits=[("pe", p.n("pe"))], incs=[("vv", 1)])
                            bbank.set(bb, ("vv", p.n("vv")))
                            k_ev[h] = 0
                            k_vv[h] = p.n("vv")
                    mask_vv = {}
                    s_mm = {}
                    for h in range(NH):
                        sbank = h % 2
                        kw = ([("ev", k_ev[h])] if k_ev[h] else
                              [("vv", k_vv[h])])
                        if h == 0:
                            kw += [("sq", ldq)] + qf_free[0]
                        p.op("tensor", lambda e, c=cc, hh=h, b=sbank: e.matmul(
                            psf[b][:, 0:128],
                            kf16[:, hh * 128:(hh + 1) * 128],
                            qf_st[:, hh * 512 + c * 128:
                                  hh * 512 + (c + 1) * 128],
                            start=True, stop=True),
                             waits=kw + fbank.wait(sbank),
                             incs=[("pe", 1)])
                        s_mm[h] = p.n("pe")
                        mw = [("pe", s_mm[h])]
                        if h == 0:
                            mw += smb_free[0] + [("dc", ph2_dc)]
                        p.op("vector", lambda e, hh=h, b=sbank: e.tensor_tensor(
                            smb16[:, hh * 128:(hh + 1) * 128],
                            psf[b][:, 0:128],
                            dmaskT[:, hh * 128:(hh + 1) * 128], ALU.mult),
                             waits=mw, incs=[("vv", 1)])
                        mask_vv[h] = p.n("vv")
                        fbank.set(sbank, ("vv", mask_vv[h]))
                    kf_free[0] = [("pe", s_mm[NH - 1])]
                    p.op("vector", lambda e, c=cc: e.tensor_tensor(
                        qf_st[:, :].rearrange("p (h t) -> p h t", h=NH)[
                            :, :, c * 128:(c + 1) * 128],
                        qf_st[:, :].rearrange("p (h t) -> p h t", h=NH)[
                            :, :, c * 128:(c + 1) * 128],
                        decayq[:, :].rearrange("p (h t) -> p h t", h=NH),
                        ALU.mult),
                         waits=[("pe", s_mm[NH - 1]), ("dc", ph2_dc)],
                         incs=[("vv", 1)])
                    qt_ok = p.n("vv")
                    p.op("scalar", lambda e: e.copy(sbf16[:, :], S_state[:, :]),
                         waits=[("vv", supd_prev[0]), ("pe", inter_prev[0])],
                         incs=[("ev", 1)])
                    snap_ok = p.n("ev")
                    o_mm = {}
                    for h in range(NH):
                        ob = 2 + h % 2
                        p.op("tensor", lambda e, c=cc, hh=h, b=ob: e.matmul(
                            psf[b][:, 0:128],
                            v_tm[:, c * D + hh * 128:c * D + (hh + 1) * 128],
                            smb16[:, hh * 128:(hh + 1) * 128],
                            start=True, stop=False),
                             waits=[("vv", mask_vv[h])] + fbank.wait(ob))
                        p.op("tensor", lambda e, c=cc, hh=h, b=ob: e.matmul(
                            psf[b][:, 0:128],
                            sbf16[:, hh * 128:(hh + 1) * 128],
                            qf_st[:, hh * 512 + c * 128:
                                  hh * 512 + (c + 1) * 128],
                            start=False, stop=True),
                             waits=[("vv", qt_ok), ("ev", snap_ok)]
                             if h == 0 else [],
                             incs=[("pe", 1)])
                        o_mm[h] = p.n("pe")
                        evw = [("pe", o_mm[h])]
                        if cc == 0 and h == 0:
                            evw.append(("dsx", p.n("dsx")))
                        p.op("scalar", lambda e, c=cc, hh=h, b=ob: e.copy(
                            fm_buf[:, hh * 512 + c * 128:
                                   hh * 512 + (c + 1) * 128],
                            psf[b][:, 0:128]),
                             waits=evw, incs=[("ev", 1)])
                        fbank.set(ob, ("ev", p.n("ev")))
                    inter_prev[0] = o_mm[NH - 1]
                    qf_free[0] = [("pe", o_mm[NH - 1])]
                    last_ds = 0
                    for h in range(NH):
                        p.op("scalar", lambda e, c=cc, hh=h: e.activation(
                            smb16[:, hh * 128:(hh + 1) * 128],
                            k_tm[:, c * D + hh * 128:c * D + (hh + 1) * 128],
                            AF.Copy, scale=dk1[:, hh:hh + 1]),
                             waits=[("pe", o_mm[h])], incs=[("ev", 1)])
                        kd_ok = p.n("ev")
                        db = 4 + h % 2
                        p.op("tensor", lambda e, c=cc, hh=h, b=db: e.matmul(
                            psf[b][:, 0:128],
                            smb16[:, hh * 128:(hh + 1) * 128],
                            v_tm[:, c * D + hh * 128:c * D + (hh + 1) * 128],
                            start=True, stop=True),
                             waits=[("ev", kd_ok)] + fbank.wait(db),
                             incs=[("pe", 1)])
                        ds_mm = p.n("pe")
                        last_ds = ds_mm
                        l128 = float(np.float32(lam[h] ** 128))
                        p.op("vector",
                             lambda e, hh=h, b=db, l=l128:
                             e.scalar_tensor_tensor(
                                 S_state[:, hh * 128:(hh + 1) * 128],
                                 S_state[:, hh * 128:(hh + 1) * 128], l,
                                 psf[b][:, 0:128], ALU.mult, ALU.add),
                             waits=[("pe", ds_mm), ("ev", snap_ok)],
                             incs=[("vv", 1)])
                        fbank.set(db, ("vv", p.n("vv")))
                    supd_prev[0] = p.n("vv")
                    smb_free[0] = [("pe", last_ds)]
                attn_ev = p.n("ev")
                qkv_free[0] = ("pe", p.n("pe"))
                if st + 1 < n_st:
                    load_qkv(st + 1)

                for fsl in range(4):
                    slot = fsl % 2
                    wsem = "dwa" if slot == 0 else "dwb"
                    wl = dma_gp(
                        wsl[slot][:, :].rearrange("p (kc f) -> p kc f", kc=NKC),
                        woutT[:, fsl * 512:(fsl + 1) * 512].rearrange(
                            "(kc p) f -> p kc f", p=128),
                        waits=wslot.wait(slot), sem=wsem)
                    for tc in range(4):
                        bank = tc % 4
                        bw = fbank.wait(bank)
                        for dc in range(NKC):
                            p.op("tensor",
                                 lambda e, b=bank, d=dc, t=tc, s=slot:
                                 e.matmul(
                                     psf[b][:, :],
                                     fm_buf[:, d * 512 + t * 128:
                                            d * 512 + (t + 1) * 128],
                                     wsl[s][:, d * 512:(d + 1) * 512],
                                     start=(d == 0), stop=False),
                                 waits=([("ev", attn_ev), (wsem, wl)] + bw)
                                 if dc == 0 else [])
                        p.op("tensor", lambda e, b=bank, f=fsl: e.matmul(
                            psf[b][:, :], ones_bf[0:1, 0:128],
                            boutr[0:1, f * 512:(f + 1) * 512],
                            start=False, stop=True),
                             waits=[("dg", dw2b)], incs=[("pe", 1)])
                        grp = p.n("pe")
                        p.op("vector",
                             lambda e, b=bank, t=tc, f=fsl, xb=x_buf[st % 2]:
                             e.tensor_tensor(
                                 xb[:, t * D + f * 512:t * D + (f + 1) * 512],
                                 psf[b][:, :],
                                 xb[:, t * D + f * 512:t * D + (f + 1) * 512],
                                 ALU.add),
                             waits=[("pe", grp), ("sx", xres_load[st])],
                             incs=[("vv", 1)])
                        fbank.set(bank, ("vv", p.n("vv")))
                    wslot.set(slot, ("pe", p.n("pe")))
                res_vv = p.n("vv")
                dma_sync(
                    out_ext[st * 512:(st + 1) * 512, :].rearrange(
                        "(tc p) d -> p tc d", p=128),
                    x_buf[st % 2][:, :].rearrange("p (tc d) -> p tc d", tc=4),
                    waits=[("vv", res_vv)], sem="ds")
                st_store = p.n("ds")

                ln_tm(lambda tc, b=x_buf[st % 2]: b[:, tc * D:(tc + 1) * D], 4,
                      lambda tc, kc: fm_buf[:, kc * 512 + tc * 128:
                                            kc * 512 + (tc + 1) * 128],
                      [("vv", res_vv)],
                      ts_dst_fn=lambda tc: rt[tc][:, 0:D],
                      ts_waits_fn=lambda tc: rt_free.wait(tc))
                xh_done = p.n("ev")
                ln_vv = p.n("vv")
                xh_stored[st] = dma_sync(
                    xhsc[:, st * 512:(st + 1) * 512].rearrange(
                        "(kc p) t -> p kc t", p=128),
                    fm_buf[:, :].rearrange("p (kc t) -> p kc t", kc=NKC),
                    waits=[("ev", xh_done), ("vv", ln_vv),
                           ("dsx", p.n("dsx"))], sem="dsx")
                xslot.set(0, ("ds", st_store), ("vv", ln_vv),
                          ("ev", xh_done))
                if st + 1 < n_st:
                    load_xres(st + 1)

                for tc in range(4):
                    tg = st * 4 + tc
                    b_r = tc % 2
                    bw = fbank.wait(b_r)
                    for kc in range(NKC):
                        p.op("tensor",
                             lambda e, b=b_r, k=kc, t=tc: e.matmul(
                                 psf[b][:, 0:NR],
                                 fm_buf[:, k * 512 + t * 128:
                                        k * 512 + (t + 1) * 128],
                                 rw_bf[:, k * NR:(k + 1) * NR],
                                 start=(k == 0), stop=False),
                             waits=([("ev", xh_done), ("vv", ln_vv),
                                     ("dg", dg_all[0])] + bw)
                             if kc == 0 else [])
                    p.op("tensor", lambda e, b=b_r: e.matmul(
                        psf[b][:, 0:NR], ones_bf[0:1, 0:128], rbr[0:1, :],
                        start=False, stop=True), incs=[("pe", 1)])
                    rt_mm = p.n("pe")
                    p.op("vector", lambda e, b=b_r: e.reduce_max(
                        stat[:, 48:49], psf[b][:, 0:NR], axis=AX.X,
                        negate=True),
                         waits=[("pe", rt_mm)], incs=[("vv", 1)])
                    nm_ok = p.n("vv")
                    p.op("scalar", lambda e, b=b_r: e.activation(
                        e_buf[:, :], psf[b][:, 0:NR], AF.Exp,
                        bias=stat[:, 48:49]),
                         waits=[("vv", nm_ok)], incs=[("ev", 1)])
                    fbank.set(b_r, ("ev", p.n("ev")))
                    exp_ok = p.n("ev")
                    p.op("vector", lambda e: e.max(out=m8[:, :],
                                                   in_=e_buf[:, :]),
                         waits=[("ev", exp_ok)], incs=[("vv", 1)])
                    p.op("vector", lambda e: e.memset(m8[:, TOPK:8], 0.0),
                         incs=[("vv", 1)])
                    p.op("vector", lambda e: e.match_replace(
                        out=erem[:, :], in_to_replace=m8[:, :],
                        in_values=e_buf[:, :], imm_value=0.0),
                         incs=[("vv", 1)])
                    p.op("vector", lambda e: e.tensor_tensor(
                        gacc[:, :], e_buf[:, :], erem[:, :], ALU.subtract),
                         incs=[("vv", 1)])
                    p.op("vector", lambda e: e.reduce_sum(
                        stat[:, 49:50], gacc[:, :], axis=AX.X),
                         incs=[("vv", 1)])
                    p.op("vector", lambda e: e.reciprocal(
                        stat[:, 50:51], stat[:, 49:50]),
                         incs=[("vv", 1)])
                    p.op("vector", lambda e: e.tensor_scalar_mul(
                        gtmp[:, :], gacc[:, :], stat[:, 50:51]),
                         incs=[("vv", 1)])
                    p.op("vector", lambda e: e.tensor_copy(
                        gbf_tm[:, 0:NR], gtmp[:, :]), incs=[("vv", 1)])
                    g_ok = p.n("vv")
                    b = tr_par[0]; tr_par[0] ^= 1
                    p.op("tensor", lambda e, bb=b: e.transpose(
                        psb[bb][0:NR, 0:128], gbf_tm[:, :], ident_bf[:, :]),
                         waits=[("vv", g_ok)] + bbank.wait(b),
                         incs=[("pe", 1)])
                    p.op("scalar", lambda e, bb=b, t=tg: e.copy(
                        gates_fm[0:nr_red, t * 128:(t + 1) * 128],
                        psb[bb][0:nr_red, 0:128]),
                         waits=[("pe", p.n("pe")), ("gp", gfm_init)],
                         incs=[("ev", 1)])
                    bbank.set(b, ("ev", p.n("ev")))
                    p.op("scalar", lambda e, t=tc: e.copy(
                        rt[t][:, D:D + NR], gtmp[:, :]),
                         waits=[("vv", g_ok)], incs=[("ev", 1)])
                    p.op("vector", lambda e, t=tc: e.tensor_copy(
                        ghif[:, :], rt[t][:, D:D + NR]),
                         waits=[("ev", p.n("ev"))], incs=[("vv", 1)])
                    p.op("vector", lambda e, t=tc: e.tensor_tensor(
                        rt[t][:, D + NR:D + 2 * NR], gtmp[:, :], ghif[:, :],
                        ALU.subtract), incs=[("vv", 1)])
                    p.op("scalar", lambda e, t=tc, g=tg: e.copy(
                        rt[t][:, D + 2 * NR:D + 2 * NR + 1],
                        tokid[:, g:g + 1]),
                         waits=[("dc", dc_all[0])], incs=[("ev", 1)])
                    p.op("vector", lambda e, t=tc: e.tensor_copy(
                        tidf[:, :], rt[t][:, D + 2 * NR:D + 2 * NR + 1]),
                         waits=[("ev", p.n("ev"))], incs=[("vv", 1)])
                    p.op("vector", lambda e, t=tc, g=tg: e.tensor_tensor(
                        rt[t][:, D + 2 * NR + 1:D + 2 * NR + 2],
                        tokid[:, g:g + 1], tidf[:, :], ALU.subtract),
                         incs=[("vv", 1)])
                    row_ev = p.n("ev")
                    p.op("vector", lambda e: e.tensor_single_scalar(
                        mk_f[:, :], gacc[:, :], 0.0, ALU.is_gt),
                         incs=[("vv", 1)])
                    p.op("vector", lambda e: e.tensor_copy(
                        mk_b[:, :], mk_f[:, :]), incs=[("vv", 1)])
                    mk_ok = p.n("vv")
                    b_p = 2 + tc % 2
                    bwp = fbank.wait(b_p)
                    p.op("tensor", lambda e, b=b_p: e.matmul(
                        psf[b][:, 0:NR], triu_sb[:, :], mk_b[:, :],
                        start=True, stop=True),
                         waits=[("vv", mk_ok), ("dc", dc_all[0])] + bwp,
                         incs=[("pe", 1)])
                    pcum_mm = p.n("pe")
                    b_c = 4 + tc % 2
                    bwc = fbank.wait(b_c)
                    p.op("tensor", lambda e, b=b_c: e.matmul(
                        psf[b][0:1, 0:NR], onesc_bf[:, :], mk_b[:, :],
                        start=True, stop=True),
                         waits=[("vv", mk_ok), ("gp", setup_gp)] + bwc,
                         incs=[("pe", 1)])
                    cnt_mm = p.n("pe")
                    p.op("vector", lambda e, b=b_p: e.tensor_tensor(
                        pe_t[:, :], psf[b][:, 0:NR], offs_full[:, :],
                        ALU.add),
                         waits=[("pe", pcum_mm)], incs=[("vv", 1)])
                    fbank.set(b_p, ("vv", p.n("vv")))
                    p.op("vector", lambda e: e.tensor_single_scalar(
                        ov_t[:, :], pe_t[:, :], CAP - 0.5, ALU.is_ge),
                         incs=[("vv", 1)])
                    p.op("vector", lambda e: e.tensor_scalar(
                        ov_t[:, :], ov_t[:, :], -1.0, 1.0, ALU.mult, ALU.add),
                         incs=[("vv", 1)])
                    p.op("vector", lambda e: e.tensor_tensor(
                        vd_t[:, :], mk_f[:, :], ov_t[:, :], ALU.mult),
                         incs=[("vv", 1)])
                    p.op("vector", lambda e: e.tensor_tensor(
                        pe_t[:, :], pe_t[:, :], eCrow[:, :], ALU.add),
                         waits=[("dc", dc_all[0])], incs=[("vv", 1)])
                    p.op("vector", lambda e, g=tg: e.tensor_single_scalar(
                        tr_t[:, :], trashb[:, :], float(g * NR),
                        ALU.add), incs=[("vv", 1)])
                    p.op("vector", lambda e: e.tensor_tensor(
                        pe_t[:, :], vd_t[:, :], pe_t[:, :], ALU.mult),
                         incs=[("vv", 1)])
                    p.op("vector", lambda e: e.tensor_scalar(
                        vd_t[:, :], vd_t[:, :], -1.0, 1.0, ALU.mult, ALU.add),
                         incs=[("vv", 1)])
                    p.op("vector", lambda e: e.tensor_tensor(
                        tr_t[:, :], vd_t[:, :], tr_t[:, :], ALU.mult),
                         incs=[("vv", 1)])
                    p.op("vector", lambda e: e.tensor_tensor(
                        of_t[:, :], pe_t[:, :], tr_t[:, :], ALU.add),
                         incs=[("vv", 1)])
                    p.op("vector", lambda e, t=tc: e.tensor_copy(
                        oint[t][:, :], of_t[:, :]),
                         waits=oint_free.wait(tc), incs=[("vv", 1)])
                    oint_ok = p.n("vv")
                    p.op("scalar", lambda e, b=b_c: e.copy(
                        cntb[0:1, :], psf[b][0:1, 0:NR]),
                         waits=[("pe", cnt_mm)], incs=[("ev", 1)])
                    cnt_ev = p.n("ev")
                    fbank.set(b_c, ("ev", cnt_ev))
                    b_c2 = 4 + (tc + 1) % 2
                    bwc2 = fbank.wait(b_c2)
                    p.op("tensor", lambda e, b=b_c2: e.matmul(
                        psf[b][:, 0:NR], ones_bf[0:1, 0:128], cntb[0:1, :],
                        start=True, stop=True),
                         waits=[("ev", cnt_ev)] + bwc2, incs=[("pe", 1)])
                    cb_mm = p.n("pe")
                    p.op("vector", lambda e, b=b_c2: e.tensor_tensor(
                        offs_full[:, :], offs_full[:, :], psf[b][:, 0:NR],
                        ALU.add),
                         waits=[("pe", cb_mm)], incs=[("vv", 1)])
                    fbank.set(b_c2, ("vv", p.n("vv")))
                    sc_w = [("vv", oint_ok), ("ev", row_ev), ("dz", dz_all)]
                    for re_ in range(nr_red):
                        p.op("gpsimd",
                             lambda e, t=tc, ee=re_: e.indirect_dma_start(
                                 out=xg_dram[:, :],
                                 out_offset=IndirectOffsetOnAxis(
                                     oint[t][:, ee:ee + 1], 0),
                                 in_=rt[t][:, :], in_offset=None,
                                 oob_is_err=False),
                             waits=sc_w if re_ == 0 else [],
                             incs=[("dsc", 16)])
                for _tc in range(4):
                    rt_free.set(_tc, ("dsc", p.n("dsc")))
                    oint_free.set(_tc, ("dsc", p.n("dsc")))
            xmid_all = p.n("ds")
            gates_ev = p.n("ev")
            gates_vv = p.n("vv")
            dsc_all = p.n("dsc")
            xh_stored_all = dict(xh_stored)

            p.barrier()
            ph2_scope.close()

        with ExitStack() as fsco:
            def fsb(name, shape, dt=F32):
                return fsco.enter_context(nc.sbuf_tensor(name, shape, dt))

            from concourse.bass_core_rust import IndirectOffsetOnAxis
            b2g = fsb("b2g_sb", [n_exp, D], BF16)
            b1c = fsb("b1c", [128, n_exp * NHC])
            dma_gp(b2g[:, :], b2g_in[:, :])
            dg_all[0] = p.n("dg")
            b1c_ld = dma_sync(b1c[:, :], b1cols_in[:, :])
            xq_buf = fsb("xq_buf", [128, NKC * 512], BF16)
            xg_tm = [fsb(f"xg_tm{i}", [128, 4 * ROWW], BF16) for i in range(2)]
            xg_fm = fsb("xg_fm", [128, NKC * 512], BF16)
            h_sb = fsb("h_sb", [128, NHC * 512], BF16)
            w1_sb = fsb("w1_sb", [128, NKC * HID], BF16)
            w2_sb = fsb("w2_sb", [128, NHC * D], BF16)
            contrib = [fsb(f"contrib{i}", [128, D]) for i in range(2)]
            gcol = fsb("gcol", [128, 8])
            tid_i = fsb("tid_i", [128, 8], I32)

            ctr = Slots(2)
            first_accum = [True]

            def accum_waits(wait_ev):
                w = [("ev", wait_ev), ("da", p.n("da"))]
                if first_accum[0]:
                    w.append(("ds", xmid_all))
                    first_accum[0] = False
                return w

            for q in range(n_st):
                for tc in range(4):
                    tg = q * 4 + tc
                    slot = tg % 2
                    cw = ctr.wait(slot)
                    for fs in range(4):
                        bank = fs % 2
                        bw = fbank.wait(bank)
                        p.op("tensor", lambda e, b=bank, t=tg, f=fs: e.matmul(
                            psf[b][:, :], gates_fm[:, t * 128:(t + 1) * 128],
                            b2g[:, f * 512:(f + 1) * 512],
                            start=True, stop=True),
                             waits=[("ev", gates_ev), ("gp", gfm_init),
                                    ("dg", dg_all[0])] + bw,
                             incs=[("pe", 1)])
                        grp = p.n("pe")
                        p.op("scalar", lambda e, b=bank, s=slot, f=fs: e.copy(
                            contrib[s][:, f * 512:(f + 1) * 512],
                            psf[b][:, :]),
                             waits=[("pe", grp)] + (cw if fs == 0 else []),
                             incs=[("ev", 1)])
                        fbank.set(bank, ("ev", p.n("ev")))
                    p.op("gpsimd", lambda e, q2=q, t2=tc, s=slot: e.dma_start(
                        out=out_ext[q2 * 512 + t2 * 128:
                                    q2 * 512 + (t2 + 1) * 128, :],
                        in_=contrib[s][:, :], accum_op=ALU.add),
                         waits=accum_waits(p.n("ev")), incs=[("da", 16)])
                    ctr.set(slot, ("da", p.n("da")))

            w1_loaded = [0]
            w2_loaded = [0]
            order = [nr_red, nr_red + 1] + list(range(nr_red))

            def load_w1(e_, wait_pe):
                dma_gp(w1_sb[:, :].rearrange("p (kc h) -> p kc h", kc=NKC),
                       w1_all[e_].rearrange("(kc p) h -> p kc h", p=128),
                       waits=[("pe", wait_pe)], sem="dw1")
                w1_loaded[0] = p.n("dw1")

            def load_w2(e_, wait_pe):
                dma_gp(w2_sb[:, :].rearrange("p (hc f) -> p hc f", hc=NHC),
                       w2_all[e_].rearrange("(hc p) f -> p hc f", p=128),
                       waits=[("pe", wait_pe)], sem="dw2")
                w2_loaded[0] = p.n("dw2")

            load_w1(order[0], 0)
            load_w2(order[0], 0)

            xq_load = [0]
            xq_free = [("pe", 0)]

            def load_xq(q_):
                xq_load[0] = dma_sync(
                    xq_buf[:, :].rearrange("p (kc t) -> p kc t", kc=NKC),
                    xhsc[:, q_ * 512:(q_ + 1) * 512].rearrange(
                        "(kc p) t -> p kc t", p=128),
                    waits=[xq_free[0], ("dsx", xh_stored_all[q_])],
                    sem="sxq0")

            for si, se in enumerate(order[:2]):
                w1_this = w1_loaded[0]
                w2_this = w2_loaded[0]
                for q in range(n_st):
                    load_xq(q)
                    xl = xq_load[0]
                    for hc in range(NHC):
                        bank = hc % 2
                        bw = fbank.wait(bank)
                        for kc in range(NKC):
                            p.op("tensor",
                                 lambda e, b=bank, hh=hc, k=kc: e.matmul(
                                     psf[b][:, :],
                                     w1_sb[:, k * HID + hh * 128:
                                           k * HID + (hh + 1) * 128],
                                     xq_buf[:, k * 512:(k + 1) * 512],
                                     start=(k == 0), stop=(k == NKC - 1)),
                                 waits=([("dw1", w1_this), ("sxq0", xl)] + bw)
                                 if kc == 0 else [],
                                 incs=([("pe", 1)] if kc == NKC - 1 else []))
                        grp = p.n("pe")
                        p.op("scalar",
                             lambda e, b=bank, hh=hc, ee=se: e.activation(
                                 h_sb[:, hh * 512:(hh + 1) * 512],
                                 psf[b][:, :], AF.Relu,
                                 bias=b1c[:, ee * NHC + hh:
                                          ee * NHC + hh + 1]),
                             waits=[("pe", grp), ("dc", b1c_ld)],
                             incs=[("ev", 1)])
                        fbank.set(bank, ("ev", p.n("ev")))
                    h_ev_end = p.n("ev")
                    xq_free[0] = ("pe", p.n("pe"))
                    for tc in range(4):
                        tg = q * 4 + tc
                        cslot = tg % 2
                        cw = ctr.wait(cslot)
                        for fs in range(4):
                            bank = 2 + (tc * 4 + fs) % 2
                            bw = fbank.wait(bank)
                            for hc in range(NHC):
                                p.op("tensor",
                                     lambda e, b=bank, hh=hc, t=tc, f=fs:
                                     e.matmul(
                                         psf[b][:, :],
                                         h_sb[:, hh * 512 + t * 128:
                                              hh * 512 + (t + 1) * 128],
                                         w2_sb[:, hh * D + f * 512:
                                               hh * D + (f + 1) * 512],
                                         start=(hh == 0),
                                         stop=(hh == NHC - 1)),
                                     waits=([("dw2", w2_this),
                                             ("ev", h_ev_end)] + bw)
                                     if hc == 0 else [],
                                     incs=([("pe", 1)] if hc == NHC - 1
                                           else []))
                            grp = p.n("pe")
                            p.op("scalar", lambda e, b=bank, s=cslot, f=fs:
                                 e.activation(
                                     contrib[s][:, f * 512:(f + 1) * 512],
                                     psf[b][:, :], AF.Copy, scale=0.5),
                                 waits=[("pe", grp)] + (cw if fs == 0 else []),
                                 incs=[("ev", 1)])
                            fbank.set(bank, ("ev", p.n("ev")))
                        p.op("gpsimd",
                             lambda e, q2=q, t2=tc, s=cslot: e.dma_start(
                                 out=out_ext[q2 * 512 + t2 * 128:
                                             q2 * 512 + (t2 + 1) * 128, :],
                                 in_=contrib[s][:, :], accum_op=ALU.add),
                             waits=accum_waits(p.n("ev")), incs=[("da", 16)])
                        ctr.set(cslot, ("da", p.n("da")))
                    if q == n_st - 1:
                        load_w1(order[si + 1], p.n("pe"))
                z_pe_end = p.n("pe")
                load_w2(order[si + 1], z_pe_end) if si + 1 < len(order) else None

            xg_load = {0: 0, 1: 0}
            xg_free = Slots(2)
            xgfm_free = [0]

            for ri, re_ in enumerate(order[2:]):
                slot = ri % 2
                gsem = "sgx0" if slot == 0 else "sgx1"
                xg_load[slot] = dma_sync(
                    xg_tm[slot][:, :].rearrange("p (tc f) -> p tc f", tc=4),
                    xg_dram[re_ * CAP:(re_ + 1) * CAP, :].rearrange(
                        "(tc p) f -> p tc f", p=128),
                    waits=[("dsc", dsc_all)] + xg_free.wait(slot), sem=gsem)
                gl = xg_load[slot]
                w1_this = w1_loaded[0]
                w2_this = w2_loaded[0]
                for tc in range(4):
                    p.op("vector", lambda e, s=slot, t=tc, ee=re_:
                         e.tensor_tensor(
                             gcol[:, s * 4 + t:s * 4 + t + 1],
                             xg_tm[s][:, t * ROWW + D + ee:
                                      t * ROWW + D + ee + 1],
                             xg_tm[s][:, t * ROWW + D + NR + ee:
                                      t * ROWW + D + NR + ee + 1],
                             ALU.add),
                         waits=[(gsem, gl)], incs=[("vv", 1)])
                    p.op("vector", lambda e, s=slot, t=tc: e.tensor_tensor(
                        tidf[:, :],
                        xg_tm[s][:, t * ROWW + D + 2 * NR:
                                 t * ROWW + D + 2 * NR + 1],
                        xg_tm[s][:, t * ROWW + D + 2 * NR + 1:
                                 t * ROWW + D + 2 * NR + 2],
                        ALU.add), incs=[("vv", 1)])
                    p.op("vector", lambda e, s=slot, t=tc: e.tensor_copy(
                        tid_i[:, s * 4 + t:s * 4 + t + 1], tidf[:, :]),
                         incs=[("vv", 1)])
                gt_ok = p.n("vv")
                for tc in range(4):
                    for kc in range(NKC):
                        transpose_128(
                            xg_tm[slot][:, tc * ROWW + kc * 128:
                                        tc * ROWW + (kc + 1) * 128],
                            xg_fm[:, kc * 512 + tc * 128:
                                  kc * 512 + (tc + 1) * 128],
                            [(gsem, gl), ("pe", xgfm_free[0])] if
                            (tc == 0 and kc == 0) else [(gsem, gl)])
                tr_ev = p.n("ev")
                tr_vv = p.n("vv")
                for hc in range(NHC):
                    bank = hc % 2
                    bw = fbank.wait(bank)
                    for kc in range(NKC):
                        p.op("tensor",
                             lambda e, b=bank, hh=hc, k=kc: e.matmul(
                                 psf[b][:, :],
                                 w1_sb[:, k * HID + hh * 128:
                                       k * HID + (hh + 1) * 128],
                                 xg_fm[:, k * 512:(k + 1) * 512],
                                 start=(k == 0), stop=(k == NKC - 1)),
                             waits=([("dw1", w1_this), ("ev", tr_ev),
                                     ("vv", tr_vv)] + bw)
                             if kc == 0 else [],
                             incs=([("pe", 1)] if kc == NKC - 1 else []))
                    grp = p.n("pe")
                    p.op("scalar",
                         lambda e, b=bank, hh=hc, ee=re_: e.activation(
                             h_sb[:, hh * 512:(hh + 1) * 512],
                             psf[b][:, :], AF.Relu,
                             bias=b1c[:, ee * NHC + hh:ee * NHC + hh + 1]),
                         waits=[("pe", grp), ("dc", b1c_ld)],
                         incs=[("ev", 1)])
                    fbank.set(bank, ("ev", p.n("ev")))
                h_ev_end = p.n("ev")
                h_pe_end = p.n("pe")
                xgfm_free[0] = h_pe_end
                xg_free.set(slot, ("pe", h_pe_end))
                if ri + 1 < nr_red:
                    load_w1(order[2 + ri + 1], h_pe_end)
                for tc in range(4):
                    cslot = tc % 2
                    cw = ctr.wait(cslot)
                    for fs in range(4):
                        bank = 2 + (tc * 4 + fs) % 2
                        bw = fbank.wait(bank)
                        for hc in range(NHC):
                            p.op("tensor",
                                 lambda e, b=bank, hh=hc, t=tc, f=fs:
                                 e.matmul(
                                     psf[b][:, :],
                                     h_sb[:, hh * 512 + t * 128:
                                          hh * 512 + (t + 1) * 128],
                                     w2_sb[:, hh * D + f * 512:
                                           hh * D + (f + 1) * 512],
                                     start=(hh == 0), stop=(hh == NHC - 1)),
                                 waits=([("dw2", w2_this),
                                         ("ev", h_ev_end)] + bw)
                                 if hc == 0 else [],
                                 incs=([("pe", 1)] if hc == NHC - 1 else []))
                        grp = p.n("pe")
                        p.op("scalar",
                             lambda e, b=bank, s=cslot, f=fs, sl=slot, t=tc:
                             e.activation(
                                 contrib[s][:, f * 512:(f + 1) * 512],
                                 psf[b][:, :], AF.Copy,
                                 scale=gcol[:, sl * 4 + t:sl * 4 + t + 1]),
                             waits=[("pe", grp), ("vv", gt_ok)] +
                             (cw if fs == 0 else []),
                             incs=[("ev", 1)])
                        fbank.set(bank, ("ev", p.n("ev")))
                    p.op("gpsimd",
                         lambda e, s=cslot, sl=slot, t=tc: e.indirect_dma_start(
                             out=out_ext[:, :],
                             out_offset=IndirectOffsetOnAxis(
                                 tid_i[:, sl * 4 + t:sl * 4 + t + 1], 0),
                             in_=contrib[s][:, :], in_offset=None,
                             oob_is_err=False, compute_op=ALU.add),
                         waits=accum_waits(p.n("ev")) + [("vv", gt_ok)],
                         incs=[("da", 16)])
                    ctr.set(cslot, ("da", p.n("da")))
                z_pe_end = p.n("pe")
                if ri + 1 < nr_red:
                    load_w2(order[2 + ri + 1], z_pe_end)

            p.op("gpsimd", lambda e: e.nop(),
                 waits=[("da", p.n("da")), ("dw1", p.n("dw1")),
                        ("dw2", p.n("dw2")), ("dg", p.n("dg")),
                        ("dwa", p.n("dwa")), ("dwb", p.n("dwb")),
                        ("dsc", p.n("dsc"))])
            p.op("sync", lambda e: e.nop(),
                 waits=[("ds", p.n("ds")), ("dsx", p.n("dsx")),
                        ("sx", p.n("sx")), ("dz", p.n("dz")),
                        ("sxq0", p.n("sxq0")), ("sgx0", p.n("sgx0")),
                        ("sgx1", p.n("sgx1"))])

        p.emit(nc, sems)

    return nc


def _host_constants(cfg):
    n_st = cfg["n_st"]
    lam = np.exp(-(2.0 ** (-0.5)) ** (np.arange(1, NH + 1, dtype=np.float64)))
    i_idx = np.arange(128, dtype=np.float64)
    dmaskT = np.zeros((128, NH * 128))
    decayq = np.zeros((128, NH * 128))
    dkx = np.zeros((128, NH * 4))
    dk1 = np.zeros((128, NH))
    for h in range(NH):
        diff = i_idx[None, :] - i_idx[:, None]
        with np.errstate(over="ignore", under="ignore"):
            m = np.where(diff >= 0, lam[h] ** diff, 0.0)
            dmaskT[:, h * 128:(h + 1) * 128] = m
            decayq[:, h * 128:(h + 1) * 128] = (lam[h] ** (i_idx + 1.0))[None, :]
            dk1[:, h] = lam[h] ** (127.0 - i_idx)
            for cc in range(4):
                dkx[:, h * 4 + cc] = lam[h] ** (127.0 - i_idx + 128.0 * (3 - cc))
    NTC = 4 * n_st
    n_exp = cfg["n_exp"]
    nr_red = n_exp - 2
    consts = dict(
        dmaskT=dmaskT.astype(np.float32),
        decayq=decayq.astype(np.float32),
        dkx=dkx.astype(np.float32),
        dk1=dk1.astype(np.float32),
        ident_bf=np.eye(128).astype(ml_dtypes.bfloat16),
        triu=np.triu(np.ones((128, 128)), k=1).astype(ml_dtypes.bfloat16),
        eCrow=np.tile((np.arange(NR) * CAP).astype(np.float32), (128, 1)),
        trashb=np.tile((nr_red * CAP + np.arange(NR)).astype(np.float32),
                       (128, 1)),
        tokid=(np.arange(NTC)[None, :] * 128 +
               np.arange(128)[:, None]).astype(np.float32),
    )
    sinw = []
    span = 512 * n_st
    for c in range(NCORES):
        w = np.zeros((128, NCORES * NH))
        r0 = c % 4
        for g in range(NCORES):
            if g // 4 != c // 4:
                continue
            rgl = g % 4
            if rgl < r0:
                for h in range(NH):
                    with np.errstate(over="ignore", under="ignore"):
                        w[:, g * NH + h] = lam[h] ** (span * (r0 - 1 - rgl))
        sinw.append(w.astype(np.float32))
    return consts, sinw


def make_in_maps(inputs, cfg):
    n_st = cfg["n_st"]
    n_exp = cfg["n_exp"]
    T = 512 * n_st
    x = np.asarray(inputs["L0_attn_x"], dtype=np.float32).reshape(-1, D)
    wqkvT = np.ascontiguousarray(
        np.asarray(inputs["L0_attn_w_qkv"], dtype=np.float32).T)
    woutT = np.ascontiguousarray(
        np.asarray(inputs["L0_attn_w_out"], dtype=np.float32).T)
    b_qkv = np.asarray(inputs["L0_attn_b_qkv"], dtype=np.float32)
    b_out = np.asarray(inputs["L0_attn_b_out"], dtype=np.float32)
    sw1 = np.asarray(inputs["L0_ffn_sw1"], dtype=np.float32)
    sb1 = np.asarray(inputs["L0_ffn_sb1"], dtype=np.float32)
    sw2 = np.asarray(inputs["L0_ffn_sw2"], dtype=np.float32)
    sb2 = np.asarray(inputs["L0_ffn_sb2"], dtype=np.float32)
    rw1 = np.asarray(inputs["L0_ffn_rw1"], dtype=np.float32)
    rb1 = np.asarray(inputs["L0_ffn_rb1"], dtype=np.float32)
    rw2 = np.asarray(inputs["L0_ffn_rw2"], dtype=np.float32)
    rb2 = np.asarray(inputs["L0_ffn_rb2"], dtype=np.float32)
    router_w = np.asarray(inputs["L0_ffn_router_w"], dtype=np.float32)
    router_b = np.asarray(inputs["L0_ffn_router_b"], dtype=np.float32)

    nr_red = n_exp - 2
    w1_all = np.ascontiguousarray(np.concatenate([rw1[:nr_red], sw1], axis=0))
    w2_all = np.ascontiguousarray(np.concatenate([rw2[:nr_red], sw2], axis=0))
    b1_all = np.concatenate([rb1[:nr_red], sb1], axis=0)
    b1cols = np.ascontiguousarray(b1_all.reshape(n_exp * NHC, 128).T)
    b2stack = np.concatenate([rb2[:nr_red], sb2], axis=0).astype(
        ml_dtypes.bfloat16)

    consts, sinw = _host_constants(cfg)
    shared = dict(
        wqkvT=wqkvT, woutT=woutT, w1_all=w1_all, w2_all=w2_all,
        routerw=router_w,
        bq_row=b_qkv[None, :].astype(ml_dtypes.bfloat16),
        bqcol=np.ascontiguousarray(b_qkv[0:D].reshape(NKC, 128).T),
        bout_row=b_out[None, :].astype(ml_dtypes.bfloat16),
        routerb_row=router_b[None, :].astype(ml_dtypes.bfloat16),
        b2g=b2stack, b1cols=b1cols, **consts)
    in_maps = []
    for c in range(NCORES):
        m = dict(shared)
        m["x"] = np.ascontiguousarray(x[c * T:(c + 1) * T])
        m["sinw"] = sinw[c]
        in_maps.append(m)
    return in_maps


def kernel(**inputs):
    cfg = default_cfg()
    in_maps = make_in_maps(inputs, cfg)
    nc = build_nc(cfg)
    res = run_bass_kernel_spmd(nc, in_maps, core_ids=list(range(NCORES)))
    out = np.concatenate([np.asarray(r["out"]) for r in res.results], axis=0)
    return out.reshape(2, 8192, D).astype(np.float32)


# version 06
# speedup vs baseline: 1.1291x; 1.1291x over previous
import math
import numpy as np
import ml_dtypes
from contextlib import ExitStack

import concourse.bass as bass
import concourse.mybir as mybir
from concourse.bass_utils import run_bass_kernel_spmd

F32 = mybir.dt.float32
BF16 = mybir.dt.bfloat16
I32 = mybir.dt.int32
AF = mybir.ActivationFunctionType
ALU = mybir.AluOpType
AX = mybir.AxisListType

D = 2048
NH = 16
HID = 1408
NHC = 11
NKC = 16
NE = 34
NR = 32
TOPK = 6
EPS = 1e-5
CAP = 512
ROWW = 2176
NCORES = 8


def default_cfg():
    return dict(n_st=4, n_exp=NE)


class Prog:
    ENGINES = ("sync", "tensor", "scalar", "vector", "gpsimd")

    def __init__(self):
        self.ops = []
        self.cnt = {}

    def n(self, sem):
        return self.cnt.get(sem, 0)

    CHAIN = {"scalar": "ev", "vector": "vv"}

    def op(self, eng, fn=None, waits=(), incs=(), chain=True):
        assert eng in self.ENGINES
        incs = list(incs)
        waits = list(waits)
        csem = self.CHAIN.get(eng)
        if fn is not None and csem is not None and chain:
            waits.append((csem, self.n(csem)))
            if not any(s == csem for s, _ in incs):
                incs.append((csem, 1))
        ws = {}
        for s, v in waits:
            if v > 0:
                assert v <= self.n(s), f"wait {s}>={v} but only {self.n(s)}"
                ws[s] = max(ws.get(s, 0), v)
        if incs:
            assert fn is not None, "incs must ride a real instruction"
        assert len(incs) <= 2
        self.ops.append((eng, fn, sorted(ws.items()), incs))
        for s, k in incs:
            self.cnt[s] = self.n(s) + k

    def barrier(self):
        snap = [(s, v) for s, v in self.cnt.items() if v > 0]
        for eng in self.ENGINES:
            self.op(eng, None, waits=snap)

    def emit(self, nc, sems):
        ops = self.ops

        def mk(eng_name):
            def inner(eng):
                for (e, fn, waits, incs) in ops:
                    if e != eng_name:
                        continue
                    for s, v in waits:
                        eng.wait_ge(sems[s], v)
                    inst = fn(eng) if fn is not None else None
                    for s, k in incs:
                        inst.then_inc(sems[s], k)
            return inner

        with nc.Block() as block:
            block.sync(mk("sync"))
            block.tensor(mk("tensor"))
            block.scalar(mk("scalar"))
            block.vector(mk("vector"))
            block.gpsimd(mk("gpsimd"))


class Slots:

    def __init__(self, n):
        self.free = {i: [] for i in range(n)}

    def wait(self, i):
        return list(self.free[i])

    def set(self, i, *conds):
        self.free[i] = [c for c in conds if c[1] > 0]

    def add(self, i, cond):
        if cond[1] > 0:
            self.free[i].append(cond)


def build_nc(cfg):
    n_st = cfg["n_st"]
    n_exp = cfg["n_exp"]
    nr_red = n_exp - 2
    T = 512 * n_st
    NTC = T // 128

    lam = np.exp(-(2.0 ** (-0.5)) ** (np.arange(1, NH + 1, dtype=np.float64)))

    nc = bass.Bass("TRN2", target_bir_lowering=False)

    def din(name, shape, dt=F32):
        return nc.dram_tensor(name, shape, dt, kind="ExternalInput")

    x_ext = din("x", [T, D])
    wqkvT = din("wqkvT", [D, 3 * D])
    woutT = din("woutT", [D, D])
    w1_all = din("w1_all", [n_exp, D, HID])
    w2_all = din("w2_all", [n_exp, HID, D])
    routerw = din("routerw", [D, NR])
    bq_row = din("bq_row", [1, 3 * D], BF16)
    bqcol_in = din("bqcol", [128, NKC])
    bout_row = din("bout_row", [1, D], BF16)
    routerb_row = din("routerb_row", [1, NR], BF16)
    b2g_in = din("b2g", [n_exp, D], BF16)
    b1cols_in = din("b1cols", [128, n_exp * NHC])
    ident_bf_in = din("ident_bf", [128, 128], BF16)
    dmaskT_in = din("dmaskT", [128, NH * 128])
    decayq_in = din("decayq", [128, NH * 128])
    dkx_in = din("dkx", [128, NH * 4])
    dk1_in = din("dk1", [128, NH])
    sinw_in = din("sinw", [128, NCORES * NH])
    triu_in = din("triu", [128, 128], BF16)
    eCrow_in = din("eCrow", [128, NR])
    trashb_in = din("trashb", [128, NR])
    tokid_in = din("tokid", [128, NTC])

    out_ext = nc.dram_tensor("out", [T, D], F32, kind="ExternalOutput")

    qsc = nc.dram_tensor("qsc", [D, T], BF16)
    ksc = nc.dram_tensor("ksc", [T, D], BF16)
    vsc = nc.dram_tensor("vsc", [T, D], BF16)
    xhsc = nc.dram_tensor("xhsc", [D, T], BF16)
    xg_rows = nr_red * CAP + NTC * NR + 128
    xg_dram = nc.dram_tensor("xg_dram", [xg_rows, ROWW], BF16)
    cc_in = nc.dram_tensor("cc_in", [128, NH * 128], F32)
    from concourse.replica_groups import (
        maybe_share_collective_output_space, ReplicaGroups)
    rg = [list(range(NCORES))]
    aspace = maybe_share_collective_output_space("AllGather", ReplicaGroups(rg))
    cc_out = nc.dram_tensor("cc_out", [NCORES * 128, NH * 128], F32,
                            addr_space=aspace)

    p = Prog()
    es = ExitStack()
    with es:
        def sb(name, shape, dt=F32):
            return es.enter_context(nc.sbuf_tensor(name, shape, dt))

        psf = [es.enter_context(nc.psum_tensor(f"psf{i}", [128, 512], F32))
               for i in range(6)]
        psb = [es.enter_context(nc.psum_tensor(f"psb{i}", [128, 1024], BF16))
               for i in range(2)]
        sem_names = ["dc", "dg", "sx", "sq", "sk", "sv", "sm", "sg2",
                     "sxq0", "sxq1", "dwa", "dwb", "dw1", "dw2",
                     "ds", "dsx", "da", "cc", "pe", "ev", "vv", "gp",
                     "dz", "dsc", "sgx0", "sgx1", "sg2b"]
        sems = {s: es.enter_context(nc.semaphore(s)) for s in sem_names}

        ident_bf = sb("ident_bf_s", [128, 128], BF16)
        ones_bf = sb("ones_bf", [1, 128], BF16)
        rbr = sb("rbr", [1, NR], BF16)
        rw_bf = sb("rw_bf", [128, NKC * NR], BF16)
        gates_fm = sb("gates_fm", [n_exp, T], BF16)
        triu_sb = sb("triu_sb", [128, 128], BF16)
        eCrow = sb("eCrow_s", [128, NR])
        trashb = sb("trashb_s", [128, NR])
        tokid = sb("tokid_s", [128, NTC])
        onesc_bf = sb("onesc_bf", [128, 1], BF16)
        zr = sb("zr", [128, ROWW], BF16)
        offs_full = sb("offs_full", [128, NR])
        gtmp = sb("gtmp", [128, NR])
        ghif = sb("ghif", [128, NR])
        mk_f = sb("mk_f", [128, NR])
        mk_b = sb("mk_b", [128, NR], BF16)
        ov_t = sb("ov_t", [128, NR])
        vd_t = sb("vd_t", [128, NR])
        pe_t = sb("pe_t", [128, NR])
        tr_t = sb("tr_t", [128, NR])
        of_t = sb("of_t", [128, NR])
        cntb = sb("cntb", [1, NR], BF16)
        tidf = sb("tidf", [128, 1])
        stat = sb("stat", [128, 64])
        e_buf = sb("e_buf", [128, NR])
        m8 = sb("m8", [128, 8])
        erem = sb("erem", [128, NR])
        gacc = sb("gacc", [128, NR])
        gbf_tm = sb("gbf_tm", [128, NR], BF16)
        sq_dump = sb("sq_dump", [128, D], BF16)

        fbank = Slots(6)
        bbank = Slots(2)
        tr_par = [0]

        def dma_sync(dst, src, waits=(), sem="dc"):
            w = list(waits) + ([(sem, p.n(sem))] if sem == "dc" else [])
            p.op("sync", lambda e, d=dst, s=src: e.dma_start(out=d, in_=s),
                 waits=w, incs=[(sem, 16)])
            return p.n(sem)

        def dma_gp(dst, src, waits=(), sem="dg"):
            w = list(waits) + ([(sem, p.n(sem))] if sem == "dg" else [])
            p.op("gpsimd", lambda e, d=dst, s=src: e.dma_start(out=d, in_=s),
                 waits=w, incs=[(sem, 16)])
            return p.n(sem)

        dma_sync(eCrow[:, :], eCrow_in[:, :])
        dma_sync(trashb[:, :], trashb_in[:, :])
        dma_sync(tokid[:, :], tokid_in[:, :])
        dma_sync(triu_sb[:, :], triu_in[:, :])
        dma_gp(ident_bf[:, :], ident_bf_in[:, :])
        dma_gp(rbr[:, :], routerb_row[:, :])
        dma_gp(rw_bf[:, :].rearrange("p (kc e) -> p kc e", kc=NKC),
               routerw.rearrange("(kc p) e -> p kc e", p=128))
        p.op("gpsimd", lambda e: e.memset(ones_bf[:, :], 1.0), incs=[("gp", 1)])
        p.op("gpsimd", lambda e: e.memset(onesc_bf[:, :], 1.0), incs=[("gp", 1)])
        p.op("gpsimd", lambda e: e.memset(zr[:, :], 0.0), incs=[("gp", 1)])
        zr_init = p.n("gp")
        p.op("vector", lambda e: e.memset(offs_full[:, :], 0.0), incs=[("vv", 1)])
        for r in range(nr_red * CAP // 128):
            p.op("sync", lambda e, rr=r: e.dma_start(
                out=xg_dram[rr * 128:(rr + 1) * 128, :], in_=zr[:, :]),
                 waits=[("gp", zr_init)] if r == 0 else [], incs=[("dz", 16)])
        dz_all = p.n("dz")
        p.op("vector", lambda e: e.memset(stat[:, 51:52], EPS), incs=[("vv", 1)])
        eps_init = p.n("vv")
        setup_dx = p.n("dc")
        dc_all = [setup_dx]
        setup_dw = p.n("dg")
        setup_gp = p.n("gp")
        dg_all = [setup_dw]

        ev_par = [0]

        def transpose_128(src_ap, dst_ap, extra_waits=()):
            b = tr_par[0]
            tr_par[0] ^= 1
            p.op("tensor",
                 lambda e, s=src_ap, bb=b: e.transpose(
                     psb[bb][:, 0:128], s, ident_bf[:, :]),
                 waits=bbank.wait(b) + list(extra_waits) +
                 [("dg", dg_all[0])], incs=[("pe", 1)])
            mm = p.n("pe")
            ev_par[0] ^= 1
            if ev_par[0]:
                p.op("scalar",
                     lambda e, d=dst_ap, bb=b: e.copy(d, psb[bb][:, 0:128]),
                     waits=[("pe", mm)], incs=[("ev", 1)])
                bbank.set(b, ("ev", p.n("ev")))
            else:
                p.op("vector",
                     lambda e, d=dst_ap, bb=b: e.tensor_copy(
                         d, psb[bb][:, 0:128]),
                     waits=[("pe", mm)], incs=[("vv", 1)])
                bbank.set(b, ("vv", p.n("vv")))

        xhat_tm_ref = [None]

        def ln_tm(src_tile_fn, n_tc, dst_fn, first_waits, ts_dst_fn=None,
                  ts_waits_fn=None):
            xhat_tm = xhat_tm_ref[0]
            c_sum = stat[:, 0:0 + n_tc]
            c_sq = stat[:, 8:8 + n_tc]
            c_m = stat[:, 16:16 + n_tc]
            c_rstd = stat[:, 24:24 + n_tc]
            c_nmr = stat[:, 32:32 + n_tc]
            c_t0 = stat[:, 40:40 + n_tc]
            for tc in range(n_tc):
                src = src_tile_fn(tc)
                w = (list(first_waits) + [("vv", p.n("vv"))]) if tc == 0 else []
                p.op("scalar",
                     lambda e, s=src, c=c_sq[:, tc:tc + 1]: e.activation(
                         sq_dump[:, :], s, AF.Square, accum_out=c),
                     waits=w + [("ev", p.n("ev"))], incs=[("ev", 1)])
                p.op("scalar",
                     lambda e, s=src, c=c_sum[:, tc:tc + 1]: e.activation(
                         sq_dump[:, :], s, AF.Copy, accum_out=c),
                     waits=[("ev", p.n("ev"))], incs=[("ev", 1)])
            act_done = p.n("ev")
            p.op("vector", lambda e: e.tensor_scalar_mul(c_m, c_sum, 1.0 / D),
                 waits=[("ev", act_done)])
            p.op("vector", lambda e: e.tensor_scalar_mul(c_t0, c_sq, 1.0 / D))
            p.op("vector", lambda e: e.tensor_tensor(c_rstd, c_m, c_m, ALU.mult))
            p.op("vector",
                 lambda e: e.tensor_tensor(c_t0, c_t0, c_rstd, ALU.subtract),
                 incs=[("vv", 1)])
            var_done = p.n("vv")
            p.op("scalar", lambda e: e.activation(c_t0, c_t0, AF.Sqrt,
                                                  bias=stat[:, 51:52]),
                 waits=[("vv", var_done)], incs=[("ev", 1)])
            sd_done = p.n("ev")
            p.op("vector", lambda e: e.reciprocal(c_rstd, c_t0),
                 waits=[("ev", sd_done)])
            p.op("vector", lambda e: e.scalar_tensor_tensor(
                c_nmr, c_m, -1.0, c_rstd, ALU.mult, ALU.mult),
                 incs=[("vv", 1)])
            stats_done = p.n("vv")
            for tc in range(n_tc):
                src = src_tile_fn(tc)
                tdst = ts_dst_fn(tc) if ts_dst_fn else xhat_tm[:, :]
                tw = (ts_waits_fn(tc) if ts_waits_fn else [])
                p.op("vector",
                     lambda e, s=src, r=c_rstd[:, tc:tc + 1],
                     nm=c_nmr[:, tc:tc + 1], xt=tdst:
                     e.tensor_scalar(xt, s, r, nm, ALU.mult, ALU.add),
                     waits=[("vv", stats_done), ("pe", p.n("pe"))] + tw,
                     incs=[("vv", 1)])
                xh = p.n("vv")
                for kc in range(NKC):
                    transpose_128(tdst[:, kc * 128:(kc + 1) * 128],
                                  dst_fn(tc, kc), [("vv", xh)])

        with ExitStack() as asco:
            def asb(name, shape, dt=F32):
                return asco.enter_context(nc.sbuf_tensor(name, shape, dt))

            sinw = asb("sinw_s", [128, NCORES * NH])
            S_delta = asb("S_delta", [128, NH * 128])
            S_state = S_delta
            x_buf = [asb("x_buf0", [128, 4 * D])]
            x_buf = [x_buf[0], x_buf[0]]
            fm_buf = asb("fm_buf", [128, NKC * 512], BF16)
            k_tm = asb("k_tm", [128, 4 * D], BF16)
            v_tm = asb("v_tm", [128, 4 * D], BF16)
            wsl = [asb(f"wsl{i}", [128, NKC * 512], BF16) for i in range(2)]
            gat_buf = None
            rt = [asb(f"rt{i}", [128, ROWW], BF16) for i in range(4)]
            oint = [asb(f"oint{i}", [128, NR], I32) for i in range(4)]
            for _i in range(4):
                p.op("vector", lambda e, i=_i: e.memset(
                    rt[i][:, D + 2 * NR + 2:ROWW], 0.0), incs=[("vv", 1)])


            consts_dx = dma_sync(sinw[:, :], sinw_in[:, :])
            dc_all[0] = p.n("dc")
            p.op("vector", lambda e: e.memset(S_delta[:, :], 0.0),
                 incs=[("vv", 1)])
            sdelta_init = p.n("vv")

            xload = {}
            xslot = Slots(1)

            def load_x(st):
                w = xslot.wait(0)
                xload[st] = dma_sync(
                    x_buf[st % 2][:, :].rearrange("p (tc d) -> p tc d", tc=4),
                    x_ext[st * 512:(st + 1) * 512, :].rearrange(
                        "(tc p) d -> p tc d", p=128), waits=w, sem="sx")

            load_x(0)

            wslot = Slots(2)
            kd_slot = Slots(2)
            rt_free = Slots(4)
            oint_free = Slots(4)

            ph1_scope = ExitStack()
            bqr = ph1_scope.enter_context(
                nc.sbuf_tensor("bqr", [1, 3 * D], BF16))
            dkx = ph1_scope.enter_context(
                nc.sbuf_tensor("dkx_s", [128, NH * 4], F32))
            kdb = ph1_scope.enter_context(
                nc.sbuf_tensor("kdb", [128, 2 * 128], BF16))
            qfm_buf = ph1_scope.enter_context(
                nc.sbuf_tensor("qfm_buf", [128, NKC * 512], BF16))
            q_tm = ph1_scope.enter_context(
                nc.sbuf_tensor("q_tm", [128, 4 * D], BF16))
            bqcol = ph1_scope.enter_context(
                nc.sbuf_tensor("bqcol_s", [128, NKC], F32))
            dma_gp(bqr[:, :], bq_row[:, :])
            setup_dw2 = p.n("dg")
            dg_all[0] = p.n("dg")
            dma_sync(dkx[:, :], dkx_in[:, :])
            dma_sync(bqcol[:, :], bqcol_in[:, :])
            dc_all[0] = p.n("dc")
            for st in range(n_st):
                xb = x_buf[st % 2]
                scst_prev = p.n("ds")
                ln_tm(lambda tc, b=xb: b[:, tc * D:(tc + 1) * D], 4,
                      lambda tc, kc: fm_buf[:, kc * 512 + tc * 128:
                                            kc * 512 + (tc + 1) * 128],
                      [("sx", xload[st])],
                      ts_dst_fn=lambda tc: rt[tc][:, 0:D],
                      ts_waits_fn=lambda tc: rt_free.wait(tc))
                for _tc in range(4):
                    rt_free.set(_tc, ("pe", p.n("pe")))
                xhat_done = p.n("ev")
                xhat_vv = p.n("vv")
                xslot.set(0, ("ev", xhat_done))

                first_evac = True
                for pi, (dst, fofs) in enumerate(
                        ((None, 0), (k_tm, D), (v_tm, 2 * D))):
                    for fsl in range(4):
                        slot = (pi * 4 + fsl) % 2
                        wsem = "dwa" if slot == 0 else "dwb"
                        wl = dma_gp(
                            wsl[slot][:, :].rearrange(
                                "p (kc f) -> p kc f", kc=NKC),
                            wqkvT[:, fofs + fsl * 512:
                                  fofs + (fsl + 1) * 512].rearrange(
                                "(kc p) f -> p kc f", p=128),
                            waits=wslot.wait(slot), sem=wsem)
                        if pi == 0:
                            for fo_i in range(4):
                                fc = fsl * 4 + fo_i
                                bank = fo_i % 4
                                bw = fbank.wait(bank)
                                for kc in range(NKC):
                                    p.op("tensor",
                                         lambda e, b=bank, k=kc, s=slot,
                                         fi=fo_i: e.matmul(
                                             psf[b][:, :],
                                             wsl[s][:, k * 512 + fi * 128:
                                                    k * 512 + (fi + 1) * 128],
                                             fm_buf[:, k * 512:(k + 1) * 512],
                                             start=(k == 0),
                                             stop=(k == NKC - 1)),
                                         waits=([("ev", xhat_done),
                                                 ("vv", xhat_vv),
                                                 (wsem, wl)] + bw)
                                         if kc == 0 else [],
                                         incs=([("pe", 1)]
                                               if kc == NKC - 1 else []))
                                grp = p.n("pe")
                                ev_w = [("pe", grp), ("dc", dc_all[0])]
                                if first_evac and st > 0:
                                    ev_w.append(("ds", scst_prev))
                                first_evac = False
                                p.op("scalar",
                                     lambda e, b=bank, c=fc: e.activation(
                                         qfm_buf[:, c * 512:(c + 1) * 512],
                                         psf[b][:, :], AF.Identity,
                                         bias=bqcol[:, c:c + 1]),
                                     waits=ev_w, incs=[("ev", 1)])
                                fbank.set(bank, ("ev", p.n("ev")))
                            wslot.set(slot, ("pe", p.n("pe")))
                            continue
                        for tc in range(4):
                            bank = tc % 4
                            bw = fbank.wait(bank)
                            for kc in range(NKC):
                                p.op("tensor",
                                     lambda e, b=bank, k=kc, t=tc, s=slot:
                                     e.matmul(
                                         psf[b][:, :],
                                         fm_buf[:, k * 512 + t * 128:
                                                k * 512 + (t + 1) * 128],
                                         wsl[s][:, k * 512:(k + 1) * 512],
                                         start=(k == 0), stop=False),
                                     waits=([("ev", xhat_done),
                                             ("vv", xhat_vv),
                                             (wsem, wl)] + bw)
                                     if kc == 0 else [])
                            p.op("tensor",
                                 lambda e, b=bank, fo=fofs, f=fsl: e.matmul(
                                     psf[b][:, :], ones_bf[0:1, 0:128],
                                     bqr[0:1, fo + f * 512:fo + (f + 1) * 512],
                                     start=False, stop=True),
                                 waits=[("dg", dg_all[0]), ("gp", setup_gp)],
                                 incs=[("pe", 1)])
                            grp = p.n("pe")
                            ev_w = [("pe", grp)]
                            if first_evac and st > 0:
                                ev_w.append(("ds", scst_prev))
                            first_evac = False
                            p.op("scalar",
                                 lambda e, d=dst, b=bank, t=tc, f=fsl: e.copy(
                                     d[:, t * D + f * 512:t * D + (f + 1) * 512],
                                     psf[b][:, :]),
                                 waits=ev_w, incs=[("ev", 1)])
                            fbank.set(bank, ("ev", p.n("ev")))
                        wslot.set(slot, ("pe", p.n("pe")))
                qkv_ev = p.n("ev")

                dma_sync(
                    qsc[:, st * 512:(st + 1) * 512].rearrange(
                        "(hc p) t -> p hc t", p=128),
                    qfm_buf[:, :].rearrange("p (hc t) -> p hc t", hc=NKC),
                    waits=[("ev", qkv_ev)], sem="ds")
                for (src, dstt) in ((k_tm, ksc), (v_tm, vsc)):
                    dma_sync(
                        dstt[st * 512:(st + 1) * 512, :].rearrange(
                            "(tc p) d -> p tc d", p=128),
                        src[:, :].rearrange("p (tc d) -> p tc d", tc=4),
                        waits=[("ev", qkv_ev)], sem="ds")

                for h in range(NH):
                    bank = 4 + h % 2
                    bw = fbank.wait(bank)
                    for cc in range(4):
                        sl = (h * 4 + cc) % 2
                        p.op("scalar",
                             lambda e, hh=h, c=cc, s=sl: e.activation(
                                 kdb[:, s * 128:(s + 1) * 128],
                                 k_tm[:, c * D + hh * 128:
                                      c * D + (hh + 1) * 128],
                                 AF.Copy,
                                 scale=dkx[:, hh * 4 + c:hh * 4 + c + 1]),
                             waits=[("ev", qkv_ev), ("dc", dc_all[0])] +
                             kd_slot.wait(sl),
                             incs=[("ev", 1)])
                        kd_ok = p.n("ev")
                        p.op("tensor",
                             lambda e, hh=h, c=cc, b=bank, s=sl: e.matmul(
                                 psf[b][:, 0:128],
                                 kdb[:, s * 128:(s + 1) * 128],
                                 v_tm[:, c * D + hh * 128:
                                      c * D + (hh + 1) * 128],
                                 start=(c == 0), stop=(c == 3)),
                             waits=[("ev", kd_ok)] + (bw if cc == 0 else []),
                             incs=[("pe", 1)])
                        kd_slot.set(sl, ("pe", p.n("pe")))
                    grp = p.n("pe")
                    l512 = float(np.float32(lam[h] ** 512))
                    p.op("vector",
                         lambda e, hh=h, b=bank, l=l512: e.scalar_tensor_tensor(
                             S_delta[:, hh * 128:(hh + 1) * 128],
                             S_delta[:, hh * 128:(hh + 1) * 128], l,
                             psf[b][:, 0:128], ALU.mult, ALU.add),
                         waits=[("pe", grp), ("vv", sdelta_init)],
                         incs=[("vv", 1)])
                    fbank.set(bank, ("vv", p.n("vv")))

                if st + 1 < n_st:
                    load_x(st + 1)

            ph1_vv = p.n("vv")
            dma_sync(cc_in[:, :], S_delta[:, :], waits=[("vv", ph1_vv)],
                     sem="ds")
            sdel_st = p.n("ds")
            p.op("gpsimd", lambda e: e.collective_compute(
                "AllGather", ALU.bypass, replica_groups=rg,
                ins=[cc_in.ap().opt()], outs=[cc_out.ap().opt()]),
                 waits=[("ds", sdel_st)], incs=[("cc", 1)])
            p.op("vector", lambda e: e.memset(S_state[:, :], 0.0),
                 waits=[("ds", sdel_st)], incs=[("vv", 1)])
            gslot = Slots(2)
            gb0 = x_buf[0][:, 0:NH * 128]
            gb1 = x_buf[0][:, NH * 128:2 * NH * 128]
            for r in range(NCORES):
                gb = gb0 if r % 2 == 0 else gb1
                gsm = "sg2" if r % 2 == 0 else "sg2b"
                gw = [("cc", 1)] + gslot.wait(r % 2)
                if r == 0:
                    gw += xslot.wait(0)
                gl = dma_sync(gb, cc_out[r * 128:(r + 1) * 128, :],
                              waits=gw, sem=gsm)
                for h in range(NH):
                    p.op("vector",
                         lambda e, b=gb, hh=h, rr=r:
                         e.scalar_tensor_tensor(
                             S_state[:, hh * 128:(hh + 1) * 128],
                             b[:, hh * 128:(hh + 1) * 128],
                             sinw[:, rr * NH + hh:rr * NH + hh + 1],
                             S_state[:, hh * 128:(hh + 1) * 128],
                             ALU.mult, ALU.add),
                         waits=[(gsm, gl)] if h == 0 else [],
                         incs=([("vv", 1)] if h == NH - 1 else []))
                gslot.set(r % 2, ("vv", p.n("vv")))
            sin_done = p.n("vv")
            xslot.set(0, ("vv", sin_done))

            ph1_scope.close()
            ph2_scope = ExitStack()

            def p2sb(name, shape, dt=BF16):
                return ph2_scope.enter_context(nc.sbuf_tensor(name, shape, dt))

            boutr = p2sb("boutr", [1, D])
            dmaskT = p2sb("dmaskT_s", [128, NH * 128], F32)
            decayq = p2sb("decayq_s", [128, NH * 128], F32)
            dk1 = p2sb("dk1_s", [128, NH], F32)
            qf_st = p2sb("qf_st", [128, NH * 512])
            kf16 = p2sb("kf16", [128, NH * 128])
            smb16 = p2sb("smb16", [128, NH * 128])
            sbf16 = p2sb("sbf16", [128, NH * 128])
            dma_gp(boutr[:, :], bout_row[:, :])
            dw2b = p.n("dg")
            dma_sync(dmaskT[:, :], dmaskT_in[:, :])
            dma_sync(decayq[:, :], decayq_in[:, :])
            ph2_dc = dma_sync(dk1[:, :], dk1_in[:, :])
            qkv_load = {}
            xres_load = {}

            qkv_free = [("pe", 0)]

            def load_qkv(st):
                w0 = [("ds", p.n("ds")), qkv_free[0]]
                vals = [dma_sync(
                    qf_st[:, :].rearrange("p (hc t) -> p hc t", hc=NH),
                    qsc[:, st * 512:(st + 1) * 512].rearrange(
                        "(hc p) t -> p hc t", p=128),
                    waits=w0, sem="sq")]
                w0 = [qkv_free[0]]
                for (srct, dst, sname) in ((ksc, k_tm, "sk"),
                                           (vsc, v_tm, "sv")):
                    vals.append(dma_sync(
                        dst[:, :].rearrange("p (tc d) -> p tc d", tc=4),
                        srct[st * 512:(st + 1) * 512, :].rearrange(
                            "(tc p) d -> p tc d", p=128),
                        waits=w0, sem=sname))
                qkv_load[st] = vals

            def load_xres(st):
                xres_load[st] = dma_sync(
                    x_buf[st % 2][:, :].rearrange("p (tc d) -> p tc d", tc=4),
                    x_ext[st * 512:(st + 1) * 512, :].rearrange(
                        "(tc p) d -> p tc d", p=128),
                    waits=xslot.wait(0), sem="sx")

            qf_free = [[]]
            kf_free = [[]]
            smb_free = [[]]
            supd_prev = [0]
            inter_prev = [0]
            xh_stored = {}
            from concourse.bass_core_rust import IndirectOffsetOnAxis
            p.op("gpsimd", lambda e: e.memset(gates_fm[:, :], 0.5),
                 incs=[("gp", 1)])
            gfm_init = p.n("gp")

            for st in range(n_st):
                if st == 0:
                    load_qkv(0)
                    load_xres(0)
                ldq, ldk, ldv = qkv_load[st]
                first_ch = True
                for cc in range(4):
                    k_ev = {}
                    k_vv = {}
                    for h in range(NH):
                        bb = tr_par[0]; tr_par[0] ^= 1
                        w0 = list(bbank.wait(bb))
                        if h == 0 and cc == 0:
                            w0 += [("sq", ldq), ("sk", ldk), ("sv", ldv)]
                        if h == 0:
                            w0 += kf_free[0]
                        if first_ch:
                            w0 += [("vv", sin_done)]
                            first_ch = False
                        p.op("tensor", lambda e, c=cc, hh=h, b=bb: e.transpose(
                            psb[b][:, 0:128],
                            k_tm[:, c * D + hh * 128:c * D + (hh + 1) * 128],
                            ident_bf[:, :]), waits=w0, incs=[("pe", 1)])
                        if h % 2 == 0:
                            p.op("scalar", lambda e, hh=h, b=bb: e.copy(
                                kf16[:, hh * 128:(hh + 1) * 128],
                                psb[b][:, 0:128]),
                                 waits=[("pe", p.n("pe"))], incs=[("ev", 1)])
                            bbank.set(bb, ("ev", p.n("ev")))
                            k_ev[h] = p.n("ev")
                            k_vv[h] = 0
                        else:
                            p.op("vector", lambda e, hh=h, b=bb: e.tensor_copy(
                                kf16[:, hh * 128:(hh + 1) * 128],
                                psb[b][:, 0:128]),
                                 waits=[("pe", p.n("pe"))], incs=[("vv", 1)])
                            bbank.set(bb, ("vv", p.n("vv")))
                            k_ev[h] = 0
                            k_vv[h] = p.n("vv")
                    mask_vv = {}
                    s_mm = {}
                    for h in range(NH):
                        sbank = h % 2
                        kw = ([("ev", k_ev[h])] if k_ev[h] else
                              [("vv", k_vv[h])])
                        if h == 0:
                            kw += [("sq", ldq)] + qf_free[0]
                        p.op("tensor", lambda e, c=cc, hh=h, b=sbank: e.matmul(
                            psf[b][:, 0:128],
                            kf16[:, hh * 128:(hh + 1) * 128],
                            qf_st[:, hh * 512 + c * 128:
                                  hh * 512 + (c + 1) * 128],
                            start=True, stop=True),
                             waits=kw + fbank.wait(sbank),
                             incs=[("pe", 1)])
                        s_mm[h] = p.n("pe")
                        mw = [("pe", s_mm[h])]
                        if h == 0:
                            mw += smb_free[0] + [("dc", ph2_dc)]
                        p.op("vector", lambda e, hh=h, b=sbank: e.tensor_tensor(
                            smb16[:, hh * 128:(hh + 1) * 128],
                            psf[b][:, 0:128],
                            dmaskT[:, hh * 128:(hh + 1) * 128], ALU.mult),
                             waits=mw, incs=[("vv", 1)])
                        mask_vv[h] = p.n("vv")
                        fbank.set(sbank, ("vv", mask_vv[h]))
                    kf_free[0] = [("pe", s_mm[NH - 1])]
                    p.op("vector", lambda e, c=cc: e.tensor_tensor(
                        qf_st[:, :].rearrange("p (h t) -> p h t", h=NH)[
                            :, :, c * 128:(c + 1) * 128],
                        qf_st[:, :].rearrange("p (h t) -> p h t", h=NH)[
                            :, :, c * 128:(c + 1) * 128],
                        decayq[:, :].rearrange("p (h t) -> p h t", h=NH),
                        ALU.mult),
                         waits=[("pe", s_mm[NH - 1]), ("dc", ph2_dc)],
                         incs=[("vv", 1)])
                    qt_ok = p.n("vv")
                    p.op("scalar", lambda e: e.copy(sbf16[:, :], S_state[:, :]),
                         waits=[("vv", supd_prev[0]), ("pe", inter_prev[0])],
                         incs=[("ev", 1)])
                    snap_ok = p.n("ev")
                    o_mm = {}
                    for h in range(NH):
                        ob = 2 + h % 2
                        p.op("tensor", lambda e, c=cc, hh=h, b=ob: e.matmul(
                            psf[b][:, 0:128],
                            v_tm[:, c * D + hh * 128:c * D + (hh + 1) * 128],
                            smb16[:, hh * 128:(hh + 1) * 128],
                            start=True, stop=False),
                             waits=[("vv", mask_vv[h])] + fbank.wait(ob))
                        p.op("tensor", lambda e, c=cc, hh=h, b=ob: e.matmul(
                            psf[b][:, 0:128],
                            sbf16[:, hh * 128:(hh + 1) * 128],
                            qf_st[:, hh * 512 + c * 128:
                                  hh * 512 + (c + 1) * 128],
                            start=False, stop=True),
                             waits=[("vv", qt_ok), ("ev", snap_ok)]
                             if h == 0 else [],
                             incs=[("pe", 1)])
                        o_mm[h] = p.n("pe")
                        evw = [("pe", o_mm[h])]
                        if cc == 0 and h == 0:
                            evw.append(("dsx", p.n("dsx")))
                        p.op("scalar", lambda e, c=cc, hh=h, b=ob: e.copy(
                            fm_buf[:, hh * 512 + c * 128:
                                   hh * 512 + (c + 1) * 128],
                            psf[b][:, 0:128]),
                             waits=evw, incs=[("ev", 1)])
                        fbank.set(ob, ("ev", p.n("ev")))
                    inter_prev[0] = o_mm[NH - 1]
                    qf_free[0] = [("pe", o_mm[NH - 1])]
                    last_ds = 0
                    for h in range(NH):
                        p.op("scalar", lambda e, c=cc, hh=h: e.activation(
                            smb16[:, hh * 128:(hh + 1) * 128],
                            k_tm[:, c * D + hh * 128:c * D + (hh + 1) * 128],
                            AF.Copy, scale=dk1[:, hh:hh + 1]),
                             waits=[("pe", o_mm[h])], incs=[("ev", 1)])
                        kd_ok = p.n("ev")
                        db = 4 + h % 2
                        p.op("tensor", lambda e, c=cc, hh=h, b=db: e.matmul(
                            psf[b][:, 0:128],
                            smb16[:, hh * 128:(hh + 1) * 128],
                            v_tm[:, c * D + hh * 128:c * D + (hh + 1) * 128],
                            start=True, stop=True),
                             waits=[("ev", kd_ok)] + fbank.wait(db),
                             incs=[("pe", 1)])
                        ds_mm = p.n("pe")
                        last_ds = ds_mm
                        l128 = float(np.float32(lam[h] ** 128))
                        p.op("vector",
                             lambda e, hh=h, b=db, l=l128:
                             e.scalar_tensor_tensor(
                                 S_state[:, hh * 128:(hh + 1) * 128],
                                 S_state[:, hh * 128:(hh + 1) * 128], l,
                                 psf[b][:, 0:128], ALU.mult, ALU.add),
                             waits=[("pe", ds_mm), ("ev", snap_ok)],
                             incs=[("vv", 1)])
                        fbank.set(db, ("vv", p.n("vv")))
                    supd_prev[0] = p.n("vv")
                    smb_free[0] = [("pe", last_ds)]
                attn_ev = p.n("ev")
                qkv_free[0] = ("pe", p.n("pe"))
                if st + 1 < n_st:
                    load_qkv(st + 1)

                for fsl in range(4):
                    slot = fsl % 2
                    wsem = "dwa" if slot == 0 else "dwb"
                    wl = dma_gp(
                        wsl[slot][:, :].rearrange("p (kc f) -> p kc f", kc=NKC),
                        woutT[:, fsl * 512:(fsl + 1) * 512].rearrange(
                            "(kc p) f -> p kc f", p=128),
                        waits=wslot.wait(slot), sem=wsem)
                    for tc in range(4):
                        bank = tc % 4
                        bw = fbank.wait(bank)
                        for dc in range(NKC):
                            p.op("tensor",
                                 lambda e, b=bank, d=dc, t=tc, s=slot:
                                 e.matmul(
                                     psf[b][:, :],
                                     fm_buf[:, d * 512 + t * 128:
                                            d * 512 + (t + 1) * 128],
                                     wsl[s][:, d * 512:(d + 1) * 512],
                                     start=(d == 0), stop=False),
                                 waits=([("ev", attn_ev), (wsem, wl)] + bw)
                                 if dc == 0 else [])
                        p.op("tensor", lambda e, b=bank, f=fsl: e.matmul(
                            psf[b][:, :], ones_bf[0:1, 0:128],
                            boutr[0:1, f * 512:(f + 1) * 512],
                            start=False, stop=True),
                             waits=[("dg", dw2b)], incs=[("pe", 1)])
                        grp = p.n("pe")
                        p.op("vector",
                             lambda e, b=bank, t=tc, f=fsl, xb=x_buf[st % 2]:
                             e.tensor_tensor(
                                 xb[:, t * D + f * 512:t * D + (f + 1) * 512],
                                 psf[b][:, :],
                                 xb[:, t * D + f * 512:t * D + (f + 1) * 512],
                                 ALU.add),
                             waits=[("pe", grp), ("sx", xres_load[st])],
                             incs=[("vv", 1)])
                        fbank.set(bank, ("vv", p.n("vv")))
                    wslot.set(slot, ("pe", p.n("pe")))
                res_vv = p.n("vv")
                dma_sync(
                    out_ext[st * 512:(st + 1) * 512, :].rearrange(
                        "(tc p) d -> p tc d", p=128),
                    x_buf[st % 2][:, :].rearrange("p (tc d) -> p tc d", tc=4),
                    waits=[("vv", res_vv)], sem="ds")
                st_store = p.n("ds")

                ln_tm(lambda tc, b=x_buf[st % 2]: b[:, tc * D:(tc + 1) * D], 4,
                      lambda tc, kc: fm_buf[:, kc * 512 + tc * 128:
                                            kc * 512 + (tc + 1) * 128],
                      [("vv", res_vv)],
                      ts_dst_fn=lambda tc: rt[tc][:, 0:D],
                      ts_waits_fn=lambda tc: rt_free.wait(tc))
                xh_done = p.n("ev")
                ln_vv = p.n("vv")
                xh_stored[st] = dma_sync(
                    xhsc[:, st * 512:(st + 1) * 512].rearrange(
                        "(kc p) t -> p kc t", p=128),
                    fm_buf[:, :].rearrange("p (kc t) -> p kc t", kc=NKC),
                    waits=[("ev", xh_done), ("vv", ln_vv),
                           ("dsx", p.n("dsx"))], sem="dsx")
                xslot.set(0, ("ds", st_store), ("vv", ln_vv),
                          ("ev", xh_done))
                if st + 1 < n_st:
                    load_xres(st + 1)

                for tc in range(4):
                    tg = st * 4 + tc
                    b_r = tc % 2
                    bw = fbank.wait(b_r)
                    for kc in range(NKC):
                        p.op("tensor",
                             lambda e, b=b_r, k=kc, t=tc: e.matmul(
                                 psf[b][:, 0:NR],
                                 fm_buf[:, k * 512 + t * 128:
                                        k * 512 + (t + 1) * 128],
                                 rw_bf[:, k * NR:(k + 1) * NR],
                                 start=(k == 0), stop=False),
                             waits=([("ev", xh_done), ("vv", ln_vv),
                                     ("dg", dg_all[0])] + bw)
                             if kc == 0 else [])
                    p.op("tensor", lambda e, b=b_r: e.matmul(
                        psf[b][:, 0:NR], ones_bf[0:1, 0:128], rbr[0:1, :],
                        start=False, stop=True), incs=[("pe", 1)])
                    rt_mm = p.n("pe")
                    p.op("vector", lambda e, b=b_r: e.reduce_max(
                        stat[:, 48:49], psf[b][:, 0:NR], axis=AX.X,
                        negate=True),
                         waits=[("pe", rt_mm)], incs=[("vv", 1)])
                    nm_ok = p.n("vv")
                    p.op("scalar", lambda e, b=b_r: e.activation(
                        e_buf[:, :], psf[b][:, 0:NR], AF.Exp,
                        bias=stat[:, 48:49]),
                         waits=[("vv", nm_ok)], incs=[("ev", 1)])
                    fbank.set(b_r, ("ev", p.n("ev")))
                    exp_ok = p.n("ev")
                    p.op("vector", lambda e: e.max(out=m8[:, :],
                                                   in_=e_buf[:, :]),
                         waits=[("ev", exp_ok)], incs=[("vv", 1)])
                    p.op("vector", lambda e: e.memset(m8[:, TOPK:8], 0.0),
                         incs=[("vv", 1)])
                    p.op("vector", lambda e: e.match_replace(
                        out=erem[:, :], in_to_replace=m8[:, :],
                        in_values=e_buf[:, :], imm_value=0.0),
                         incs=[("vv", 1)])
                    p.op("vector", lambda e: e.tensor_tensor(
                        gacc[:, :], e_buf[:, :], erem[:, :], ALU.subtract),
                         incs=[("vv", 1)])
                    p.op("vector", lambda e: e.reduce_sum(
                        stat[:, 49:50], gacc[:, :], axis=AX.X),
                         incs=[("vv", 1)])
                    p.op("vector", lambda e: e.reciprocal(
                        stat[:, 50:51], stat[:, 49:50]),
                         incs=[("vv", 1)])
                    p.op("vector", lambda e: e.tensor_scalar_mul(
                        gtmp[:, :], gacc[:, :], stat[:, 50:51]),
                         incs=[("vv", 1)])
                    p.op("vector", lambda e: e.tensor_copy(
                        gbf_tm[:, 0:NR], gtmp[:, :]), incs=[("vv", 1)])
                    g_ok = p.n("vv")
                    b = tr_par[0]; tr_par[0] ^= 1
                    p.op("tensor", lambda e, bb=b: e.transpose(
                        psb[bb][0:NR, 0:128], gbf_tm[:, :], ident_bf[:, :]),
                         waits=[("vv", g_ok)] + bbank.wait(b),
                         incs=[("pe", 1)])
                    p.op("scalar", lambda e, bb=b, t=tg: e.copy(
                        gates_fm[0:nr_red, t * 128:(t + 1) * 128],
                        psb[bb][0:nr_red, 0:128]),
                         waits=[("pe", p.n("pe")), ("gp", gfm_init)],
                         incs=[("ev", 1)])
                    bbank.set(b, ("ev", p.n("ev")))
                    p.op("scalar", lambda e, t=tc: e.copy(
                        rt[t][:, D:D + NR], gtmp[:, :]),
                         waits=[("vv", g_ok)], incs=[("ev", 1)])
                    p.op("vector", lambda e, t=tc: e.tensor_copy(
                        ghif[:, :], rt[t][:, D:D + NR]),
                         waits=[("ev", p.n("ev"))], incs=[("vv", 1)])
                    p.op("vector", lambda e, t=tc: e.tensor_tensor(
                        rt[t][:, D + NR:D + 2 * NR], gtmp[:, :], ghif[:, :],
                        ALU.subtract), incs=[("vv", 1)])
                    p.op("scalar", lambda e, t=tc, g=tg: e.copy(
                        rt[t][:, D + 2 * NR:D + 2 * NR + 1],
                        tokid[:, g:g + 1]),
                         waits=[("dc", dc_all[0])], incs=[("ev", 1)])
                    p.op("vector", lambda e, t=tc: e.tensor_copy(
                        tidf[:, :], rt[t][:, D + 2 * NR:D + 2 * NR + 1]),
                         waits=[("ev", p.n("ev"))], incs=[("vv", 1)])
                    p.op("vector", lambda e, t=tc, g=tg: e.tensor_tensor(
                        rt[t][:, D + 2 * NR + 1:D + 2 * NR + 2],
                        tokid[:, g:g + 1], tidf[:, :], ALU.subtract),
                         incs=[("vv", 1)])
                    row_ev = p.n("ev")
                    p.op("vector", lambda e: e.tensor_single_scalar(
                        mk_f[:, :], gacc[:, :], 0.0, ALU.is_gt),
                         incs=[("vv", 1)])
                    p.op("vector", lambda e: e.tensor_copy(
                        mk_b[:, :], mk_f[:, :]), incs=[("vv", 1)])
                    mk_ok = p.n("vv")
                    b_p = 2 + tc % 2
                    bwp = fbank.wait(b_p)
                    p.op("tensor", lambda e, b=b_p: e.matmul(
                        psf[b][:, 0:NR], triu_sb[:, :], mk_b[:, :],
                        start=True, stop=True),
                         waits=[("vv", mk_ok), ("dc", dc_all[0])] + bwp,
                         incs=[("pe", 1)])
                    pcum_mm = p.n("pe")
                    b_c = 4 + tc % 2
                    bwc = fbank.wait(b_c)
                    p.op("tensor", lambda e, b=b_c: e.matmul(
                        psf[b][0:1, 0:NR], onesc_bf[:, :], mk_b[:, :],
                        start=True, stop=True),
                         waits=[("vv", mk_ok), ("gp", setup_gp)] + bwc,
                         incs=[("pe", 1)])
                    cnt_mm = p.n("pe")
                    p.op("vector", lambda e, b=b_p: e.tensor_tensor(
                        pe_t[:, :], psf[b][:, 0:NR], offs_full[:, :],
                        ALU.add),
                         waits=[("pe", pcum_mm)], incs=[("vv", 1)])
                    fbank.set(b_p, ("vv", p.n("vv")))
                    p.op("vector", lambda e: e.tensor_single_scalar(
                        ov_t[:, :], pe_t[:, :], CAP - 0.5, ALU.is_ge),
                         incs=[("vv", 1)])
                    p.op("vector", lambda e: e.tensor_scalar(
                        ov_t[:, :], ov_t[:, :], -1.0, 1.0, ALU.mult, ALU.add),
                         incs=[("vv", 1)])
                    p.op("vector", lambda e: e.tensor_tensor(
                        vd_t[:, :], mk_f[:, :], ov_t[:, :], ALU.mult),
                         incs=[("vv", 1)])
                    p.op("vector", lambda e: e.tensor_tensor(
                        pe_t[:, :], pe_t[:, :], eCrow[:, :], ALU.add),
                         waits=[("dc", dc_all[0])], incs=[("vv", 1)])
                    p.op("vector", lambda e, g=tg: e.tensor_single_scalar(
                        tr_t[:, :], trashb[:, :], float(g * NR),
                        ALU.add), incs=[("vv", 1)])
                    p.op("vector", lambda e: e.tensor_tensor(
                        pe_t[:, :], vd_t[:, :], pe_t[:, :], ALU.mult),
                         incs=[("vv", 1)])
                    p.op("vector", lambda e: e.tensor_scalar(
                        vd_t[:, :], vd_t[:, :], -1.0, 1.0, ALU.mult, ALU.add),
                         incs=[("vv", 1)])
                    p.op("vector", lambda e: e.tensor_tensor(
                        tr_t[:, :], vd_t[:, :], tr_t[:, :], ALU.mult),
                         incs=[("vv", 1)])
                    p.op("vector", lambda e: e.tensor_tensor(
                        of_t[:, :], pe_t[:, :], tr_t[:, :], ALU.add),
                         incs=[("vv", 1)])
                    p.op("vector", lambda e, t=tc: e.tensor_copy(
                        oint[t][:, :], of_t[:, :]),
                         waits=oint_free.wait(tc), incs=[("vv", 1)])
                    oint_ok = p.n("vv")
                    p.op("scalar", lambda e, b=b_c: e.copy(
                        cntb[0:1, :], psf[b][0:1, 0:NR]),
                         waits=[("pe", cnt_mm)], incs=[("ev", 1)])
                    cnt_ev = p.n("ev")
                    fbank.set(b_c, ("ev", cnt_ev))
                    b_c2 = 4 + (tc + 1) % 2
                    bwc2 = fbank.wait(b_c2)
                    p.op("tensor", lambda e, b=b_c2: e.matmul(
                        psf[b][:, 0:NR], ones_bf[0:1, 0:128], cntb[0:1, :],
                        start=True, stop=True),
                         waits=[("ev", cnt_ev)] + bwc2, incs=[("pe", 1)])
                    cb_mm = p.n("pe")
                    p.op("vector", lambda e, b=b_c2: e.tensor_tensor(
                        offs_full[:, :], offs_full[:, :], psf[b][:, 0:NR],
                        ALU.add),
                         waits=[("pe", cb_mm)], incs=[("vv", 1)])
                    fbank.set(b_c2, ("vv", p.n("vv")))
                    sc_w = [("vv", oint_ok), ("ev", row_ev), ("dz", dz_all)]
                    for re_ in range(nr_red):
                        p.op("gpsimd",
                             lambda e, t=tc, ee=re_: e.indirect_dma_start(
                                 out=xg_dram[:, :],
                                 out_offset=IndirectOffsetOnAxis(
                                     oint[t][:, ee:ee + 1], 0),
                                 in_=rt[t][:, :], in_offset=None,
                                 oob_is_err=False),
                             waits=sc_w if re_ == 0 else [],
                             incs=[("dsc", 16)])
                for _tc in range(4):
                    rt_free.set(_tc, ("dsc", p.n("dsc")))
                    oint_free.set(_tc, ("dsc", p.n("dsc")))
            xmid_all = p.n("ds")
            gates_ev = p.n("ev")
            gates_vv = p.n("vv")
            dsc_all = p.n("dsc")
            xh_stored_all = dict(xh_stored)

            p.barrier()
            ph2_scope.close()

        with ExitStack() as fsco:
            def fsb(name, shape, dt=F32):
                return fsco.enter_context(nc.sbuf_tensor(name, shape, dt))

            from concourse.bass_core_rust import IndirectOffsetOnAxis
            b2g = fsb("b2g_sb", [n_exp, D], BF16)
            b1c = fsb("b1c", [128, n_exp * NHC])
            dma_gp(b2g[:, :], b2g_in[:, :])
            dg_all[0] = p.n("dg")
            b1c_ld = dma_sync(b1c[:, :], b1cols_in[:, :])
            xg_tm = [fsb(f"xg_tm{i}", [128, 4 * ROWW], BF16) for i in range(2)]
            xg_fm = fsb("xg_fm", [128, NKC * 512], BF16)
            h_sb = fsb("h_sb", [128, NHC * 512], BF16)
            w1_sb = fsb("w1_sb", [128, NKC * HID], BF16)
            w2_sb = fsb("w2_sb", [128, NHC * D], BF16)
            contrib = [fsb(f"contrib{i}", [128, D]) for i in range(2)]
            gcol = fsb("gcol", [128, 8])
            tid_i = fsb("tid_i", [128, 8], I32)
            xq_scope = ExitStack()
            xq_buf = xq_scope.enter_context(
                nc.sbuf_tensor("xq_buf", [128, NKC * 512], BF16))

            ctr = Slots(4)
            first_accum = [True]

            def accum_waits(wait_ev):
                w = [("ev", wait_ev), ("da", p.n("da"))]
                if first_accum[0]:
                    w.append(("ds", xmid_all))
                    first_accum[0] = False
                return w

            for q in range(n_st):
                for tc in range(4):
                    tg = q * 4 + tc
                    slot = tg % 2
                    cw = ctr.wait(slot)
                    for fs in range(4):
                        bank = fs % 2
                        bw = fbank.wait(bank)
                        p.op("tensor", lambda e, b=bank, t=tg, f=fs: e.matmul(
                            psf[b][:, :], gates_fm[:, t * 128:(t + 1) * 128],
                            b2g[:, f * 512:(f + 1) * 512],
                            start=True, stop=True),
                             waits=[("ev", gates_ev), ("gp", gfm_init),
                                    ("dg", dg_all[0])] + bw,
                             incs=[("pe", 1)])
                        grp = p.n("pe")
                        p.op("scalar", lambda e, b=bank, s=slot, f=fs: e.copy(
                            contrib[s][:, f * 512:(f + 1) * 512],
                            psf[b][:, :]),
                             waits=[("pe", grp)] + (cw if fs == 0 else []),
                             incs=[("ev", 1)])
                        fbank.set(bank, ("ev", p.n("ev")))
                    p.op("gpsimd", lambda e, q2=q, t2=tc, s=slot: e.dma_start(
                        out=out_ext[q2 * 512 + t2 * 128:
                                    q2 * 512 + (t2 + 1) * 128, :],
                        in_=contrib[s][:, :], accum_op=ALU.add),
                         waits=accum_waits(p.n("ev")), incs=[("da", 16)])
                    ctr.set(slot, ("da", p.n("da")))

            w1_loaded = [0]
            w2_loaded = [0]
            order = [nr_red, nr_red + 1] + list(range(nr_red))

            def load_w1(e_, wait_pe):
                dma_gp(w1_sb[:, :].rearrange("p (kc h) -> p kc h", kc=NKC),
                       w1_all[e_].rearrange("(kc p) h -> p kc h", p=128),
                       waits=[("pe", wait_pe)], sem="dw1")
                w1_loaded[0] = p.n("dw1")

            def load_w2(e_, wait_pe):
                dma_gp(w2_sb[:, :].rearrange("p (hc f) -> p hc f", hc=NHC),
                       w2_all[e_].rearrange("(hc p) f -> p hc f", p=128),
                       waits=[("pe", wait_pe)], sem="dw2")
                w2_loaded[0] = p.n("dw2")

            load_w1(order[0], 0)
            load_w2(order[0], 0)

            xq_load = [0]
            xq_free = [("pe", 0)]

            def load_xq(q_):
                xq_load[0] = dma_sync(
                    xq_buf[:, :].rearrange("p (kc t) -> p kc t", kc=NKC),
                    xhsc[:, q_ * 512:(q_ + 1) * 512].rearrange(
                        "(kc p) t -> p kc t", p=128),
                    waits=[xq_free[0], ("dsx", xh_stored_all[q_])],
                    sem="sxq0")

            for si, se in enumerate(order[:2]):
                w1_this = w1_loaded[0]
                w2_this = w2_loaded[0]
                for q in range(n_st):
                    load_xq(q)
                    xl = xq_load[0]
                    for hc in range(NHC):
                        bank = hc % 2
                        bw = fbank.wait(bank)
                        for kc in range(NKC):
                            p.op("tensor",
                                 lambda e, b=bank, hh=hc, k=kc: e.matmul(
                                     psf[b][:, :],
                                     w1_sb[:, k * HID + hh * 128:
                                           k * HID + (hh + 1) * 128],
                                     xq_buf[:, k * 512:(k + 1) * 512],
                                     start=(k == 0), stop=(k == NKC - 1)),
                                 waits=([("dw1", w1_this), ("sxq0", xl)] + bw)
                                 if kc == 0 else [],
                                 incs=([("pe", 1)] if kc == NKC - 1 else []))
                        grp = p.n("pe")
                        p.op("scalar",
                             lambda e, b=bank, hh=hc, ee=se: e.activation(
                                 h_sb[:, hh * 512:(hh + 1) * 512],
                                 psf[b][:, :], AF.Relu,
                                 bias=b1c[:, ee * NHC + hh:
                                          ee * NHC + hh + 1]),
                             waits=[("pe", grp), ("dc", b1c_ld)],
                             incs=[("ev", 1)])
                        fbank.set(bank, ("ev", p.n("ev")))
                    h_ev_end = p.n("ev")
                    xq_free[0] = ("pe", p.n("pe"))
                    for tc in range(4):
                        tg = q * 4 + tc
                        cslot = tg % 2
                        cw = ctr.wait(cslot)
                        for fs in range(4):
                            bank = 2 + (tc * 4 + fs) % 2
                            bw = fbank.wait(bank)
                            for hc in range(NHC):
                                p.op("tensor",
                                     lambda e, b=bank, hh=hc, t=tc, f=fs:
                                     e.matmul(
                                         psf[b][:, :],
                                         h_sb[:, hh * 512 + t * 128:
                                              hh * 512 + (t + 1) * 128],
                                         w2_sb[:, hh * D + f * 512:
                                               hh * D + (f + 1) * 512],
                                         start=(hh == 0),
                                         stop=(hh == NHC - 1)),
                                     waits=([("dw2", w2_this),
                                             ("ev", h_ev_end)] + bw)
                                     if hc == 0 else [],
                                     incs=([("pe", 1)] if hc == NHC - 1
                                           else []))
                            grp = p.n("pe")
                            p.op("scalar", lambda e, b=bank, s=cslot, f=fs:
                                 e.activation(
                                     contrib[s][:, f * 512:(f + 1) * 512],
                                     psf[b][:, :], AF.Copy, scale=0.5),
                                 waits=[("pe", grp)] + (cw if fs == 0 else []),
                                 incs=[("ev", 1)])
                            fbank.set(bank, ("ev", p.n("ev")))
                        p.op("gpsimd",
                             lambda e, q2=q, t2=tc, s=cslot: e.dma_start(
                                 out=out_ext[q2 * 512 + t2 * 128:
                                             q2 * 512 + (t2 + 1) * 128, :],
                                 in_=contrib[s][:, :], accum_op=ALU.add),
                             waits=accum_waits(p.n("ev")), incs=[("da", 16)])
                        ctr.set(cslot, ("da", p.n("da")))
                    if q == n_st - 1:
                        load_w1(order[si + 1], p.n("pe"))
                z_pe_end = p.n("pe")
                load_w2(order[si + 1], z_pe_end) if si + 1 < len(order) else None

            xq_scope.close()
            contrib += [fsb(f"contrib{i}", [128, D]) for i in (2, 3)]
            xg_load = {0: 0, 1: 0}
            xg_free = Slots(2)
            xgfm_free = [0]

            for ri, re_ in enumerate(order[2:]):
                slot = ri % 2
                gsem = "sgx0" if slot == 0 else "sgx1"
                xg_load[slot] = dma_sync(
                    xg_tm[slot][:, :].rearrange("p (tc f) -> p tc f", tc=4),
                    xg_dram[re_ * CAP:(re_ + 1) * CAP, :].rearrange(
                        "(tc p) f -> p tc f", p=128),
                    waits=[("dsc", dsc_all)] + xg_free.wait(slot), sem=gsem)
                gl = xg_load[slot]
                w1_this = w1_loaded[0]
                w2_this = w2_loaded[0]
                for tc in range(4):
                    p.op("vector", lambda e, s=slot, t=tc, ee=re_:
                         e.tensor_tensor(
                             gcol[:, s * 4 + t:s * 4 + t + 1],
                             xg_tm[s][:, t * ROWW + D + ee:
                                      t * ROWW + D + ee + 1],
                             xg_tm[s][:, t * ROWW + D + NR + ee:
                                      t * ROWW + D + NR + ee + 1],
                             ALU.add),
                         waits=[(gsem, gl)], incs=[("vv", 1)])
                    p.op("vector", lambda e, s=slot, t=tc: e.tensor_tensor(
                        tidf[:, :],
                        xg_tm[s][:, t * ROWW + D + 2 * NR:
                                 t * ROWW + D + 2 * NR + 1],
                        xg_tm[s][:, t * ROWW + D + 2 * NR + 1:
                                 t * ROWW + D + 2 * NR + 2],
                        ALU.add), incs=[("vv", 1)])
                    p.op("vector", lambda e, s=slot, t=tc: e.tensor_copy(
                        tid_i[:, s * 4 + t:s * 4 + t + 1], tidf[:, :]),
                         incs=[("vv", 1)])
                gt_ok = p.n("vv")
                for tc in range(4):
                    for kc in range(NKC):
                        transpose_128(
                            xg_tm[slot][:, tc * ROWW + kc * 128:
                                        tc * ROWW + (kc + 1) * 128],
                            xg_fm[:, kc * 512 + tc * 128:
                                  kc * 512 + (tc + 1) * 128],
                            [(gsem, gl), ("pe", xgfm_free[0])] if
                            (tc == 0 and kc == 0) else [(gsem, gl)])
                tr_ev = p.n("ev")
                tr_vv = p.n("vv")
                for hc in range(NHC):
                    bank = hc % 2
                    bw = fbank.wait(bank)
                    for kc in range(NKC):
                        p.op("tensor",
                             lambda e, b=bank, hh=hc, k=kc: e.matmul(
                                 psf[b][:, :],
                                 w1_sb[:, k * HID + hh * 128:
                                       k * HID + (hh + 1) * 128],
                                 xg_fm[:, k * 512:(k + 1) * 512],
                                 start=(k == 0), stop=(k == NKC - 1)),
                             waits=([("dw1", w1_this), ("ev", tr_ev),
                                     ("vv", tr_vv)] + bw)
                             if kc == 0 else [],
                             incs=([("pe", 1)] if kc == NKC - 1 else []))
                    grp = p.n("pe")
                    p.op("scalar",
                         lambda e, b=bank, hh=hc, ee=re_: e.activation(
                             h_sb[:, hh * 512:(hh + 1) * 512],
                             psf[b][:, :], AF.Relu,
                             bias=b1c[:, ee * NHC + hh:ee * NHC + hh + 1]),
                         waits=[("pe", grp), ("dc", b1c_ld)],
                         incs=[("ev", 1)])
                    fbank.set(bank, ("ev", p.n("ev")))
                h_ev_end = p.n("ev")
                h_pe_end = p.n("pe")
                xgfm_free[0] = h_pe_end
                xg_free.set(slot, ("pe", h_pe_end))
                if ri + 1 < nr_red:
                    load_w1(order[2 + ri + 1], h_pe_end)
                for tc in range(4):
                    cslot = tc
                    cw = ctr.wait(cslot)
                    for fs in range(4):
                        bank = 2 + (tc * 4 + fs) % 2
                        bw = fbank.wait(bank)
                        for hc in range(NHC):
                            p.op("tensor",
                                 lambda e, b=bank, hh=hc, t=tc, f=fs:
                                 e.matmul(
                                     psf[b][:, :],
                                     h_sb[:, hh * 512 + t * 128:
                                          hh * 512 + (t + 1) * 128],
                                     w2_sb[:, hh * D + f * 512:
                                           hh * D + (f + 1) * 512],
                                     start=(hh == 0), stop=(hh == NHC - 1)),
                                 waits=([("dw2", w2_this),
                                         ("ev", h_ev_end)] + bw)
                                 if hc == 0 else [],
                                 incs=([("pe", 1)] if hc == NHC - 1 else []))
                        grp = p.n("pe")
                        p.op("scalar",
                             lambda e, b=bank, s=cslot, f=fs, sl=slot, t=tc:
                             e.activation(
                                 contrib[s][:, f * 512:(f + 1) * 512],
                                 psf[b][:, :], AF.Copy,
                                 scale=gcol[:, sl * 4 + t:sl * 4 + t + 1]),
                             waits=[("pe", grp), ("vv", gt_ok)] +
                             (cw if fs == 0 else []),
                             incs=[("ev", 1)])
                        fbank.set(bank, ("ev", p.n("ev")))
                    p.op("gpsimd",
                         lambda e, s=cslot, sl=slot, t=tc: e.indirect_dma_start(
                             out=out_ext[:, :],
                             out_offset=IndirectOffsetOnAxis(
                                 tid_i[:, sl * 4 + t:sl * 4 + t + 1], 0),
                             in_=contrib[s][:, :], in_offset=None,
                             oob_is_err=False, compute_op=ALU.add),
                         waits=accum_waits(p.n("ev")) + [("vv", gt_ok)],
                         incs=[("da", 16)])
                    ctr.set(cslot, ("da", p.n("da")))
                z_pe_end = p.n("pe")
                if ri + 1 < nr_red:
                    load_w2(order[2 + ri + 1], z_pe_end)

            p.op("gpsimd", lambda e: e.nop(),
                 waits=[("da", p.n("da")), ("dw1", p.n("dw1")),
                        ("dw2", p.n("dw2")), ("dg", p.n("dg")),
                        ("dwa", p.n("dwa")), ("dwb", p.n("dwb")),
                        ("dsc", p.n("dsc"))])
            p.op("sync", lambda e: e.nop(),
                 waits=[("ds", p.n("ds")), ("dsx", p.n("dsx")),
                        ("sx", p.n("sx")), ("dz", p.n("dz")),
                        ("sxq0", p.n("sxq0")), ("sgx0", p.n("sgx0")),
                        ("sgx1", p.n("sgx1"))])

        p.emit(nc, sems)

    return nc


def _host_constants(cfg):
    n_st = cfg["n_st"]
    lam = np.exp(-(2.0 ** (-0.5)) ** (np.arange(1, NH + 1, dtype=np.float64)))
    i_idx = np.arange(128, dtype=np.float64)
    dmaskT = np.zeros((128, NH * 128))
    decayq = np.zeros((128, NH * 128))
    dkx = np.zeros((128, NH * 4))
    dk1 = np.zeros((128, NH))
    for h in range(NH):
        diff = i_idx[None, :] - i_idx[:, None]
        with np.errstate(over="ignore", under="ignore"):
            m = np.where(diff >= 0, lam[h] ** diff, 0.0)
            dmaskT[:, h * 128:(h + 1) * 128] = m
            decayq[:, h * 128:(h + 1) * 128] = (lam[h] ** (i_idx + 1.0))[None, :]
            dk1[:, h] = lam[h] ** (127.0 - i_idx)
            for cc in range(4):
                dkx[:, h * 4 + cc] = lam[h] ** (127.0 - i_idx + 128.0 * (3 - cc))
    NTC = 4 * n_st
    n_exp = cfg["n_exp"]
    nr_red = n_exp - 2
    consts = dict(
        dmaskT=dmaskT.astype(np.float32),
        decayq=decayq.astype(np.float32),
        dkx=dkx.astype(np.float32),
        dk1=dk1.astype(np.float32),
        ident_bf=np.eye(128).astype(ml_dtypes.bfloat16),
        triu=np.triu(np.ones((128, 128)), k=1).astype(ml_dtypes.bfloat16),
        eCrow=np.tile((np.arange(NR) * CAP).astype(np.float32), (128, 1)),
        trashb=np.tile((nr_red * CAP + np.arange(NR)).astype(np.float32),
                       (128, 1)),
        tokid=(np.arange(NTC)[None, :] * 128 +
               np.arange(128)[:, None]).astype(np.float32),
    )
    sinw = []
    span = 512 * n_st
    for c in range(NCORES):
        w = np.zeros((128, NCORES * NH))
        r0 = c % 4
        for g in range(NCORES):
            if g // 4 != c // 4:
                continue
            rgl = g % 4
            if rgl < r0:
                for h in range(NH):
                    with np.errstate(over="ignore", under="ignore"):
                        w[:, g * NH + h] = lam[h] ** (span * (r0 - 1 - rgl))
        sinw.append(w.astype(np.float32))
    return consts, sinw


def make_in_maps(inputs, cfg):
    n_st = cfg["n_st"]
    n_exp = cfg["n_exp"]
    T = 512 * n_st
    x = np.asarray(inputs["L0_attn_x"], dtype=np.float32).reshape(-1, D)
    wqkvT = np.ascontiguousarray(
        np.asarray(inputs["L0_attn_w_qkv"], dtype=np.float32).T)
    woutT = np.ascontiguousarray(
        np.asarray(inputs["L0_attn_w_out"], dtype=np.float32).T)
    b_qkv = np.asarray(inputs["L0_attn_b_qkv"], dtype=np.float32)
    b_out = np.asarray(inputs["L0_attn_b_out"], dtype=np.float32)
    sw1 = np.asarray(inputs["L0_ffn_sw1"], dtype=np.float32)
    sb1 = np.asarray(inputs["L0_ffn_sb1"], dtype=np.float32)
    sw2 = np.asarray(inputs["L0_ffn_sw2"], dtype=np.float32)
    sb2 = np.asarray(inputs["L0_ffn_sb2"], dtype=np.float32)
    rw1 = np.asarray(inputs["L0_ffn_rw1"], dtype=np.float32)
    rb1 = np.asarray(inputs["L0_ffn_rb1"], dtype=np.float32)
    rw2 = np.asarray(inputs["L0_ffn_rw2"], dtype=np.float32)
    rb2 = np.asarray(inputs["L0_ffn_rb2"], dtype=np.float32)
    router_w = np.asarray(inputs["L0_ffn_router_w"], dtype=np.float32)
    router_b = np.asarray(inputs["L0_ffn_router_b"], dtype=np.float32)

    nr_red = n_exp - 2
    w1_all = np.ascontiguousarray(np.concatenate([rw1[:nr_red], sw1], axis=0))
    w2_all = np.ascontiguousarray(np.concatenate([rw2[:nr_red], sw2], axis=0))
    b1_all = np.concatenate([rb1[:nr_red], sb1], axis=0)
    b1cols = np.ascontiguousarray(b1_all.reshape(n_exp * NHC, 128).T)
    b2stack = np.concatenate([rb2[:nr_red], sb2], axis=0).astype(
        ml_dtypes.bfloat16)

    consts, sinw = _host_constants(cfg)
    shared = dict(
        wqkvT=wqkvT, woutT=woutT, w1_all=w1_all, w2_all=w2_all,
        routerw=router_w,
        bq_row=b_qkv[None, :].astype(ml_dtypes.bfloat16),
        bqcol=np.ascontiguousarray(b_qkv[0:D].reshape(NKC, 128).T),
        bout_row=b_out[None, :].astype(ml_dtypes.bfloat16),
        routerb_row=router_b[None, :].astype(ml_dtypes.bfloat16),
        b2g=b2stack, b1cols=b1cols, **consts)
    in_maps = []
    for c in range(NCORES):
        m = dict(shared)
        m["x"] = np.ascontiguousarray(x[c * T:(c + 1) * T])
        m["sinw"] = sinw[c]
        in_maps.append(m)
    return in_maps


def kernel(**inputs):
    cfg = default_cfg()
    in_maps = make_in_maps(inputs, cfg)
    nc = build_nc(cfg)
    res = run_bass_kernel_spmd(nc, in_maps, core_ids=list(range(NCORES)))
    out = np.concatenate([np.asarray(r["out"]) for r in res.results], axis=0)
    return out.reshape(2, 8192, D).astype(np.float32)


# version 07
# speedup vs baseline: 1.1540x; 1.0220x over previous
import math
import numpy as np
import ml_dtypes
from contextlib import ExitStack

import concourse.bass as bass
import concourse.mybir as mybir
from concourse.bass_utils import run_bass_kernel_spmd

F32 = mybir.dt.float32
BF16 = mybir.dt.bfloat16
I32 = mybir.dt.int32
AF = mybir.ActivationFunctionType
ALU = mybir.AluOpType
AX = mybir.AxisListType

D = 2048
NH = 16
HID = 1408
NHC = 11
NKC = 16
NE = 34
NR = 32
TOPK = 6
EPS = 1e-5
CAP = 512
ROWW = 2176
NCORES = 8


def default_cfg():
    return dict(n_st=4, n_exp=NE)


class Prog:
    ENGINES = ("sync", "tensor", "scalar", "vector", "gpsimd")

    def __init__(self):
        self.ops = []
        self.cnt = {}

    def n(self, sem):
        return self.cnt.get(sem, 0)

    CHAIN = {"scalar": "ev", "vector": "vv"}

    def op(self, eng, fn=None, waits=(), incs=(), chain=True):
        assert eng in self.ENGINES
        incs = list(incs)
        waits = list(waits)
        csem = self.CHAIN.get(eng)
        if fn is not None and csem is not None and chain:
            waits.append((csem, self.n(csem)))
            if not any(s == csem for s, _ in incs):
                incs.append((csem, 1))
        ws = {}
        for s, v in waits:
            if v > 0:
                assert v <= self.n(s), f"wait {s}>={v} but only {self.n(s)}"
                ws[s] = max(ws.get(s, 0), v)
        if incs:
            assert fn is not None, "incs must ride a real instruction"
        assert len(incs) <= 2
        self.ops.append((eng, fn, sorted(ws.items()), incs))
        for s, k in incs:
            self.cnt[s] = self.n(s) + k

    def barrier(self):
        snap = [(s, v) for s, v in self.cnt.items() if v > 0]
        for eng in self.ENGINES:
            self.op(eng, None, waits=snap)

    def emit(self, nc, sems):
        ops = self.ops

        def mk(eng_name):
            def inner(eng):
                for (e, fn, waits, incs) in ops:
                    if e != eng_name:
                        continue
                    for s, v in waits:
                        eng.wait_ge(sems[s], v)
                    inst = fn(eng) if fn is not None else None
                    for s, k in incs:
                        inst.then_inc(sems[s], k)
            return inner

        with nc.Block() as block:
            block.sync(mk("sync"))
            block.tensor(mk("tensor"))
            block.scalar(mk("scalar"))
            block.vector(mk("vector"))
            block.gpsimd(mk("gpsimd"))


class Slots:

    def __init__(self, n):
        self.free = {i: [] for i in range(n)}

    def wait(self, i):
        return list(self.free[i])

    def set(self, i, *conds):
        self.free[i] = [c for c in conds if c[1] > 0]

    def add(self, i, cond):
        if cond[1] > 0:
            self.free[i].append(cond)


def build_nc(cfg):
    n_st = cfg["n_st"]
    n_exp = cfg["n_exp"]
    nr_red = n_exp - 2
    T = 512 * n_st
    NTC = T // 128

    lam = np.exp(-(2.0 ** (-0.5)) ** (np.arange(1, NH + 1, dtype=np.float64)))

    nc = bass.Bass("TRN2", target_bir_lowering=False)

    def din(name, shape, dt=F32):
        return nc.dram_tensor(name, shape, dt, kind="ExternalInput")

    x_ext = din("x", [T, D])
    wqkvT = din("wqkvT", [D, 3 * D])
    woutT = din("woutT", [D, D])
    w1_all = din("w1_all", [n_exp, D, HID])
    w2_all = din("w2_all", [n_exp, HID, D])
    routerw = din("routerw", [D, NR])
    bq_row = din("bq_row", [1, 3 * D], BF16)
    bqcol_in = din("bqcol", [128, NKC])
    bout_row = din("bout_row", [1, D], BF16)
    routerb_row = din("routerb_row", [1, NR], BF16)
    b2g_in = din("b2g", [n_exp, D], BF16)
    b1cols_in = din("b1cols", [128, n_exp * NHC])
    ident_bf_in = din("ident_bf", [128, 128], BF16)
    dmaskT_in = din("dmaskT", [128, NH * 128])
    decayq_in = din("decayq", [128, NH * 128])
    dkx_in = din("dkx", [128, NH * 4])
    dk1_in = din("dk1", [128, NH])
    sinw_in = din("sinw", [128, NCORES * NH])
    triu_in = din("triu", [128, 128], BF16)
    eCrow_in = din("eCrow", [128, NR])
    trashb_in = din("trashb", [128, NR])
    tokid_in = din("tokid", [128, NTC])

    out_ext = nc.dram_tensor("out", [T, D], F32, kind="ExternalOutput")

    qsc = nc.dram_tensor("qsc", [D, T], BF16)
    ksc = nc.dram_tensor("ksc", [T, D], BF16)
    vsc = nc.dram_tensor("vsc", [T, D], BF16)
    xhsc = nc.dram_tensor("xhsc", [D, T], BF16)
    xg_rows = nr_red * CAP + NTC * NR + 128
    xg_dram = nc.dram_tensor("xg_dram", [xg_rows, ROWW], BF16)
    cc_in = nc.dram_tensor("cc_in", [128, NH * 128], F32)
    from concourse.replica_groups import (
        maybe_share_collective_output_space, ReplicaGroups)
    rg = [list(range(NCORES))]
    aspace = maybe_share_collective_output_space("AllGather", ReplicaGroups(rg))
    cc_out = nc.dram_tensor("cc_out", [NCORES * 128, NH * 128], F32,
                            addr_space=aspace)

    p = Prog()
    es = ExitStack()
    with es:
        def sb(name, shape, dt=F32):
            return es.enter_context(nc.sbuf_tensor(name, shape, dt))

        psf = [es.enter_context(nc.psum_tensor(f"psf{i}", [128, 512], F32))
               for i in range(6)]
        psb = [es.enter_context(nc.psum_tensor(f"psb{i}", [128, 1024], BF16))
               for i in range(2)]
        sem_names = ["dc", "dg", "sx", "sq", "sk", "sv", "sm", "sg2",
                     "sxq0", "sxq1", "dwa", "dwb", "dw1", "dw2",
                     "ds", "dsx", "da", "cc", "pe", "ev", "vv", "gp",
                     "dz", "dsc", "sgx0", "sgx1", "sg2b"]
        sems = {s: es.enter_context(nc.semaphore(s)) for s in sem_names}

        ident_bf = sb("ident_bf_s", [128, 128], BF16)
        ones_bf = sb("ones_bf", [1, 128], BF16)
        rbr = sb("rbr", [1, NR], BF16)
        rw_bf = sb("rw_bf", [128, NKC * NR], BF16)
        gates_fm = sb("gates_fm", [n_exp, T], BF16)
        triu_sb = sb("triu_sb", [128, 128], BF16)
        eCrow = sb("eCrow_s", [128, NR])
        trashb = sb("trashb_s", [128, NR])
        tokid = sb("tokid_s", [128, NTC])
        onesc_bf = sb("onesc_bf", [128, 1], BF16)
        zr = sb("zr", [128, ROWW], BF16)
        offs_full = sb("offs_full", [128, NR])
        gtmp = sb("gtmp", [128, NR])
        ghif = sb("ghif", [128, NR])
        mk_f = sb("mk_f", [128, NR])
        mk_b = sb("mk_b", [128, NR], BF16)
        ov_t = sb("ov_t", [128, NR])
        vd_t = sb("vd_t", [128, NR])
        pe_t = sb("pe_t", [128, NR])
        tr_t = sb("tr_t", [128, NR])
        of_t = sb("of_t", [128, NR])
        cntb = sb("cntb", [1, NR], BF16)
        tidf = sb("tidf", [128, 1])
        stat = sb("stat", [128, 64])
        e_buf = sb("e_buf", [128, NR])
        m8 = sb("m8", [128, 8])
        erem = sb("erem", [128, NR])
        gacc = sb("gacc", [128, NR])
        gbf_tm = sb("gbf_tm", [128, NR], BF16)
        sq_dump = sb("sq_dump", [128, D], BF16)

        fbank = Slots(6)
        bbank = Slots(2)
        tr_par = [0]

        def dma_sync(dst, src, waits=(), sem="dc"):
            w = list(waits) + ([(sem, p.n(sem))] if sem == "dc" else [])
            p.op("sync", lambda e, d=dst, s=src: e.dma_start(out=d, in_=s),
                 waits=w, incs=[(sem, 16)])
            return p.n(sem)

        def dma_gp(dst, src, waits=(), sem="dg"):
            w = list(waits) + ([(sem, p.n(sem))] if sem == "dg" else [])
            p.op("gpsimd", lambda e, d=dst, s=src: e.dma_start(out=d, in_=s),
                 waits=w, incs=[(sem, 16)])
            return p.n(sem)

        dma_sync(eCrow[:, :], eCrow_in[:, :])
        dma_sync(trashb[:, :], trashb_in[:, :])
        dma_sync(tokid[:, :], tokid_in[:, :])
        dma_sync(triu_sb[:, :], triu_in[:, :])
        dma_gp(ident_bf[:, :], ident_bf_in[:, :])
        dma_gp(rbr[:, :], routerb_row[:, :])
        dma_gp(rw_bf[:, :].rearrange("p (kc e) -> p kc e", kc=NKC),
               routerw.rearrange("(kc p) e -> p kc e", p=128))
        p.op("gpsimd", lambda e: e.memset(ones_bf[:, :], 1.0), incs=[("gp", 1)])
        p.op("gpsimd", lambda e: e.memset(onesc_bf[:, :], 1.0), incs=[("gp", 1)])
        p.op("gpsimd", lambda e: e.memset(zr[:, :], 0.0), incs=[("gp", 1)])
        zr_init = p.n("gp")
        p.op("vector", lambda e: e.memset(offs_full[:, :], 0.0), incs=[("vv", 1)])
        for r in range(nr_red * CAP // 128):
            p.op("sync", lambda e, rr=r: e.dma_start(
                out=xg_dram[rr * 128:(rr + 1) * 128, :], in_=zr[:, :]),
                 waits=[("gp", zr_init)] if r == 0 else [], incs=[("dz", 16)])
        dz_all = p.n("dz")
        p.op("vector", lambda e: e.memset(stat[:, 51:52], EPS), incs=[("vv", 1)])
        eps_init = p.n("vv")
        setup_dx = p.n("dc")
        dc_all = [setup_dx]
        setup_dw = p.n("dg")
        setup_gp = p.n("gp")
        dg_all = [setup_dw]

        ev_par = [0]

        def transpose_128(src_ap, dst_ap, extra_waits=()):
            b = tr_par[0]
            tr_par[0] ^= 1
            p.op("tensor",
                 lambda e, s=src_ap, bb=b: e.transpose(
                     psb[bb][:, 0:128], s, ident_bf[:, :]),
                 waits=bbank.wait(b) + list(extra_waits) +
                 [("dg", dg_all[0])], incs=[("pe", 1)])
            mm = p.n("pe")
            ev_par[0] ^= 1
            if ev_par[0]:
                p.op("scalar",
                     lambda e, d=dst_ap, bb=b: e.copy(d, psb[bb][:, 0:128]),
                     waits=[("pe", mm)], incs=[("ev", 1)])
                bbank.set(b, ("ev", p.n("ev")))
            else:
                p.op("vector",
                     lambda e, d=dst_ap, bb=b: e.tensor_copy(
                         d, psb[bb][:, 0:128]),
                     waits=[("pe", mm)], incs=[("vv", 1)])
                bbank.set(b, ("vv", p.n("vv")))

        xhat_tm_ref = [None]

        def ln_tm(src_tile_fn, n_tc, dst_fn, first_waits, ts_dst_fn=None,
                  ts_waits_fn=None):
            xhat_tm = xhat_tm_ref[0]
            c_sum = stat[:, 0:0 + n_tc]
            c_sq = stat[:, 8:8 + n_tc]
            c_m = stat[:, 16:16 + n_tc]
            c_rstd = stat[:, 24:24 + n_tc]
            c_nmr = stat[:, 32:32 + n_tc]
            c_t0 = stat[:, 40:40 + n_tc]
            for tc in range(n_tc):
                src = src_tile_fn(tc)
                w = (list(first_waits) + [("vv", p.n("vv"))]) if tc == 0 else []
                p.op("scalar",
                     lambda e, s=src, c=c_sq[:, tc:tc + 1]: e.activation(
                         sq_dump[:, :], s, AF.Square, accum_out=c),
                     waits=w + [("ev", p.n("ev"))], incs=[("ev", 1)])
                p.op("scalar",
                     lambda e, s=src, c=c_sum[:, tc:tc + 1]: e.activation(
                         sq_dump[:, :], s, AF.Copy, accum_out=c),
                     waits=[("ev", p.n("ev"))], incs=[("ev", 1)])
            act_done = p.n("ev")
            p.op("vector", lambda e: e.tensor_scalar_mul(c_m, c_sum, 1.0 / D),
                 waits=[("ev", act_done)])
            p.op("vector", lambda e: e.tensor_scalar_mul(c_t0, c_sq, 1.0 / D))
            p.op("vector", lambda e: e.tensor_tensor(c_rstd, c_m, c_m, ALU.mult))
            p.op("vector",
                 lambda e: e.tensor_tensor(c_t0, c_t0, c_rstd, ALU.subtract),
                 incs=[("vv", 1)])
            var_done = p.n("vv")
            p.op("scalar", lambda e: e.activation(c_t0, c_t0, AF.Sqrt,
                                                  bias=stat[:, 51:52]),
                 waits=[("vv", var_done)], incs=[("ev", 1)])
            sd_done = p.n("ev")
            p.op("vector", lambda e: e.reciprocal(c_rstd, c_t0),
                 waits=[("ev", sd_done)])
            p.op("vector", lambda e: e.scalar_tensor_tensor(
                c_nmr, c_m, -1.0, c_rstd, ALU.mult, ALU.mult),
                 incs=[("vv", 1)])
            stats_done = p.n("vv")
            for tc in range(n_tc):
                src = src_tile_fn(tc)
                tdst = ts_dst_fn(tc) if ts_dst_fn else xhat_tm[:, :]
                tw = (ts_waits_fn(tc) if ts_waits_fn else [])
                p.op("vector",
                     lambda e, s=src, r=c_rstd[:, tc:tc + 1],
                     nm=c_nmr[:, tc:tc + 1], xt=tdst:
                     e.tensor_scalar(xt, s, r, nm, ALU.mult, ALU.add),
                     waits=[("vv", stats_done), ("pe", p.n("pe"))] + tw,
                     incs=[("vv", 1)])
                xh = p.n("vv")
                for kc in range(NKC):
                    transpose_128(tdst[:, kc * 128:(kc + 1) * 128],
                                  dst_fn(tc, kc), [("vv", xh)])

        with ExitStack() as asco:
            def asb(name, shape, dt=F32):
                return asco.enter_context(nc.sbuf_tensor(name, shape, dt))

            sinw = asb("sinw_s", [128, NCORES * NH])
            S_delta = asb("S_delta", [128, NH * 128])
            S_state = S_delta
            x_buf = [asb("x_buf0", [128, 4 * D])]
            x_buf = [x_buf[0], x_buf[0]]
            fm_buf = asb("fm_buf", [128, NKC * 512], BF16)
            k_tm = asb("k_tm", [128, 4 * D], BF16)
            v_tm = asb("v_tm", [128, 4 * D], BF16)
            wsl = [asb(f"wsl{i}", [128, NKC * 512], BF16) for i in range(2)]
            gat_buf = None
            rt = [asb(f"rt{i}", [128, ROWW], BF16) for i in range(4)]
            oint = [asb(f"oint{i}", [128, NR], I32) for i in range(4)]
            for _i in range(4):
                p.op("vector", lambda e, i=_i: e.memset(
                    rt[i][:, D + 2 * NR + 2:ROWW], 0.0), incs=[("vv", 1)])


            consts_dx = dma_sync(sinw[:, :], sinw_in[:, :])
            dc_all[0] = p.n("dc")
            p.op("vector", lambda e: e.memset(S_delta[:, :], 0.0),
                 incs=[("vv", 1)])
            sdelta_init = p.n("vv")

            xload = {}
            xslot = Slots(1)

            def load_x(st):
                w = xslot.wait(0)
                xload[st] = dma_sync(
                    x_buf[st % 2][:, :].rearrange("p (tc d) -> p tc d", tc=4),
                    x_ext[st * 512:(st + 1) * 512, :].rearrange(
                        "(tc p) d -> p tc d", p=128), waits=w, sem="sx")

            load_x(0)

            wslot = Slots(2)
            kd_slot = Slots(2)
            rt_free = Slots(4)
            oint_free = Slots(4)

            ph1_scope = ExitStack()
            bqr = ph1_scope.enter_context(
                nc.sbuf_tensor("bqr", [1, 3 * D], BF16))
            dkx = ph1_scope.enter_context(
                nc.sbuf_tensor("dkx_s", [128, NH * 4], F32))
            kdb = ph1_scope.enter_context(
                nc.sbuf_tensor("kdb", [128, 2 * 128], BF16))
            qfm_buf = ph1_scope.enter_context(
                nc.sbuf_tensor("qfm_buf", [128, NKC * 512], BF16))
            q_tm = ph1_scope.enter_context(
                nc.sbuf_tensor("q_tm", [128, 4 * D], BF16))
            bqcol = ph1_scope.enter_context(
                nc.sbuf_tensor("bqcol_s", [128, NKC], F32))
            dma_gp(bqr[:, :], bq_row[:, :])
            setup_dw2 = p.n("dg")
            dg_all[0] = p.n("dg")
            dma_sync(dkx[:, :], dkx_in[:, :])
            dma_sync(bqcol[:, :], bqcol_in[:, :])
            dc_all[0] = p.n("dc")
            for st in range(n_st):
                xb = x_buf[st % 2]
                scst_prev = p.n("ds")
                ln_tm(lambda tc, b=xb: b[:, tc * D:(tc + 1) * D], 4,
                      lambda tc, kc: fm_buf[:, kc * 512 + tc * 128:
                                            kc * 512 + (tc + 1) * 128],
                      [("sx", xload[st])],
                      ts_dst_fn=lambda tc: rt[tc][:, 0:D],
                      ts_waits_fn=lambda tc: rt_free.wait(tc))
                for _tc in range(4):
                    rt_free.set(_tc, ("pe", p.n("pe")))
                xhat_done = p.n("ev")
                xhat_vv = p.n("vv")
                xslot.set(0, ("ev", xhat_done))

                first_evac = True
                for pi, (dst, fofs) in enumerate(
                        ((None, 0), (k_tm, D), (v_tm, 2 * D))):
                    for fsl in range(4):
                        slot = (pi * 4 + fsl) % 2
                        wsem = "dwa" if slot == 0 else "dwb"
                        wl = dma_gp(
                            wsl[slot][:, :].rearrange(
                                "p (kc f) -> p kc f", kc=NKC),
                            wqkvT[:, fofs + fsl * 512:
                                  fofs + (fsl + 1) * 512].rearrange(
                                "(kc p) f -> p kc f", p=128),
                            waits=wslot.wait(slot), sem=wsem)
                        if pi == 0:
                            for fo_i in range(4):
                                fc = fsl * 4 + fo_i
                                bank = fo_i % 4
                                bw = fbank.wait(bank)
                                for kc in range(NKC):
                                    p.op("tensor",
                                         lambda e, b=bank, k=kc, s=slot,
                                         fi=fo_i: e.matmul(
                                             psf[b][:, :],
                                             wsl[s][:, k * 512 + fi * 128:
                                                    k * 512 + (fi + 1) * 128],
                                             fm_buf[:, k * 512:(k + 1) * 512],
                                             start=(k == 0),
                                             stop=(k == NKC - 1)),
                                         waits=([("ev", xhat_done),
                                                 ("vv", xhat_vv),
                                                 (wsem, wl)] + bw)
                                         if kc == 0 else [],
                                         incs=([("pe", 1)]
                                               if kc == NKC - 1 else []))
                                grp = p.n("pe")
                                ev_w = [("pe", grp), ("dc", dc_all[0])]
                                if first_evac and st > 0:
                                    ev_w.append(("ds", scst_prev))
                                first_evac = False
                                p.op("scalar",
                                     lambda e, b=bank, c=fc: e.activation(
                                         qfm_buf[:, c * 512:(c + 1) * 512],
                                         psf[b][:, :], AF.Identity,
                                         bias=bqcol[:, c:c + 1]),
                                     waits=ev_w, incs=[("ev", 1)])
                                fbank.set(bank, ("ev", p.n("ev")))
                            wslot.set(slot, ("pe", p.n("pe")))
                            continue
                        for tc in range(4):
                            bank = tc % 4
                            bw = fbank.wait(bank)
                            for kc in range(NKC):
                                p.op("tensor",
                                     lambda e, b=bank, k=kc, t=tc, s=slot:
                                     e.matmul(
                                         psf[b][:, :],
                                         fm_buf[:, k * 512 + t * 128:
                                                k * 512 + (t + 1) * 128],
                                         wsl[s][:, k * 512:(k + 1) * 512],
                                         start=(k == 0), stop=False),
                                     waits=([("ev", xhat_done),
                                             ("vv", xhat_vv),
                                             (wsem, wl)] + bw)
                                     if kc == 0 else [])
                            p.op("tensor",
                                 lambda e, b=bank, fo=fofs, f=fsl: e.matmul(
                                     psf[b][:, :], ones_bf[0:1, 0:128],
                                     bqr[0:1, fo + f * 512:fo + (f + 1) * 512],
                                     start=False, stop=True),
                                 waits=[("dg", dg_all[0]), ("gp", setup_gp)],
                                 incs=[("pe", 1)])
                            grp = p.n("pe")
                            ev_w = [("pe", grp)]
                            if first_evac and st > 0:
                                ev_w.append(("ds", scst_prev))
                            first_evac = False
                            p.op("scalar",
                                 lambda e, d=dst, b=bank, t=tc, f=fsl: e.copy(
                                     d[:, t * D + f * 512:t * D + (f + 1) * 512],
                                     psf[b][:, :]),
                                 waits=ev_w, incs=[("ev", 1)])
                            fbank.set(bank, ("ev", p.n("ev")))
                        wslot.set(slot, ("pe", p.n("pe")))
                qkv_ev = p.n("ev")

                dma_sync(
                    qsc[:, st * 512:(st + 1) * 512].rearrange(
                        "(hc p) t -> p hc t", p=128),
                    qfm_buf[:, :].rearrange("p (hc t) -> p hc t", hc=NKC),
                    waits=[("ev", qkv_ev)], sem="ds")
                for (src, dstt) in ((k_tm, ksc), (v_tm, vsc)):
                    dma_sync(
                        dstt[st * 512:(st + 1) * 512, :].rearrange(
                            "(tc p) d -> p tc d", p=128),
                        src[:, :].rearrange("p (tc d) -> p tc d", tc=4),
                        waits=[("ev", qkv_ev)], sem="ds")

                for h in range(NH):
                    bank = 4 + h % 2
                    bw = fbank.wait(bank)
                    for cc in range(4):
                        sl = (h * 4 + cc) % 2
                        p.op("scalar",
                             lambda e, hh=h, c=cc, s=sl: e.activation(
                                 kdb[:, s * 128:(s + 1) * 128],
                                 k_tm[:, c * D + hh * 128:
                                      c * D + (hh + 1) * 128],
                                 AF.Copy,
                                 scale=dkx[:, hh * 4 + c:hh * 4 + c + 1]),
                             waits=[("ev", qkv_ev), ("dc", dc_all[0])] +
                             kd_slot.wait(sl),
                             incs=[("ev", 1)])
                        kd_ok = p.n("ev")
                        p.op("tensor",
                             lambda e, hh=h, c=cc, b=bank, s=sl: e.matmul(
                                 psf[b][:, 0:128],
                                 kdb[:, s * 128:(s + 1) * 128],
                                 v_tm[:, c * D + hh * 128:
                                      c * D + (hh + 1) * 128],
                                 start=(c == 0), stop=(c == 3)),
                             waits=[("ev", kd_ok)] + (bw if cc == 0 else []),
                             incs=[("pe", 1)])
                        kd_slot.set(sl, ("pe", p.n("pe")))
                    grp = p.n("pe")
                    l512 = float(np.float32(lam[h] ** 512))
                    p.op("vector",
                         lambda e, hh=h, b=bank, l=l512: e.scalar_tensor_tensor(
                             S_delta[:, hh * 128:(hh + 1) * 128],
                             S_delta[:, hh * 128:(hh + 1) * 128], l,
                             psf[b][:, 0:128], ALU.mult, ALU.add),
                         waits=[("pe", grp), ("vv", sdelta_init)],
                         incs=[("vv", 1)])
                    fbank.set(bank, ("vv", p.n("vv")))

                if st + 1 < n_st:
                    load_x(st + 1)

            ph1_vv = p.n("vv")
            dma_sync(cc_in[:, :], S_delta[:, :], waits=[("vv", ph1_vv)],
                     sem="ds")
            sdel_st = p.n("ds")
            p.op("gpsimd", lambda e: e.collective_compute(
                "AllGather", ALU.bypass, replica_groups=rg,
                ins=[cc_in.ap().opt()], outs=[cc_out.ap().opt()]),
                 waits=[("ds", sdel_st)], incs=[("cc", 1)])
            p.op("vector", lambda e: e.memset(S_state[:, :], 0.0),
                 waits=[("ds", sdel_st)], incs=[("vv", 1)])
            gslot = Slots(2)
            gb0 = x_buf[0][:, 0:NH * 128]
            gb1 = x_buf[0][:, NH * 128:2 * NH * 128]
            for r in range(NCORES):
                gb = gb0 if r % 2 == 0 else gb1
                gsm = "sg2" if r % 2 == 0 else "sg2b"
                gw = [("cc", 1)] + gslot.wait(r % 2)
                if r == 0:
                    gw += xslot.wait(0)
                gl = dma_sync(gb, cc_out[r * 128:(r + 1) * 128, :],
                              waits=gw, sem=gsm)
                for h in range(NH):
                    p.op("vector",
                         lambda e, b=gb, hh=h, rr=r:
                         e.scalar_tensor_tensor(
                             S_state[:, hh * 128:(hh + 1) * 128],
                             b[:, hh * 128:(hh + 1) * 128],
                             sinw[:, rr * NH + hh:rr * NH + hh + 1],
                             S_state[:, hh * 128:(hh + 1) * 128],
                             ALU.mult, ALU.add),
                         waits=[(gsm, gl)] if h == 0 else [],
                         incs=([("vv", 1)] if h == NH - 1 else []))
                gslot.set(r % 2, ("vv", p.n("vv")))
            sin_done = p.n("vv")
            xslot.set(0, ("vv", sin_done))

            ph1_scope.close()
            ph2_scope = ExitStack()

            def p2sb(name, shape, dt=BF16):
                return ph2_scope.enter_context(nc.sbuf_tensor(name, shape, dt))

            boutr = p2sb("boutr", [1, D])
            dmaskT = p2sb("dmaskT_s", [128, NH * 128], F32)
            decayq = p2sb("decayq_s", [128, NH * 128], F32)
            dk1 = p2sb("dk1_s", [128, NH], F32)
            qf_st = p2sb("qf_st", [128, NH * 512])
            kf16 = p2sb("kf16", [128, NH * 128])
            smb16 = p2sb("smb16", [128, NH * 128])
            sbf16 = p2sb("sbf16", [128, NH * 128])
            dma_gp(boutr[:, :], bout_row[:, :])
            dw2b = p.n("dg")
            dma_sync(dmaskT[:, :], dmaskT_in[:, :])
            dma_sync(decayq[:, :], decayq_in[:, :])
            ph2_dc = dma_sync(dk1[:, :], dk1_in[:, :])
            qkv_load = {}
            xres_load = {}

            qkv_free = [("pe", 0)]

            def load_qkv(st):
                w0 = [("ds", p.n("ds")), qkv_free[0]]
                vals = [dma_sync(
                    qf_st[:, :].rearrange("p (hc t) -> p hc t", hc=NH),
                    qsc[:, st * 512:(st + 1) * 512].rearrange(
                        "(hc p) t -> p hc t", p=128),
                    waits=w0, sem="sq")]
                w0 = [qkv_free[0]]
                for (srct, dst, sname) in ((ksc, k_tm, "sk"),
                                           (vsc, v_tm, "sv")):
                    vals.append(dma_sync(
                        dst[:, :].rearrange("p (tc d) -> p tc d", tc=4),
                        srct[st * 512:(st + 1) * 512, :].rearrange(
                            "(tc p) d -> p tc d", p=128),
                        waits=w0, sem=sname))
                qkv_load[st] = vals

            def load_xres(st):
                xres_load[st] = dma_sync(
                    x_buf[st % 2][:, :].rearrange("p (tc d) -> p tc d", tc=4),
                    x_ext[st * 512:(st + 1) * 512, :].rearrange(
                        "(tc p) d -> p tc d", p=128),
                    waits=xslot.wait(0), sem="sx")

            qf_free = [[]]
            kf_free = [[]]
            smb_free = [[]]
            supd_prev = [0]
            inter_prev = [0]
            xh_stored = {}
            from concourse.bass_core_rust import IndirectOffsetOnAxis
            p.op("gpsimd", lambda e: e.memset(gates_fm[:, :], 0.5),
                 incs=[("gp", 1)])
            gfm_init = p.n("gp")

            for st in range(n_st):
                if st == 0:
                    load_qkv(0)
                    load_xres(0)
                ldq, ldk, ldv = qkv_load[st]
                first_ch = True
                for cc in range(4):
                    k_ev = {}
                    k_vv = {}
                    for h in range(NH):
                        bb = tr_par[0]; tr_par[0] ^= 1
                        w0 = list(bbank.wait(bb))
                        if h == 0 and cc == 0:
                            w0 += [("sq", ldq), ("sk", ldk), ("sv", ldv)]
                        if h == 0:
                            w0 += kf_free[0]
                        if first_ch:
                            w0 += [("vv", sin_done)]
                            first_ch = False
                        p.op("tensor", lambda e, c=cc, hh=h, b=bb: e.transpose(
                            psb[b][:, 0:128],
                            k_tm[:, c * D + hh * 128:c * D + (hh + 1) * 128],
                            ident_bf[:, :]), waits=w0, incs=[("pe", 1)])
                        if h % 2 == 0:
                            p.op("scalar", lambda e, hh=h, b=bb: e.copy(
                                kf16[:, hh * 128:(hh + 1) * 128],
                                psb[b][:, 0:128]),
                                 waits=[("pe", p.n("pe"))], incs=[("ev", 1)])
                            bbank.set(bb, ("ev", p.n("ev")))
                            k_ev[h] = p.n("ev")
                            k_vv[h] = 0
                        else:
                            p.op("vector", lambda e, hh=h, b=bb: e.tensor_copy(
                                kf16[:, hh * 128:(hh + 1) * 128],
                                psb[b][:, 0:128]),
                                 waits=[("pe", p.n("pe"))], incs=[("vv", 1)])
                            bbank.set(bb, ("vv", p.n("vv")))
                            k_ev[h] = 0
                            k_vv[h] = p.n("vv")
                    mask_vv = {}
                    s_mm = {}
                    for h in range(NH):
                        sbank = h % 2
                        kw = ([("ev", k_ev[h])] if k_ev[h] else
                              [("vv", k_vv[h])])
                        if h == 0:
                            kw += [("sq", ldq)] + qf_free[0]
                        p.op("tensor", lambda e, c=cc, hh=h, b=sbank: e.matmul(
                            psf[b][:, 0:128],
                            kf16[:, hh * 128:(hh + 1) * 128],
                            qf_st[:, hh * 512 + c * 128:
                                  hh * 512 + (c + 1) * 128],
                            start=True, stop=True),
                             waits=kw + fbank.wait(sbank),
                             incs=[("pe", 1)])
                        s_mm[h] = p.n("pe")
                        mw = [("pe", s_mm[h])]
                        if h == 0:
                            mw += smb_free[0] + [("dc", ph2_dc)]
                        p.op("vector", lambda e, hh=h, b=sbank: e.tensor_tensor(
                            smb16[:, hh * 128:(hh + 1) * 128],
                            psf[b][:, 0:128],
                            dmaskT[:, hh * 128:(hh + 1) * 128], ALU.mult),
                             waits=mw, incs=[("vv", 1)])
                        mask_vv[h] = p.n("vv")
                        fbank.set(sbank, ("vv", mask_vv[h]))
                    kf_free[0] = [("pe", s_mm[NH - 1])]
                    p.op("vector", lambda e, c=cc: e.tensor_tensor(
                        qf_st[:, :].rearrange("p (h t) -> p h t", h=NH)[
                            :, :, c * 128:(c + 1) * 128],
                        qf_st[:, :].rearrange("p (h t) -> p h t", h=NH)[
                            :, :, c * 128:(c + 1) * 128],
                        decayq[:, :].rearrange("p (h t) -> p h t", h=NH),
                        ALU.mult),
                         waits=[("pe", s_mm[NH - 1]), ("dc", ph2_dc)],
                         incs=[("vv", 1)])
                    qt_ok = p.n("vv")
                    p.op("scalar", lambda e: e.copy(sbf16[:, :], S_state[:, :]),
                         waits=[("vv", supd_prev[0]), ("pe", inter_prev[0])],
                         incs=[("ev", 1)])
                    snap_ok = p.n("ev")
                    o_mm = {}
                    for h in range(NH):
                        ob = 2 + h % 2
                        p.op("tensor", lambda e, c=cc, hh=h, b=ob: e.matmul(
                            psf[b][:, 0:128],
                            v_tm[:, c * D + hh * 128:c * D + (hh + 1) * 128],
                            smb16[:, hh * 128:(hh + 1) * 128],
                            start=True, stop=False),
                             waits=[("vv", mask_vv[h])] + fbank.wait(ob))
                        p.op("tensor", lambda e, c=cc, hh=h, b=ob: e.matmul(
                            psf[b][:, 0:128],
                            sbf16[:, hh * 128:(hh + 1) * 128],
                            qf_st[:, hh * 512 + c * 128:
                                  hh * 512 + (c + 1) * 128],
                            start=False, stop=True),
                             waits=[("vv", qt_ok), ("ev", snap_ok)]
                             if h == 0 else [],
                             incs=[("pe", 1)])
                        o_mm[h] = p.n("pe")
                        evw = [("pe", o_mm[h])]
                        if cc == 0 and h == 0:
                            evw.append(("dsx", p.n("dsx")))
                        p.op("scalar", lambda e, c=cc, hh=h, b=ob: e.copy(
                            fm_buf[:, hh * 512 + c * 128:
                                   hh * 512 + (c + 1) * 128],
                            psf[b][:, 0:128]),
                             waits=evw, incs=[("ev", 1)])
                        fbank.set(ob, ("ev", p.n("ev")))
                    inter_prev[0] = o_mm[NH - 1]
                    qf_free[0] = [("pe", o_mm[NH - 1])]
                    last_ds = 0
                    for h in range(NH):
                        p.op("scalar", lambda e, c=cc, hh=h: e.activation(
                            smb16[:, hh * 128:(hh + 1) * 128],
                            k_tm[:, c * D + hh * 128:c * D + (hh + 1) * 128],
                            AF.Copy, scale=dk1[:, hh:hh + 1]),
                             waits=[("pe", o_mm[h])], incs=[("ev", 1)])
                        kd_ok = p.n("ev")
                        db = 4 + h % 2
                        p.op("tensor", lambda e, c=cc, hh=h, b=db: e.matmul(
                            psf[b][:, 0:128],
                            smb16[:, hh * 128:(hh + 1) * 128],
                            v_tm[:, c * D + hh * 128:c * D + (hh + 1) * 128],
                            start=True, stop=True),
                             waits=[("ev", kd_ok)] + fbank.wait(db),
                             incs=[("pe", 1)])
                        ds_mm = p.n("pe")
                        last_ds = ds_mm
                        l128 = float(np.float32(lam[h] ** 128))
                        p.op("vector",
                             lambda e, hh=h, b=db, l=l128:
                             e.scalar_tensor_tensor(
                                 S_state[:, hh * 128:(hh + 1) * 128],
                                 S_state[:, hh * 128:(hh + 1) * 128], l,
                                 psf[b][:, 0:128], ALU.mult, ALU.add),
                             waits=[("pe", ds_mm), ("ev", snap_ok)],
                             incs=[("vv", 1)])
                        fbank.set(db, ("vv", p.n("vv")))
                    supd_prev[0] = p.n("vv")
                    smb_free[0] = [("pe", last_ds)]
                attn_ev = p.n("ev")
                qkv_free[0] = ("pe", p.n("pe"))
                if st + 1 < n_st:
                    load_qkv(st + 1)

                for fsl in range(4):
                    slot = fsl % 2
                    wsem = "dwa" if slot == 0 else "dwb"
                    wl = dma_gp(
                        wsl[slot][:, :].rearrange("p (kc f) -> p kc f", kc=NKC),
                        woutT[:, fsl * 512:(fsl + 1) * 512].rearrange(
                            "(kc p) f -> p kc f", p=128),
                        waits=wslot.wait(slot), sem=wsem)
                    for tc in range(4):
                        bank = tc % 4
                        bw = fbank.wait(bank)
                        for dc in range(NKC):
                            p.op("tensor",
                                 lambda e, b=bank, d=dc, t=tc, s=slot:
                                 e.matmul(
                                     psf[b][:, :],
                                     fm_buf[:, d * 512 + t * 128:
                                            d * 512 + (t + 1) * 128],
                                     wsl[s][:, d * 512:(d + 1) * 512],
                                     start=(d == 0), stop=False),
                                 waits=([("ev", attn_ev), (wsem, wl)] + bw)
                                 if dc == 0 else [])
                        p.op("tensor", lambda e, b=bank, f=fsl: e.matmul(
                            psf[b][:, :], ones_bf[0:1, 0:128],
                            boutr[0:1, f * 512:(f + 1) * 512],
                            start=False, stop=True),
                             waits=[("dg", dw2b)], incs=[("pe", 1)])
                        grp = p.n("pe")
                        p.op("vector",
                             lambda e, b=bank, t=tc, f=fsl, xb=x_buf[st % 2]:
                             e.tensor_tensor(
                                 xb[:, t * D + f * 512:t * D + (f + 1) * 512],
                                 psf[b][:, :],
                                 xb[:, t * D + f * 512:t * D + (f + 1) * 512],
                                 ALU.add),
                             waits=[("pe", grp), ("sx", xres_load[st])],
                             incs=[("vv", 1)])
                        fbank.set(bank, ("vv", p.n("vv")))
                    wslot.set(slot, ("pe", p.n("pe")))
                res_vv = p.n("vv")
                dma_sync(
                    out_ext[st * 512:(st + 1) * 512, :].rearrange(
                        "(tc p) d -> p tc d", p=128),
                    x_buf[st % 2][:, :].rearrange("p (tc d) -> p tc d", tc=4),
                    waits=[("vv", res_vv)], sem="ds")
                st_store = p.n("ds")

                ln_tm(lambda tc, b=x_buf[st % 2]: b[:, tc * D:(tc + 1) * D], 4,
                      lambda tc, kc: fm_buf[:, kc * 512 + tc * 128:
                                            kc * 512 + (tc + 1) * 128],
                      [("vv", res_vv)],
                      ts_dst_fn=lambda tc: rt[tc][:, 0:D],
                      ts_waits_fn=lambda tc: rt_free.wait(tc))
                xh_done = p.n("ev")
                ln_vv = p.n("vv")
                xh_stored[st] = dma_sync(
                    xhsc[:, st * 512:(st + 1) * 512].rearrange(
                        "(kc p) t -> p kc t", p=128),
                    fm_buf[:, :].rearrange("p (kc t) -> p kc t", kc=NKC),
                    waits=[("ev", xh_done), ("vv", ln_vv),
                           ("dsx", p.n("dsx"))], sem="dsx")
                xslot.set(0, ("ds", st_store), ("vv", ln_vv),
                          ("ev", xh_done))
                if st + 1 < n_st:
                    load_xres(st + 1)

                for tc in range(4):
                    tg = st * 4 + tc
                    b_r = tc % 2
                    bw = fbank.wait(b_r)
                    for kc in range(NKC):
                        p.op("tensor",
                             lambda e, b=b_r, k=kc, t=tc: e.matmul(
                                 psf[b][:, 0:NR],
                                 fm_buf[:, k * 512 + t * 128:
                                        k * 512 + (t + 1) * 128],
                                 rw_bf[:, k * NR:(k + 1) * NR],
                                 start=(k == 0), stop=False),
                             waits=([("ev", xh_done), ("vv", ln_vv),
                                     ("dg", dg_all[0])] + bw)
                             if kc == 0 else [])
                    p.op("tensor", lambda e, b=b_r: e.matmul(
                        psf[b][:, 0:NR], ones_bf[0:1, 0:128], rbr[0:1, :],
                        start=False, stop=True), incs=[("pe", 1)])
                    rt_mm = p.n("pe")
                    p.op("vector", lambda e, b=b_r: e.reduce_max(
                        stat[:, 48:49], psf[b][:, 0:NR], axis=AX.X,
                        negate=True),
                         waits=[("pe", rt_mm)], incs=[("vv", 1)])
                    nm_ok = p.n("vv")
                    p.op("scalar", lambda e, b=b_r: e.activation(
                        e_buf[:, :], psf[b][:, 0:NR], AF.Exp,
                        bias=stat[:, 48:49]),
                         waits=[("vv", nm_ok)], incs=[("ev", 1)])
                    fbank.set(b_r, ("ev", p.n("ev")))
                    exp_ok = p.n("ev")
                    p.op("vector", lambda e: e.max(out=m8[:, :],
                                                   in_=e_buf[:, :]),
                         waits=[("ev", exp_ok)], incs=[("vv", 1)])
                    p.op("vector", lambda e: e.memset(m8[:, TOPK:8], 0.0),
                         incs=[("vv", 1)])
                    p.op("vector", lambda e: e.match_replace(
                        out=erem[:, :], in_to_replace=m8[:, :],
                        in_values=e_buf[:, :], imm_value=0.0),
                         incs=[("vv", 1)])
                    p.op("vector", lambda e: e.tensor_tensor(
                        gacc[:, :], e_buf[:, :], erem[:, :], ALU.subtract),
                         incs=[("vv", 1)])
                    p.op("vector", lambda e: e.reduce_sum(
                        stat[:, 49:50], gacc[:, :], axis=AX.X),
                         incs=[("vv", 1)])
                    p.op("vector", lambda e: e.reciprocal(
                        stat[:, 50:51], stat[:, 49:50]),
                         incs=[("vv", 1)])
                    p.op("vector", lambda e: e.tensor_scalar_mul(
                        gtmp[:, :], gacc[:, :], stat[:, 50:51]),
                         incs=[("vv", 1)])
                    p.op("vector", lambda e: e.tensor_copy(
                        gbf_tm[:, 0:NR], gtmp[:, :]), incs=[("vv", 1)])
                    g_ok = p.n("vv")
                    b = tr_par[0]; tr_par[0] ^= 1
                    p.op("tensor", lambda e, bb=b: e.transpose(
                        psb[bb][0:NR, 0:128], gbf_tm[:, :], ident_bf[:, :]),
                         waits=[("vv", g_ok)] + bbank.wait(b),
                         incs=[("pe", 1)])
                    p.op("scalar", lambda e, bb=b, t=tg: e.copy(
                        gates_fm[0:nr_red, t * 128:(t + 1) * 128],
                        psb[bb][0:nr_red, 0:128]),
                         waits=[("pe", p.n("pe")), ("gp", gfm_init)],
                         incs=[("ev", 1)])
                    bbank.set(b, ("ev", p.n("ev")))
                    p.op("scalar", lambda e, t=tc: e.copy(
                        rt[t][:, D:D + NR], gtmp[:, :]),
                         waits=[("vv", g_ok)], incs=[("ev", 1)])
                    p.op("vector", lambda e, t=tc: e.tensor_copy(
                        ghif[:, :], rt[t][:, D:D + NR]),
                         waits=[("ev", p.n("ev"))], incs=[("vv", 1)])
                    p.op("vector", lambda e, t=tc: e.tensor_tensor(
                        rt[t][:, D + NR:D + 2 * NR], gtmp[:, :], ghif[:, :],
                        ALU.subtract), incs=[("vv", 1)])
                    p.op("scalar", lambda e, t=tc, g=tg: e.copy(
                        rt[t][:, D + 2 * NR:D + 2 * NR + 1],
                        tokid[:, g:g + 1]),
                         waits=[("dc", dc_all[0])], incs=[("ev", 1)])
                    p.op("vector", lambda e, t=tc: e.tensor_copy(
                        tidf[:, :], rt[t][:, D + 2 * NR:D + 2 * NR + 1]),
                         waits=[("ev", p.n("ev"))], incs=[("vv", 1)])
                    p.op("vector", lambda e, t=tc, g=tg: e.tensor_tensor(
                        rt[t][:, D + 2 * NR + 1:D + 2 * NR + 2],
                        tokid[:, g:g + 1], tidf[:, :], ALU.subtract),
                         incs=[("vv", 1)])
                    row_ev = p.n("ev")
                    p.op("vector", lambda e: e.tensor_single_scalar(
                        mk_f[:, :], gacc[:, :], 0.0, ALU.is_gt),
                         incs=[("vv", 1)])
                    p.op("vector", lambda e: e.tensor_copy(
                        mk_b[:, :], mk_f[:, :]), incs=[("vv", 1)])
                    mk_ok = p.n("vv")
                    b_p = 2 + tc % 2
                    bwp = fbank.wait(b_p)
                    p.op("tensor", lambda e, b=b_p: e.matmul(
                        psf[b][:, 0:NR], triu_sb[:, :], mk_b[:, :],
                        start=True, stop=True),
                         waits=[("vv", mk_ok), ("dc", dc_all[0])] + bwp,
                         incs=[("pe", 1)])
                    pcum_mm = p.n("pe")
                    b_c = 4 + tc % 2
                    bwc = fbank.wait(b_c)
                    p.op("tensor", lambda e, b=b_c: e.matmul(
                        psf[b][0:1, 0:NR], onesc_bf[:, :], mk_b[:, :],
                        start=True, stop=True),
                         waits=[("vv", mk_ok), ("gp", setup_gp)] + bwc,
                         incs=[("pe", 1)])
                    cnt_mm = p.n("pe")
                    p.op("vector", lambda e, b=b_p: e.tensor_tensor(
                        pe_t[:, :], psf[b][:, 0:NR], offs_full[:, :],
                        ALU.add),
                         waits=[("pe", pcum_mm)], incs=[("vv", 1)])
                    fbank.set(b_p, ("vv", p.n("vv")))
                    p.op("vector", lambda e: e.tensor_single_scalar(
                        ov_t[:, :], pe_t[:, :], CAP - 0.5, ALU.is_ge),
                         incs=[("vv", 1)])
                    p.op("vector", lambda e: e.tensor_scalar(
                        ov_t[:, :], ov_t[:, :], -1.0, 1.0, ALU.mult, ALU.add),
                         incs=[("vv", 1)])
                    p.op("vector", lambda e: e.tensor_tensor(
                        vd_t[:, :], mk_f[:, :], ov_t[:, :], ALU.mult),
                         incs=[("vv", 1)])
                    p.op("vector", lambda e: e.tensor_tensor(
                        pe_t[:, :], pe_t[:, :], eCrow[:, :], ALU.add),
                         waits=[("dc", dc_all[0])], incs=[("vv", 1)])
                    p.op("vector", lambda e, g=tg: e.tensor_single_scalar(
                        tr_t[:, :], trashb[:, :], float(g * NR),
                        ALU.add), incs=[("vv", 1)])
                    p.op("vector", lambda e: e.tensor_tensor(
                        pe_t[:, :], vd_t[:, :], pe_t[:, :], ALU.mult),
                         incs=[("vv", 1)])
                    p.op("vector", lambda e: e.tensor_scalar(
                        vd_t[:, :], vd_t[:, :], -1.0, 1.0, ALU.mult, ALU.add),
                         incs=[("vv", 1)])
                    p.op("vector", lambda e: e.tensor_tensor(
                        tr_t[:, :], vd_t[:, :], tr_t[:, :], ALU.mult),
                         incs=[("vv", 1)])
                    p.op("vector", lambda e: e.tensor_tensor(
                        of_t[:, :], pe_t[:, :], tr_t[:, :], ALU.add),
                         incs=[("vv", 1)])
                    p.op("vector", lambda e, t=tc: e.tensor_copy(
                        oint[t][:, :], of_t[:, :]),
                         waits=oint_free.wait(tc), incs=[("vv", 1)])
                    oint_ok = p.n("vv")
                    p.op("scalar", lambda e, b=b_c: e.copy(
                        cntb[0:1, :], psf[b][0:1, 0:NR]),
                         waits=[("pe", cnt_mm)], incs=[("ev", 1)])
                    cnt_ev = p.n("ev")
                    fbank.set(b_c, ("ev", cnt_ev))
                    b_c2 = 4 + (tc + 1) % 2
                    bwc2 = fbank.wait(b_c2)
                    p.op("tensor", lambda e, b=b_c2: e.matmul(
                        psf[b][:, 0:NR], ones_bf[0:1, 0:128], cntb[0:1, :],
                        start=True, stop=True),
                         waits=[("ev", cnt_ev)] + bwc2, incs=[("pe", 1)])
                    cb_mm = p.n("pe")
                    p.op("vector", lambda e, b=b_c2: e.tensor_tensor(
                        offs_full[:, :], offs_full[:, :], psf[b][:, 0:NR],
                        ALU.add),
                         waits=[("pe", cb_mm)], incs=[("vv", 1)])
                    fbank.set(b_c2, ("vv", p.n("vv")))
                    sc_w = [("vv", oint_ok), ("ev", row_ev), ("dz", dz_all)]
                    for re_ in range(nr_red):
                        p.op("gpsimd",
                             lambda e, t=tc, ee=re_: e.indirect_dma_start(
                                 out=xg_dram[:, :],
                                 out_offset=IndirectOffsetOnAxis(
                                     oint[t][:, ee:ee + 1], 0),
                                 in_=rt[t][:, :], in_offset=None,
                                 oob_is_err=False),
                             waits=sc_w if re_ == 0 else [],
                             incs=[("dsc", 16)])
                for _tc in range(4):
                    rt_free.set(_tc, ("dsc", p.n("dsc")))
                    oint_free.set(_tc, ("dsc", p.n("dsc")))
            xmid_all = p.n("ds")
            gates_ev = p.n("ev")
            gates_vv = p.n("vv")
            dsc_all = p.n("dsc")
            xh_stored_all = dict(xh_stored)

            p.barrier()
            ph2_scope.close()

        with ExitStack() as fsco:
            def fsb(name, shape, dt=F32):
                return fsco.enter_context(nc.sbuf_tensor(name, shape, dt))

            from concourse.bass_core_rust import IndirectOffsetOnAxis
            b2g = fsb("b2g_sb", [n_exp, D], BF16)
            b1c = fsb("b1c", [128, n_exp * NHC])
            dma_gp(b2g[:, :], b2g_in[:, :])
            dg_all[0] = p.n("dg")
            b1c_ld = dma_sync(b1c[:, :], b1cols_in[:, :])
            xg_tm = [fsb(f"xg_tm{i}", [128, 4 * ROWW], BF16) for i in range(2)]
            xg_fm = fsb("xg_fm", [128, NKC * 512], BF16)
            h_sb = fsb("h_sb", [128, NHC * 512], BF16)
            w1_sb = fsb("w1_sb", [128, NKC * HID], BF16)
            w2_sb = fsb("w2_sb", [128, NHC * D], BF16)
            contrib = [fsb(f"contrib{i}", [128, D]) for i in range(2)]
            gcol = fsb("gcol", [128, 8])
            tid_i = fsb("tid_i", [128, 8], I32)
            xq_scope = ExitStack()
            xq_buf = xq_scope.enter_context(
                nc.sbuf_tensor("xq_buf", [128, NKC * 512], BF16))

            ctr = Slots(4)
            first_accum = [True]

            def accum_waits(wait_ev):
                w = [("ev", wait_ev), ("da", p.n("da"))]
                if first_accum[0]:
                    w.append(("ds", xmid_all))
                    first_accum[0] = False
                return w

            p.op("vector", lambda e: e.tensor_scalar_mul(
                b2g[:, :], b2g[:, :], 2.0),
                 waits=[("dg", dg_all[0])], incs=[("vv", 1)])
            b2x2 = p.n("vv")

            w1_loaded = [0]
            w2_loaded = [0]
            order = [nr_red, nr_red + 1] + list(range(nr_red))

            def load_w1(e_, wait_pe):
                dma_gp(w1_sb[:, :].rearrange("p (kc h) -> p kc h", kc=NKC),
                       w1_all[e_].rearrange("(kc p) h -> p kc h", p=128),
                       waits=[("pe", wait_pe)], sem="dw1")
                w1_loaded[0] = p.n("dw1")

            def load_w2(e_, wait_pe):
                dma_gp(w2_sb[:, :].rearrange("p (hc f) -> p hc f", hc=NHC),
                       w2_all[e_].rearrange("(hc p) f -> p hc f", p=128),
                       waits=[("pe", wait_pe)], sem="dw2")
                w2_loaded[0] = p.n("dw2")

            load_w1(order[0], 0)
            load_w2(order[0], 0)

            xq_load = [0]
            xq_free = [("pe", 0)]

            def load_xq(q_):
                xq_load[0] = dma_sync(
                    xq_buf[:, :].rearrange("p (kc t) -> p kc t", kc=NKC),
                    xhsc[:, q_ * 512:(q_ + 1) * 512].rearrange(
                        "(kc p) t -> p kc t", p=128),
                    waits=[xq_free[0], ("dsx", xh_stored_all[q_])],
                    sem="sxq0")

            for si, se in enumerate(order[:2]):
                w1_this = w1_loaded[0]
                w2_this = w2_loaded[0]
                for q in range(n_st):
                    load_xq(q)
                    xl = xq_load[0]
                    for hc in range(NHC):
                        bank = hc % 2
                        bw = fbank.wait(bank)
                        for kc in range(NKC):
                            p.op("tensor",
                                 lambda e, b=bank, hh=hc, k=kc: e.matmul(
                                     psf[b][:, :],
                                     w1_sb[:, k * HID + hh * 128:
                                           k * HID + (hh + 1) * 128],
                                     xq_buf[:, k * 512:(k + 1) * 512],
                                     start=(k == 0), stop=(k == NKC - 1)),
                                 waits=([("dw1", w1_this), ("sxq0", xl)] + bw)
                                 if kc == 0 else [],
                                 incs=([("pe", 1)] if kc == NKC - 1 else []))
                        grp = p.n("pe")
                        p.op("scalar",
                             lambda e, b=bank, hh=hc, ee=se: e.activation(
                                 h_sb[:, hh * 512:(hh + 1) * 512],
                                 psf[b][:, :], AF.Relu,
                                 bias=b1c[:, ee * NHC + hh:
                                          ee * NHC + hh + 1]),
                             waits=[("pe", grp), ("dc", b1c_ld)],
                             incs=[("ev", 1)])
                        fbank.set(bank, ("ev", p.n("ev")))
                    h_ev_end = p.n("ev")
                    xq_free[0] = ("pe", p.n("pe"))
                    for tc in range(4):
                        tg = q * 4 + tc
                        cslot = tg % 2
                        cw = ctr.wait(cslot)
                        for fs in range(4):
                            bank = 2 + (tc * 4 + fs) % 2
                            bw = fbank.wait(bank)
                            if si == 0:
                                p.op("tensor",
                                     lambda e, b=bank, t=tg, f=fs: e.matmul(
                                         psf[b][:, :],
                                         gates_fm[:, t * 128:(t + 1) * 128],
                                         b2g[:, f * 512:(f + 1) * 512],
                                         start=True, stop=False),
                                     waits=[("dw2", w2_this),
                                            ("ev", h_ev_end),
                                            ("vv", b2x2), ("ev", gates_ev),
                                            ("gp", gfm_init)] + bw)
                            for hc in range(NHC):
                                p.op("tensor",
                                     lambda e, b=bank, hh=hc, t=tc, f=fs,
                                     fst=(si != 0):
                                     e.matmul(
                                         psf[b][:, :],
                                         h_sb[:, hh * 512 + t * 128:
                                              hh * 512 + (t + 1) * 128],
                                         w2_sb[:, hh * D + f * 512:
                                               hh * D + (f + 1) * 512],
                                         start=(hh == 0 and fst),
                                         stop=(hh == NHC - 1)),
                                     waits=([("dw2", w2_this),
                                             ("ev", h_ev_end)] + bw)
                                     if (hc == 0 and si != 0) else [],
                                     incs=([("pe", 1)] if hc == NHC - 1
                                           else []))
                            grp = p.n("pe")
                            p.op("scalar", lambda e, b=bank, s=cslot, f=fs:
                                 e.activation(
                                     contrib[s][:, f * 512:(f + 1) * 512],
                                     psf[b][:, :], AF.Copy, scale=0.5),
                                 waits=[("pe", grp)] + (cw if fs == 0 else []),
                                 incs=[("ev", 1)])
                            fbank.set(bank, ("ev", p.n("ev")))
                        p.op("gpsimd",
                             lambda e, q2=q, t2=tc, s=cslot: e.dma_start(
                                 out=out_ext[q2 * 512 + t2 * 128:
                                             q2 * 512 + (t2 + 1) * 128, :],
                                 in_=contrib[s][:, :], accum_op=ALU.add),
                             waits=accum_waits(p.n("ev")), incs=[("da", 16)])
                        ctr.set(cslot, ("da", p.n("da")))
                    if q == n_st - 1:
                        load_w1(order[si + 1], p.n("pe"))
                z_pe_end = p.n("pe")
                load_w2(order[si + 1], z_pe_end) if si + 1 < len(order) else None

            xq_scope.close()
            contrib += [fsb(f"contrib{i}", [128, D]) for i in (2, 3)]
            xg_load = {0: 0, 1: 0}
            xg_free = Slots(2)
            xgfm_free = [0]

            for ri, re_ in enumerate(order[2:]):
                slot = ri % 2
                gsem = "sgx0" if slot == 0 else "sgx1"
                xg_load[slot] = dma_sync(
                    xg_tm[slot][:, :].rearrange("p (tc f) -> p tc f", tc=4),
                    xg_dram[re_ * CAP:(re_ + 1) * CAP, :].rearrange(
                        "(tc p) f -> p tc f", p=128),
                    waits=[("dsc", dsc_all)] + xg_free.wait(slot), sem=gsem)
                gl = xg_load[slot]
                w1_this = w1_loaded[0]
                w2_this = w2_loaded[0]
                for tc in range(4):
                    p.op("vector", lambda e, s=slot, t=tc, ee=re_:
                         e.tensor_tensor(
                             gcol[:, s * 4 + t:s * 4 + t + 1],
                             xg_tm[s][:, t * ROWW + D + ee:
                                      t * ROWW + D + ee + 1],
                             xg_tm[s][:, t * ROWW + D + NR + ee:
                                      t * ROWW + D + NR + ee + 1],
                             ALU.add),
                         waits=[(gsem, gl)], incs=[("vv", 1)])
                    p.op("vector", lambda e, s=slot, t=tc: e.tensor_tensor(
                        tidf[:, :],
                        xg_tm[s][:, t * ROWW + D + 2 * NR:
                                 t * ROWW + D + 2 * NR + 1],
                        xg_tm[s][:, t * ROWW + D + 2 * NR + 1:
                                 t * ROWW + D + 2 * NR + 2],
                        ALU.add), incs=[("vv", 1)])
                    p.op("vector", lambda e, s=slot, t=tc: e.tensor_copy(
                        tid_i[:, s * 4 + t:s * 4 + t + 1], tidf[:, :]),
                         incs=[("vv", 1)])
                gt_ok = p.n("vv")
                for tc in range(4):
                    for kc in range(NKC):
                        transpose_128(
                            xg_tm[slot][:, tc * ROWW + kc * 128:
                                        tc * ROWW + (kc + 1) * 128],
                            xg_fm[:, kc * 512 + tc * 128:
                                  kc * 512 + (tc + 1) * 128],
                            [(gsem, gl), ("pe", xgfm_free[0])] if
                            (tc == 0 and kc == 0) else [(gsem, gl)])
                tr_ev = p.n("ev")
                tr_vv = p.n("vv")
                for hc in range(NHC):
                    bank = hc % 2
                    bw = fbank.wait(bank)
                    for kc in range(NKC):
                        p.op("tensor",
                             lambda e, b=bank, hh=hc, k=kc: e.matmul(
                                 psf[b][:, :],
                                 w1_sb[:, k * HID + hh * 128:
                                       k * HID + (hh + 1) * 128],
                                 xg_fm[:, k * 512:(k + 1) * 512],
                                 start=(k == 0), stop=(k == NKC - 1)),
                             waits=([("dw1", w1_this), ("ev", tr_ev),
                                     ("vv", tr_vv)] + bw)
                             if kc == 0 else [],
                             incs=([("pe", 1)] if kc == NKC - 1 else []))
                    grp = p.n("pe")
                    p.op("scalar",
                         lambda e, b=bank, hh=hc, ee=re_: e.activation(
                             h_sb[:, hh * 512:(hh + 1) * 512],
                             psf[b][:, :], AF.Relu,
                             bias=b1c[:, ee * NHC + hh:ee * NHC + hh + 1]),
                         waits=[("pe", grp), ("dc", b1c_ld)],
                         incs=[("ev", 1)])
                    fbank.set(bank, ("ev", p.n("ev")))
                h_ev_end = p.n("ev")
                h_pe_end = p.n("pe")
                xgfm_free[0] = h_pe_end
                xg_free.set(slot, ("pe", h_pe_end))
                if ri + 1 < nr_red:
                    load_w1(order[2 + ri + 1], h_pe_end)
                for tc in range(4):
                    cslot = tc
                    cw = ctr.wait(cslot)
                    for fs in range(4):
                        bank = 2 + (tc * 4 + fs) % 2
                        bw = fbank.wait(bank)
                        for hc in range(NHC):
                            p.op("tensor",
                                 lambda e, b=bank, hh=hc, t=tc, f=fs:
                                 e.matmul(
                                     psf[b][:, :],
                                     h_sb[:, hh * 512 + t * 128:
                                          hh * 512 + (t + 1) * 128],
                                     w2_sb[:, hh * D + f * 512:
                                           hh * D + (f + 1) * 512],
                                     start=(hh == 0), stop=(hh == NHC - 1)),
                                 waits=([("dw2", w2_this),
                                         ("ev", h_ev_end)] + bw)
                                 if hc == 0 else [],
                                 incs=([("pe", 1)] if hc == NHC - 1 else []))
                        grp = p.n("pe")
                        p.op("scalar",
                             lambda e, b=bank, s=cslot, f=fs, sl=slot, t=tc:
                             e.activation(
                                 contrib[s][:, f * 512:(f + 1) * 512],
                                 psf[b][:, :], AF.Copy,
                                 scale=gcol[:, sl * 4 + t:sl * 4 + t + 1]),
                             waits=[("pe", grp), ("vv", gt_ok)] +
                             (cw if fs == 0 else []),
                             incs=[("ev", 1)])
                        fbank.set(bank, ("ev", p.n("ev")))
                    p.op("gpsimd",
                         lambda e, s=cslot, sl=slot, t=tc: e.indirect_dma_start(
                             out=out_ext[:, :],
                             out_offset=IndirectOffsetOnAxis(
                                 tid_i[:, sl * 4 + t:sl * 4 + t + 1], 0),
                             in_=contrib[s][:, :], in_offset=None,
                             oob_is_err=False, compute_op=ALU.add),
                         waits=accum_waits(p.n("ev")) + [("vv", gt_ok)],
                         incs=[("da", 16)])
                    ctr.set(cslot, ("da", p.n("da")))
                z_pe_end = p.n("pe")
                if ri + 1 < nr_red:
                    load_w2(order[2 + ri + 1], z_pe_end)

            p.op("gpsimd", lambda e: e.nop(),
                 waits=[("da", p.n("da")), ("dw1", p.n("dw1")),
                        ("dw2", p.n("dw2")), ("dg", p.n("dg")),
                        ("dwa", p.n("dwa")), ("dwb", p.n("dwb")),
                        ("dsc", p.n("dsc"))])
            p.op("sync", lambda e: e.nop(),
                 waits=[("ds", p.n("ds")), ("dsx", p.n("dsx")),
                        ("sx", p.n("sx")), ("dz", p.n("dz")),
                        ("sxq0", p.n("sxq0")), ("sgx0", p.n("sgx0")),
                        ("sgx1", p.n("sgx1"))])

        p.emit(nc, sems)

    return nc


def _host_constants(cfg):
    n_st = cfg["n_st"]
    lam = np.exp(-(2.0 ** (-0.5)) ** (np.arange(1, NH + 1, dtype=np.float64)))
    i_idx = np.arange(128, dtype=np.float64)
    dmaskT = np.zeros((128, NH * 128))
    decayq = np.zeros((128, NH * 128))
    dkx = np.zeros((128, NH * 4))
    dk1 = np.zeros((128, NH))
    for h in range(NH):
        diff = i_idx[None, :] - i_idx[:, None]
        with np.errstate(over="ignore", under="ignore"):
            m = np.where(diff >= 0, lam[h] ** diff, 0.0)
            dmaskT[:, h * 128:(h + 1) * 128] = m
            decayq[:, h * 128:(h + 1) * 128] = (lam[h] ** (i_idx + 1.0))[None, :]
            dk1[:, h] = lam[h] ** (127.0 - i_idx)
            for cc in range(4):
                dkx[:, h * 4 + cc] = lam[h] ** (127.0 - i_idx + 128.0 * (3 - cc))
    NTC = 4 * n_st
    n_exp = cfg["n_exp"]
    nr_red = n_exp - 2
    consts = dict(
        dmaskT=dmaskT.astype(np.float32),
        decayq=decayq.astype(np.float32),
        dkx=dkx.astype(np.float32),
        dk1=dk1.astype(np.float32),
        ident_bf=np.eye(128).astype(ml_dtypes.bfloat16),
        triu=np.triu(np.ones((128, 128)), k=1).astype(ml_dtypes.bfloat16),
        eCrow=np.tile((np.arange(NR) * CAP).astype(np.float32), (128, 1)),
        trashb=np.tile((nr_red * CAP + np.arange(NR)).astype(np.float32),
                       (128, 1)),
        tokid=(np.arange(NTC)[None, :] * 128 +
               np.arange(128)[:, None]).astype(np.float32),
    )
    sinw = []
    span = 512 * n_st
    for c in range(NCORES):
        w = np.zeros((128, NCORES * NH))
        r0 = c % 4
        for g in range(NCORES):
            if g // 4 != c // 4:
                continue
            rgl = g % 4
            if rgl < r0:
                for h in range(NH):
                    with np.errstate(over="ignore", under="ignore"):
                        w[:, g * NH + h] = lam[h] ** (span * (r0 - 1 - rgl))
        sinw.append(w.astype(np.float32))
    return consts, sinw


def make_in_maps(inputs, cfg):
    n_st = cfg["n_st"]
    n_exp = cfg["n_exp"]
    T = 512 * n_st
    x = np.asarray(inputs["L0_attn_x"], dtype=np.float32).reshape(-1, D)
    wqkvT = np.ascontiguousarray(
        np.asarray(inputs["L0_attn_w_qkv"], dtype=np.float32).T)
    woutT = np.ascontiguousarray(
        np.asarray(inputs["L0_attn_w_out"], dtype=np.float32).T)
    b_qkv = np.asarray(inputs["L0_attn_b_qkv"], dtype=np.float32)
    b_out = np.asarray(inputs["L0_attn_b_out"], dtype=np.float32)
    sw1 = np.asarray(inputs["L0_ffn_sw1"], dtype=np.float32)
    sb1 = np.asarray(inputs["L0_ffn_sb1"], dtype=np.float32)
    sw2 = np.asarray(inputs["L0_ffn_sw2"], dtype=np.float32)
    sb2 = np.asarray(inputs["L0_ffn_sb2"], dtype=np.float32)
    rw1 = np.asarray(inputs["L0_ffn_rw1"], dtype=np.float32)
    rb1 = np.asarray(inputs["L0_ffn_rb1"], dtype=np.float32)
    rw2 = np.asarray(inputs["L0_ffn_rw2"], dtype=np.float32)
    rb2 = np.asarray(inputs["L0_ffn_rb2"], dtype=np.float32)
    router_w = np.asarray(inputs["L0_ffn_router_w"], dtype=np.float32)
    router_b = np.asarray(inputs["L0_ffn_router_b"], dtype=np.float32)

    nr_red = n_exp - 2
    w1_all = np.ascontiguousarray(np.concatenate([rw1[:nr_red], sw1], axis=0))
    w2_all = np.ascontiguousarray(np.concatenate([rw2[:nr_red], sw2], axis=0))
    b1_all = np.concatenate([rb1[:nr_red], sb1], axis=0)
    b1cols = np.ascontiguousarray(b1_all.reshape(n_exp * NHC, 128).T)
    b2stack = np.concatenate([rb2[:nr_red], sb2], axis=0).astype(
        ml_dtypes.bfloat16)

    consts, sinw = _host_constants(cfg)
    shared = dict(
        wqkvT=wqkvT, woutT=woutT, w1_all=w1_all, w2_all=w2_all,
        routerw=router_w,
        bq_row=b_qkv[None, :].astype(ml_dtypes.bfloat16),
        bqcol=np.ascontiguousarray(b_qkv[0:D].reshape(NKC, 128).T),
        bout_row=b_out[None, :].astype(ml_dtypes.bfloat16),
        routerb_row=router_b[None, :].astype(ml_dtypes.bfloat16),
        b2g=b2stack, b1cols=b1cols, **consts)
    in_maps = []
    for c in range(NCORES):
        m = dict(shared)
        m["x"] = np.ascontiguousarray(x[c * T:(c + 1) * T])
        m["sinw"] = sinw[c]
        in_maps.append(m)
    return in_maps


def kernel(**inputs):
    cfg = default_cfg()
    in_maps = make_in_maps(inputs, cfg)
    nc = build_nc(cfg)
    res = run_bass_kernel_spmd(nc, in_maps, core_ids=list(range(NCORES)))
    out = np.concatenate([np.asarray(r["out"]) for r in res.results], axis=0)
    return out.reshape(2, 8192, D).astype(np.float32)


# version 08 (final)
# speedup vs baseline: 1.1558x; 1.0016x over previous
import math
import numpy as np
import ml_dtypes
from contextlib import ExitStack

import concourse.bass as bass
import concourse.mybir as mybir
from concourse.bass_utils import run_bass_kernel_spmd

F32 = mybir.dt.float32
BF16 = mybir.dt.bfloat16
I32 = mybir.dt.int32
AF = mybir.ActivationFunctionType
ALU = mybir.AluOpType
AX = mybir.AxisListType

D = 2048
NH = 16
HID = 1408
NHC = 11
NKC = 16
NE = 34
NR = 32
TOPK = 6
EPS = 1e-5
CAP = 512
ROWW = 2176
NCORES = 8


def default_cfg():
    return dict(n_st=4, n_exp=NE)


class Prog:
    ENGINES = ("sync", "tensor", "scalar", "vector", "gpsimd")

    def __init__(self):
        self.ops = []
        self.cnt = {}

    def n(self, sem):
        return self.cnt.get(sem, 0)

    CHAIN = {"scalar": "ev", "vector": "vv"}

    def op(self, eng, fn=None, waits=(), incs=(), chain=True):
        assert eng in self.ENGINES
        incs = list(incs)
        waits = list(waits)
        csem = self.CHAIN.get(eng)
        if fn is not None and csem is not None and chain:
            waits.append((csem, self.n(csem)))
            if not any(s == csem for s, _ in incs):
                incs.append((csem, 1))
        ws = {}
        for s, v in waits:
            if v > 0:
                assert v <= self.n(s), f"wait {s}>={v} but only {self.n(s)}"
                ws[s] = max(ws.get(s, 0), v)
        if incs:
            assert fn is not None, "incs must ride a real instruction"
        assert len(incs) <= 2
        self.ops.append((eng, fn, sorted(ws.items()), incs))
        for s, k in incs:
            self.cnt[s] = self.n(s) + k

    def barrier(self):
        snap = [(s, v) for s, v in self.cnt.items() if v > 0]
        for eng in self.ENGINES:
            self.op(eng, None, waits=snap)

    def emit(self, nc, sems):
        ops = self.ops

        def mk(eng_name):
            def inner(eng):
                for (e, fn, waits, incs) in ops:
                    if e != eng_name:
                        continue
                    for s, v in waits:
                        eng.wait_ge(sems[s], v)
                    inst = fn(eng) if fn is not None else None
                    for s, k in incs:
                        inst.then_inc(sems[s], k)
            return inner

        with nc.Block() as block:
            block.sync(mk("sync"))
            block.tensor(mk("tensor"))
            block.scalar(mk("scalar"))
            block.vector(mk("vector"))
            block.gpsimd(mk("gpsimd"))


class Slots:

    def __init__(self, n):
        self.free = {i: [] for i in range(n)}

    def wait(self, i):
        return list(self.free[i])

    def set(self, i, *conds):
        self.free[i] = [c for c in conds if c[1] > 0]

    def add(self, i, cond):
        if cond[1] > 0:
            self.free[i].append(cond)


def build_nc(cfg):
    n_st = cfg["n_st"]
    n_exp = cfg["n_exp"]
    nr_red = n_exp - 2
    T = 512 * n_st
    NTC = T // 128

    lam = np.exp(-(2.0 ** (-0.5)) ** (np.arange(1, NH + 1, dtype=np.float64)))

    nc = bass.Bass("TRN2", target_bir_lowering=False)

    def din(name, shape, dt=F32):
        return nc.dram_tensor(name, shape, dt, kind="ExternalInput")

    x_ext = din("x", [T, D])
    wqkvT = din("wqkvT", [D, 3 * D])
    woutT = din("woutT", [D, D])
    w1_all = din("w1_all", [n_exp, D, HID])
    w2_all = din("w2_all", [n_exp, HID, D])
    routerw = din("routerw", [D, NR])
    bq_row = din("bq_row", [1, 3 * D], BF16)
    bqcol_in = din("bqcol", [128, NKC])
    bout_row = din("bout_row", [1, D], BF16)
    routerb_row = din("routerb_row", [1, NR], BF16)
    b2g_in = din("b2g", [n_exp, D], BF16)
    b1cols_in = din("b1cols", [128, n_exp * NHC])
    ident_bf_in = din("ident_bf", [128, 128], BF16)
    dmaskT_in = din("dmaskT", [128, NH * 128])
    decayq_in = din("decayq", [128, NH * 128])
    dkx_in = din("dkx", [128, NH * 4])
    dk1_in = din("dk1", [128, NH])
    sinw_in = din("sinw", [128, NCORES * NH])
    triu_in = din("triu", [128, 128], BF16)
    eCrow_in = din("eCrow", [128, NR])
    trashb_in = din("trashb", [128, NR])
    tokid_in = din("tokid", [128, NTC])

    out_ext = nc.dram_tensor("out", [T, D], F32, kind="ExternalOutput")

    qsc = nc.dram_tensor("qsc", [D, T], BF16)
    ksc = nc.dram_tensor("ksc", [T, D], BF16)
    vsc = nc.dram_tensor("vsc", [T, D], BF16)
    xhsc = nc.dram_tensor("xhsc", [D, T], BF16)
    xg_rows = nr_red * CAP + NTC * NR + 128
    xg_dram = nc.dram_tensor("xg_dram", [xg_rows, ROWW], BF16)
    cc_in = nc.dram_tensor("cc_in", [128, NH * 128], F32)
    from concourse.replica_groups import (
        maybe_share_collective_output_space, ReplicaGroups)
    rg = [list(range(NCORES))]
    aspace = maybe_share_collective_output_space("AllGather", ReplicaGroups(rg))
    cc_out = nc.dram_tensor("cc_out", [NCORES * 128, NH * 128], F32,
                            addr_space=aspace)

    p = Prog()
    es = ExitStack()
    with es:
        def sb(name, shape, dt=F32):
            return es.enter_context(nc.sbuf_tensor(name, shape, dt))

        psf = [es.enter_context(nc.psum_tensor(f"psf{i}", [128, 512], F32))
               for i in range(6)]
        psb = [es.enter_context(nc.psum_tensor(f"psb{i}", [128, 1024], BF16))
               for i in range(2)]
        sem_names = ["dc", "dg", "sx", "sq", "sk", "sv", "sm", "sg2",
                     "sxq0", "sxq1", "dwa", "dwb", "dw1", "dw2",
                     "ds", "dsx", "da", "cc", "pe", "ev", "vv", "gp",
                     "dz", "dsc", "sgx0", "sgx1", "sg2b"]
        sems = {s: es.enter_context(nc.semaphore(s)) for s in sem_names}

        ident_bf = sb("ident_bf_s", [128, 128], BF16)
        ones_bf = sb("ones_bf", [1, 128], BF16)
        rbr = sb("rbr", [1, NR], BF16)
        rw_bf = sb("rw_bf", [128, NKC * NR], BF16)
        gates_fm = sb("gates_fm", [n_exp, T], BF16)
        triu_sb = sb("triu_sb", [128, 128], BF16)
        eCrow = sb("eCrow_s", [128, NR])
        trashb = sb("trashb_s", [128, NR])
        tokid = sb("tokid_s", [128, NTC])
        onesc_bf = sb("onesc_bf", [128, 1], BF16)
        zr = sb("zr", [128, ROWW], BF16)
        offs_full = sb("offs_full", [128, NR])
        gtmp = sb("gtmp", [128, NR])
        ghif = sb("ghif", [128, NR])
        mk_f = sb("mk_f", [128, NR])
        mk_b = sb("mk_b", [128, NR], BF16)
        ov_t = sb("ov_t", [128, NR])
        vd_t = sb("vd_t", [128, NR])
        pe_t = sb("pe_t", [128, NR])
        tr_t = sb("tr_t", [128, NR])
        of_t = sb("of_t", [128, NR])
        cntb = sb("cntb", [1, NR], BF16)
        tidf = sb("tidf", [128, 1])
        stat = sb("stat", [128, 64])
        e_buf = sb("e_buf", [128, NR])
        m8 = sb("m8", [128, 8])
        erem = sb("erem", [128, NR])
        gacc = sb("gacc", [128, NR])
        gbf_tm = sb("gbf_tm", [128, NR], BF16)
        sq_dump = sb("sq_dump", [128, D], BF16)

        fbank = Slots(6)
        bbank = Slots(2)
        tr_par = [0]

        def dma_sync(dst, src, waits=(), sem="dc"):
            w = list(waits) + ([(sem, p.n(sem))] if sem == "dc" else [])
            p.op("sync", lambda e, d=dst, s=src: e.dma_start(out=d, in_=s),
                 waits=w, incs=[(sem, 16)])
            return p.n(sem)

        def dma_gp(dst, src, waits=(), sem="dg"):
            w = list(waits) + ([(sem, p.n(sem))] if sem == "dg" else [])
            p.op("gpsimd", lambda e, d=dst, s=src: e.dma_start(out=d, in_=s),
                 waits=w, incs=[(sem, 16)])
            return p.n(sem)

        dma_sync(eCrow[:, :], eCrow_in[:, :])
        dma_sync(trashb[:, :], trashb_in[:, :])
        dma_sync(tokid[:, :], tokid_in[:, :])
        dma_sync(triu_sb[:, :], triu_in[:, :])
        dma_gp(ident_bf[:, :], ident_bf_in[:, :])
        dma_gp(rbr[:, :], routerb_row[:, :])
        dma_gp(rw_bf[:, :].rearrange("p (kc e) -> p kc e", kc=NKC),
               routerw.rearrange("(kc p) e -> p kc e", p=128))
        p.op("gpsimd", lambda e: e.memset(ones_bf[:, :], 1.0), incs=[("gp", 1)])
        p.op("gpsimd", lambda e: e.memset(onesc_bf[:, :], 1.0), incs=[("gp", 1)])
        p.op("gpsimd", lambda e: e.memset(zr[:, :], 0.0), incs=[("gp", 1)])
        zr_init = p.n("gp")
        p.op("vector", lambda e: e.memset(offs_full[:, :], 0.0), incs=[("vv", 1)])
        for r in range(nr_red * CAP // 128):
            p.op("sync", lambda e, rr=r: e.dma_start(
                out=xg_dram[rr * 128:(rr + 1) * 128, :], in_=zr[:, :]),
                 waits=[("gp", zr_init)] if r == 0 else [], incs=[("dz", 16)])
        dz_all = p.n("dz")
        p.op("vector", lambda e: e.memset(stat[:, 51:52], EPS), incs=[("vv", 1)])
        eps_init = p.n("vv")
        setup_dx = p.n("dc")
        dc_all = [setup_dx]
        setup_dw = p.n("dg")
        setup_gp = p.n("gp")
        dg_all = [setup_dw]

        ev_par = [0]

        def transpose_128(src_ap, dst_ap, extra_waits=()):
            b = tr_par[0]
            tr_par[0] ^= 1
            p.op("tensor",
                 lambda e, s=src_ap, bb=b: e.transpose(
                     psb[bb][:, 0:128], s, ident_bf[:, :]),
                 waits=bbank.wait(b) + list(extra_waits) +
                 [("dg", dg_all[0])], incs=[("pe", 1)])
            mm = p.n("pe")
            ev_par[0] ^= 1
            if ev_par[0]:
                p.op("scalar",
                     lambda e, d=dst_ap, bb=b: e.copy(d, psb[bb][:, 0:128]),
                     waits=[("pe", mm)], incs=[("ev", 1)])
                bbank.set(b, ("ev", p.n("ev")))
            else:
                p.op("vector",
                     lambda e, d=dst_ap, bb=b: e.tensor_copy(
                         d, psb[bb][:, 0:128]),
                     waits=[("pe", mm)], incs=[("vv", 1)])
                bbank.set(b, ("vv", p.n("vv")))

        xhat_tm_ref = [None]

        def ln_tm(src_tile_fn, n_tc, dst_fn, first_waits, ts_dst_fn=None,
                  ts_waits_fn=None):
            xhat_tm = xhat_tm_ref[0]
            c_sum = stat[:, 0:0 + n_tc]
            c_sq = stat[:, 8:8 + n_tc]
            c_m = stat[:, 16:16 + n_tc]
            c_rstd = stat[:, 24:24 + n_tc]
            c_nmr = stat[:, 32:32 + n_tc]
            c_t0 = stat[:, 40:40 + n_tc]
            for tc in range(n_tc):
                src = src_tile_fn(tc)
                w = (list(first_waits) + [("vv", p.n("vv"))]) if tc == 0 else []
                p.op("scalar",
                     lambda e, s=src, c=c_sq[:, tc:tc + 1]: e.activation(
                         sq_dump[:, :], s, AF.Square, accum_out=c),
                     waits=w + [("ev", p.n("ev"))], incs=[("ev", 1)])
                p.op("scalar",
                     lambda e, s=src, c=c_sum[:, tc:tc + 1]: e.activation(
                         sq_dump[:, :], s, AF.Copy, accum_out=c),
                     waits=[("ev", p.n("ev"))], incs=[("ev", 1)])
            act_done = p.n("ev")
            p.op("vector", lambda e: e.tensor_scalar_mul(c_m, c_sum, 1.0 / D),
                 waits=[("ev", act_done)])
            p.op("vector", lambda e: e.tensor_scalar_mul(c_t0, c_sq, 1.0 / D))
            p.op("vector", lambda e: e.tensor_tensor(c_rstd, c_m, c_m, ALU.mult))
            p.op("vector",
                 lambda e: e.tensor_tensor(c_t0, c_t0, c_rstd, ALU.subtract),
                 incs=[("vv", 1)])
            var_done = p.n("vv")
            p.op("scalar", lambda e: e.activation(c_t0, c_t0, AF.Sqrt,
                                                  bias=stat[:, 51:52]),
                 waits=[("vv", var_done)], incs=[("ev", 1)])
            sd_done = p.n("ev")
            p.op("vector", lambda e: e.reciprocal(c_rstd, c_t0),
                 waits=[("ev", sd_done)])
            p.op("vector", lambda e: e.scalar_tensor_tensor(
                c_nmr, c_m, -1.0, c_rstd, ALU.mult, ALU.mult),
                 incs=[("vv", 1)])
            stats_done = p.n("vv")
            for tc in range(n_tc):
                src = src_tile_fn(tc)
                tdst = ts_dst_fn(tc) if ts_dst_fn else xhat_tm[:, :]
                tw = (ts_waits_fn(tc) if ts_waits_fn else [])
                p.op("vector",
                     lambda e, s=src, r=c_rstd[:, tc:tc + 1],
                     nm=c_nmr[:, tc:tc + 1], xt=tdst:
                     e.tensor_scalar(xt, s, r, nm, ALU.mult, ALU.add),
                     waits=[("vv", stats_done), ("pe", p.n("pe"))] + tw,
                     incs=[("vv", 1)])
                xh = p.n("vv")
                for kc in range(NKC):
                    transpose_128(tdst[:, kc * 128:(kc + 1) * 128],
                                  dst_fn(tc, kc), [("vv", xh)])

        with ExitStack() as asco:
            def asb(name, shape, dt=F32):
                return asco.enter_context(nc.sbuf_tensor(name, shape, dt))

            sinw = asb("sinw_s", [128, NCORES * NH])
            S_delta = asb("S_delta", [128, NH * 128])
            S_state = S_delta
            x_buf = [asb("x_buf0", [128, 4 * D])]
            x_buf = [x_buf[0], x_buf[0]]
            fm_buf = asb("fm_buf", [128, NKC * 512], BF16)
            k_tm = asb("k_tm", [128, 4 * D], BF16)
            v_tm = asb("v_tm", [128, 4 * D], BF16)
            wsl = [asb(f"wsl{i}", [128, NKC * 512], BF16) for i in range(2)]
            gat_buf = None
            rt = [asb(f"rt{i}", [128, ROWW], BF16) for i in range(4)]
            oint = [asb(f"oint{i}", [128, NR], I32) for i in range(4)]
            for _i in range(4):
                p.op("vector", lambda e, i=_i: e.memset(
                    rt[i][:, D + 2 * NR + 2:ROWW], 0.0), incs=[("vv", 1)])


            consts_dx = dma_sync(sinw[:, :], sinw_in[:, :])
            dc_all[0] = p.n("dc")
            p.op("vector", lambda e: e.memset(S_delta[:, :], 0.0),
                 incs=[("vv", 1)])
            sdelta_init = p.n("vv")

            xload = {}
            xslot = Slots(1)

            def load_x(st):
                w = xslot.wait(0)
                xload[st] = dma_sync(
                    x_buf[st % 2][:, :].rearrange("p (tc d) -> p tc d", tc=4),
                    x_ext[st * 512:(st + 1) * 512, :].rearrange(
                        "(tc p) d -> p tc d", p=128), waits=w, sem="sx")

            load_x(0)

            wslot = Slots(2)
            kd_slot = Slots(2)
            rt_free = Slots(4)
            oint_free = Slots(4)

            ph1_scope = ExitStack()
            bqr = ph1_scope.enter_context(
                nc.sbuf_tensor("bqr", [1, 3 * D], BF16))
            dkx = ph1_scope.enter_context(
                nc.sbuf_tensor("dkx_s", [128, NH * 4], F32))
            kdb = ph1_scope.enter_context(
                nc.sbuf_tensor("kdb", [128, 2 * 128], BF16))
            qfm_buf = ph1_scope.enter_context(
                nc.sbuf_tensor("qfm_buf", [128, NKC * 512], BF16))
            q_tm = ph1_scope.enter_context(
                nc.sbuf_tensor("q_tm", [128, 4 * D], BF16))
            bqcol = ph1_scope.enter_context(
                nc.sbuf_tensor("bqcol_s", [128, NKC], F32))
            dma_gp(bqr[:, :], bq_row[:, :])
            setup_dw2 = p.n("dg")
            dg_all[0] = p.n("dg")
            dma_sync(dkx[:, :], dkx_in[:, :])
            dma_sync(bqcol[:, :], bqcol_in[:, :])
            dc_all[0] = p.n("dc")
            for st in range(n_st):
                xb = x_buf[st % 2]
                scst_prev = p.n("ds")
                ln_tm(lambda tc, b=xb: b[:, tc * D:(tc + 1) * D], 4,
                      lambda tc, kc: fm_buf[:, kc * 512 + tc * 128:
                                            kc * 512 + (tc + 1) * 128],
                      [("sx", xload[st])],
                      ts_dst_fn=lambda tc: rt[tc][:, 0:D],
                      ts_waits_fn=lambda tc: rt_free.wait(tc))
                for _tc in range(4):
                    rt_free.set(_tc, ("pe", p.n("pe")))
                xhat_done = p.n("ev")
                xhat_vv = p.n("vv")
                xslot.set(0, ("ev", xhat_done))

                first_evac = True
                for pi, (dst, fofs) in enumerate(
                        ((None, 0), (k_tm, D), (v_tm, 2 * D))):
                    for fsl in range(4):
                        slot = (pi * 4 + fsl) % 2
                        wsem = "dwa" if slot == 0 else "dwb"
                        wl = dma_gp(
                            wsl[slot][:, :].rearrange(
                                "p (kc f) -> p kc f", kc=NKC),
                            wqkvT[:, fofs + fsl * 512:
                                  fofs + (fsl + 1) * 512].rearrange(
                                "(kc p) f -> p kc f", p=128),
                            waits=wslot.wait(slot), sem=wsem)
                        if pi == 0:
                            for fo_i in range(4):
                                fc = fsl * 4 + fo_i
                                bank = fo_i % 4
                                bw = fbank.wait(bank)
                                for kc in range(NKC):
                                    p.op("tensor",
                                         lambda e, b=bank, k=kc, s=slot,
                                         fi=fo_i: e.matmul(
                                             psf[b][:, :],
                                             wsl[s][:, k * 512 + fi * 128:
                                                    k * 512 + (fi + 1) * 128],
                                             fm_buf[:, k * 512:(k + 1) * 512],
                                             start=(k == 0),
                                             stop=(k == NKC - 1)),
                                         waits=([("ev", xhat_done),
                                                 ("vv", xhat_vv),
                                                 (wsem, wl)] + bw)
                                         if kc == 0 else [],
                                         incs=([("pe", 1)]
                                               if kc == NKC - 1 else []))
                                grp = p.n("pe")
                                ev_w = [("pe", grp), ("dc", dc_all[0])]
                                if first_evac and st > 0:
                                    ev_w.append(("ds", scst_prev))
                                first_evac = False
                                p.op("scalar",
                                     lambda e, b=bank, c=fc: e.activation(
                                         qfm_buf[:, c * 512:(c + 1) * 512],
                                         psf[b][:, :], AF.Identity,
                                         bias=bqcol[:, c:c + 1]),
                                     waits=ev_w, incs=[("ev", 1)])
                                fbank.set(bank, ("ev", p.n("ev")))
                            wslot.set(slot, ("pe", p.n("pe")))
                            continue
                        for tc in range(4):
                            bank = tc % 4
                            bw = fbank.wait(bank)
                            for kc in range(NKC):
                                p.op("tensor",
                                     lambda e, b=bank, k=kc, t=tc, s=slot:
                                     e.matmul(
                                         psf[b][:, :],
                                         fm_buf[:, k * 512 + t * 128:
                                                k * 512 + (t + 1) * 128],
                                         wsl[s][:, k * 512:(k + 1) * 512],
                                         start=(k == 0), stop=False),
                                     waits=([("ev", xhat_done),
                                             ("vv", xhat_vv),
                                             (wsem, wl)] + bw)
                                     if kc == 0 else [])
                            p.op("tensor",
                                 lambda e, b=bank, fo=fofs, f=fsl: e.matmul(
                                     psf[b][:, :], ones_bf[0:1, 0:128],
                                     bqr[0:1, fo + f * 512:fo + (f + 1) * 512],
                                     start=False, stop=True),
                                 waits=[("dg", dg_all[0]), ("gp", setup_gp)],
                                 incs=[("pe", 1)])
                            grp = p.n("pe")
                            ev_w = [("pe", grp)]
                            if first_evac and st > 0:
                                ev_w.append(("ds", scst_prev))
                            first_evac = False
                            p.op("scalar",
                                 lambda e, d=dst, b=bank, t=tc, f=fsl: e.copy(
                                     d[:, t * D + f * 512:t * D + (f + 1) * 512],
                                     psf[b][:, :]),
                                 waits=ev_w, incs=[("ev", 1)])
                            fbank.set(bank, ("ev", p.n("ev")))
                        wslot.set(slot, ("pe", p.n("pe")))
                qkv_ev = p.n("ev")

                dma_sync(
                    qsc[:, st * 512:(st + 1) * 512].rearrange(
                        "(hc p) t -> p hc t", p=128),
                    qfm_buf[:, :].rearrange("p (hc t) -> p hc t", hc=NKC),
                    waits=[("ev", qkv_ev)], sem="ds")
                for (src, dstt) in ((k_tm, ksc), (v_tm, vsc)):
                    dma_sync(
                        dstt[st * 512:(st + 1) * 512, :].rearrange(
                            "(tc p) d -> p tc d", p=128),
                        src[:, :].rearrange("p (tc d) -> p tc d", tc=4),
                        waits=[("ev", qkv_ev)], sem="ds")

                for h in range(NH):
                    bank = 4 + h % 2
                    bw = fbank.wait(bank)
                    for cc in range(4):
                        sl = (h * 4 + cc) % 2
                        p.op("scalar",
                             lambda e, hh=h, c=cc, s=sl: e.activation(
                                 kdb[:, s * 128:(s + 1) * 128],
                                 k_tm[:, c * D + hh * 128:
                                      c * D + (hh + 1) * 128],
                                 AF.Copy,
                                 scale=dkx[:, hh * 4 + c:hh * 4 + c + 1]),
                             waits=[("ev", qkv_ev), ("dc", dc_all[0])] +
                             kd_slot.wait(sl),
                             incs=[("ev", 1)])
                        kd_ok = p.n("ev")
                        p.op("tensor",
                             lambda e, hh=h, c=cc, b=bank, s=sl: e.matmul(
                                 psf[b][:, 0:128],
                                 kdb[:, s * 128:(s + 1) * 128],
                                 v_tm[:, c * D + hh * 128:
                                      c * D + (hh + 1) * 128],
                                 start=(c == 0), stop=(c == 3)),
                             waits=[("ev", kd_ok)] + (bw if cc == 0 else []),
                             incs=[("pe", 1)])
                        kd_slot.set(sl, ("pe", p.n("pe")))
                    grp = p.n("pe")
                    l512 = float(np.float32(lam[h] ** 512))
                    p.op("vector",
                         lambda e, hh=h, b=bank, l=l512: e.scalar_tensor_tensor(
                             S_delta[:, hh * 128:(hh + 1) * 128],
                             S_delta[:, hh * 128:(hh + 1) * 128], l,
                             psf[b][:, 0:128], ALU.mult, ALU.add),
                         waits=[("pe", grp), ("vv", sdelta_init)],
                         incs=[("vv", 1)])
                    fbank.set(bank, ("vv", p.n("vv")))

                if st + 1 < n_st:
                    load_x(st + 1)

            ph1_vv = p.n("vv")
            dma_sync(cc_in[:, :], S_delta[:, :], waits=[("vv", ph1_vv)],
                     sem="ds")
            sdel_st = p.n("ds")
            p.op("gpsimd", lambda e: e.collective_compute(
                "AllGather", ALU.bypass, replica_groups=rg,
                ins=[cc_in.ap().opt()], outs=[cc_out.ap().opt()]),
                 waits=[("ds", sdel_st)], incs=[("cc", 1)])
            p.op("vector", lambda e: e.memset(S_state[:, :], 0.0),
                 waits=[("ds", sdel_st)], incs=[("vv", 1)])
            gslot = Slots(2)
            gb0 = x_buf[0][:, 0:NH * 128]
            gb1 = x_buf[0][:, NH * 128:2 * NH * 128]
            for r in range(NCORES):
                gb = gb0 if r % 2 == 0 else gb1
                gsm = "sg2" if r % 2 == 0 else "sg2b"
                gw = [("cc", 1)] + gslot.wait(r % 2)
                if r == 0:
                    gw += xslot.wait(0)
                gl = dma_sync(gb, cc_out[r * 128:(r + 1) * 128, :],
                              waits=gw, sem=gsm)
                for h in range(NH):
                    p.op("vector",
                         lambda e, b=gb, hh=h, rr=r:
                         e.scalar_tensor_tensor(
                             S_state[:, hh * 128:(hh + 1) * 128],
                             b[:, hh * 128:(hh + 1) * 128],
                             sinw[:, rr * NH + hh:rr * NH + hh + 1],
                             S_state[:, hh * 128:(hh + 1) * 128],
                             ALU.mult, ALU.add),
                         waits=[(gsm, gl)] if h == 0 else [],
                         incs=([("vv", 1)] if h == NH - 1 else []))
                gslot.set(r % 2, ("vv", p.n("vv")))
            sin_done = p.n("vv")
            xslot.set(0, ("vv", sin_done))

            ph1_scope.close()
            ph2_scope = ExitStack()

            def p2sb(name, shape, dt=BF16):
                return ph2_scope.enter_context(nc.sbuf_tensor(name, shape, dt))

            boutr = p2sb("boutr", [1, D])
            dmaskT = p2sb("dmaskT_s", [128, NH * 128], F32)
            decayq = p2sb("decayq_s", [128, NH * 128], F32)
            dk1 = p2sb("dk1_s", [128, NH], F32)
            qf_st = p2sb("qf_st", [128, NH * 512])
            kf16 = p2sb("kf16", [128, NH * 128])
            smb16 = p2sb("smb16", [128, NH * 128])
            sbf16 = p2sb("sbf16", [128, NH * 128])
            dma_gp(boutr[:, :], bout_row[:, :])
            dw2b = p.n("dg")
            dma_sync(dmaskT[:, :], dmaskT_in[:, :])
            dma_sync(decayq[:, :], decayq_in[:, :])
            ph2_dc = dma_sync(dk1[:, :], dk1_in[:, :])
            qkv_load = {}
            xres_load = {}

            qkv_free = [("pe", 0)]

            def load_qkv(st):
                w0 = [("ds", p.n("ds")), qkv_free[0]]
                vals = [dma_sync(
                    qf_st[:, :].rearrange("p (hc t) -> p hc t", hc=NH),
                    qsc[:, st * 512:(st + 1) * 512].rearrange(
                        "(hc p) t -> p hc t", p=128),
                    waits=w0, sem="sq")]
                w0 = [qkv_free[0]]
                for (srct, dst, sname) in ((ksc, k_tm, "sk"),
                                           (vsc, v_tm, "sv")):
                    vals.append(dma_sync(
                        dst[:, :].rearrange("p (tc d) -> p tc d", tc=4),
                        srct[st * 512:(st + 1) * 512, :].rearrange(
                            "(tc p) d -> p tc d", p=128),
                        waits=w0, sem=sname))
                qkv_load[st] = vals

            def load_xres(st):
                xres_load[st] = dma_sync(
                    x_buf[st % 2][:, :].rearrange("p (tc d) -> p tc d", tc=4),
                    x_ext[st * 512:(st + 1) * 512, :].rearrange(
                        "(tc p) d -> p tc d", p=128),
                    waits=xslot.wait(0), sem="sx")

            qf_free = [[]]
            kf_free = [[]]
            smb_free = [[]]
            supd_prev = [0]
            inter_prev = [0]
            xh_stored = {}
            from concourse.bass_core_rust import IndirectOffsetOnAxis
            p.op("gpsimd", lambda e: e.memset(gates_fm[:, :], 0.5),
                 incs=[("gp", 1)])
            gfm_init = p.n("gp")

            for st in range(n_st):
                if st == 0:
                    load_qkv(0)
                    load_xres(0)
                ldq, ldk, ldv = qkv_load[st]
                first_ch = True
                for cc in range(4):
                    k_ev = {}
                    k_vv = {}
                    for h in range(NH):
                        bb = tr_par[0]; tr_par[0] ^= 1
                        w0 = list(bbank.wait(bb))
                        if h == 0 and cc == 0:
                            w0 += [("sq", ldq), ("sk", ldk), ("sv", ldv)]
                        if h == 0:
                            w0 += kf_free[0]
                        if first_ch:
                            w0 += [("vv", sin_done)]
                            first_ch = False
                        p.op("tensor", lambda e, c=cc, hh=h, b=bb: e.transpose(
                            psb[b][:, 0:128],
                            k_tm[:, c * D + hh * 128:c * D + (hh + 1) * 128],
                            ident_bf[:, :]), waits=w0, incs=[("pe", 1)])
                        if h % 2 == 0:
                            p.op("scalar", lambda e, hh=h, b=bb: e.copy(
                                kf16[:, hh * 128:(hh + 1) * 128],
                                psb[b][:, 0:128]),
                                 waits=[("pe", p.n("pe"))], incs=[("ev", 1)])
                            bbank.set(bb, ("ev", p.n("ev")))
                            k_ev[h] = p.n("ev")
                            k_vv[h] = 0
                        else:
                            p.op("vector", lambda e, hh=h, b=bb: e.tensor_copy(
                                kf16[:, hh * 128:(hh + 1) * 128],
                                psb[b][:, 0:128]),
                                 waits=[("pe", p.n("pe"))], incs=[("vv", 1)])
                            bbank.set(bb, ("vv", p.n("vv")))
                            k_ev[h] = 0
                            k_vv[h] = p.n("vv")
                    mask_vv = {}
                    s_mm = {}
                    for h in range(NH):
                        sbank = h % 2
                        kw = ([("ev", k_ev[h])] if k_ev[h] else
                              [("vv", k_vv[h])])
                        if h == 0:
                            kw += [("sq", ldq)] + qf_free[0]
                        p.op("tensor", lambda e, c=cc, hh=h, b=sbank: e.matmul(
                            psf[b][:, 0:128],
                            kf16[:, hh * 128:(hh + 1) * 128],
                            qf_st[:, hh * 512 + c * 128:
                                  hh * 512 + (c + 1) * 128],
                            start=True, stop=True),
                             waits=kw + fbank.wait(sbank),
                             incs=[("pe", 1)])
                        s_mm[h] = p.n("pe")
                        mw = [("pe", s_mm[h])]
                        if h == 0:
                            mw += smb_free[0] + [("dc", ph2_dc)]
                        p.op("vector", lambda e, hh=h, b=sbank: e.tensor_tensor(
                            smb16[:, hh * 128:(hh + 1) * 128],
                            psf[b][:, 0:128],
                            dmaskT[:, hh * 128:(hh + 1) * 128], ALU.mult),
                             waits=mw, incs=[("vv", 1)])
                        mask_vv[h] = p.n("vv")
                        fbank.set(sbank, ("vv", mask_vv[h]))
                    kf_free[0] = [("pe", s_mm[NH - 1])]
                    p.op("vector", lambda e, c=cc: e.tensor_tensor(
                        qf_st[:, :].rearrange("p (h t) -> p h t", h=NH)[
                            :, :, c * 128:(c + 1) * 128],
                        qf_st[:, :].rearrange("p (h t) -> p h t", h=NH)[
                            :, :, c * 128:(c + 1) * 128],
                        decayq[:, :].rearrange("p (h t) -> p h t", h=NH),
                        ALU.mult),
                         waits=[("pe", s_mm[NH - 1]), ("dc", ph2_dc)],
                         incs=[("vv", 1)])
                    qt_ok = p.n("vv")
                    p.op("scalar", lambda e: e.copy(sbf16[:, :], S_state[:, :]),
                         waits=[("vv", supd_prev[0]), ("pe", inter_prev[0])],
                         incs=[("ev", 1)])
                    snap_ok = p.n("ev")
                    o_mm = {}
                    for h in range(NH):
                        ob = 2 + h % 2
                        p.op("tensor", lambda e, c=cc, hh=h, b=ob: e.matmul(
                            psf[b][:, 0:128],
                            v_tm[:, c * D + hh * 128:c * D + (hh + 1) * 128],
                            smb16[:, hh * 128:(hh + 1) * 128],
                            start=True, stop=False),
                             waits=[("vv", mask_vv[h])] + fbank.wait(ob))
                        p.op("tensor", lambda e, c=cc, hh=h, b=ob: e.matmul(
                            psf[b][:, 0:128],
                            sbf16[:, hh * 128:(hh + 1) * 128],
                            qf_st[:, hh * 512 + c * 128:
                                  hh * 512 + (c + 1) * 128],
                            start=False, stop=True),
                             waits=[("vv", qt_ok), ("ev", snap_ok)]
                             if h == 0 else [],
                             incs=[("pe", 1)])
                        o_mm[h] = p.n("pe")
                        evw = [("pe", o_mm[h])]
                        if cc == 0 and h == 0:
                            evw.append(("dsx", p.n("dsx")))
                        p.op("scalar", lambda e, c=cc, hh=h, b=ob: e.copy(
                            fm_buf[:, hh * 512 + c * 128:
                                   hh * 512 + (c + 1) * 128],
                            psf[b][:, 0:128]),
                             waits=evw, incs=[("ev", 1)])
                        fbank.set(ob, ("ev", p.n("ev")))
                    inter_prev[0] = o_mm[NH - 1]
                    qf_free[0] = [("pe", o_mm[NH - 1])]
                    last_ds = 0
                    for h in range(NH):
                        p.op("scalar", lambda e, c=cc, hh=h: e.activation(
                            smb16[:, hh * 128:(hh + 1) * 128],
                            k_tm[:, c * D + hh * 128:c * D + (hh + 1) * 128],
                            AF.Copy, scale=dk1[:, hh:hh + 1]),
                             waits=[("pe", o_mm[h])], incs=[("ev", 1)])
                        kd_ok = p.n("ev")
                        db = 4 + h % 2
                        p.op("tensor", lambda e, c=cc, hh=h, b=db: e.matmul(
                            psf[b][:, 0:128],
                            smb16[:, hh * 128:(hh + 1) * 128],
                            v_tm[:, c * D + hh * 128:c * D + (hh + 1) * 128],
                            start=True, stop=True),
                             waits=[("ev", kd_ok)] + fbank.wait(db),
                             incs=[("pe", 1)])
                        ds_mm = p.n("pe")
                        last_ds = ds_mm
                        l128 = float(np.float32(lam[h] ** 128))
                        p.op("vector",
                             lambda e, hh=h, b=db, l=l128:
                             e.scalar_tensor_tensor(
                                 S_state[:, hh * 128:(hh + 1) * 128],
                                 S_state[:, hh * 128:(hh + 1) * 128], l,
                                 psf[b][:, 0:128], ALU.mult, ALU.add),
                             waits=[("pe", ds_mm), ("ev", snap_ok)],
                             incs=[("vv", 1)])
                        fbank.set(db, ("vv", p.n("vv")))
                    supd_prev[0] = p.n("vv")
                    smb_free[0] = [("pe", last_ds)]
                attn_ev = p.n("ev")
                qkv_free[0] = ("pe", p.n("pe"))
                if st + 1 < n_st:
                    load_qkv(st + 1)

                for fsl in range(4):
                    slot = fsl % 2
                    wsem = "dwa" if slot == 0 else "dwb"
                    wl = dma_gp(
                        wsl[slot][:, :].rearrange("p (kc f) -> p kc f", kc=NKC),
                        woutT[:, fsl * 512:(fsl + 1) * 512].rearrange(
                            "(kc p) f -> p kc f", p=128),
                        waits=wslot.wait(slot), sem=wsem)
                    for tc in range(4):
                        bank = tc % 4
                        bw = fbank.wait(bank)
                        for dc in range(NKC):
                            p.op("tensor",
                                 lambda e, b=bank, d=dc, t=tc, s=slot:
                                 e.matmul(
                                     psf[b][:, :],
                                     fm_buf[:, d * 512 + t * 128:
                                            d * 512 + (t + 1) * 128],
                                     wsl[s][:, d * 512:(d + 1) * 512],
                                     start=(d == 0), stop=False),
                                 waits=([("ev", attn_ev), (wsem, wl)] + bw)
                                 if dc == 0 else [])
                        p.op("tensor", lambda e, b=bank, f=fsl: e.matmul(
                            psf[b][:, :], ones_bf[0:1, 0:128],
                            boutr[0:1, f * 512:(f + 1) * 512],
                            start=False, stop=True),
                             waits=[("dg", dw2b)], incs=[("pe", 1)])
                        grp = p.n("pe")
                        p.op("vector",
                             lambda e, b=bank, t=tc, f=fsl, xb=x_buf[st % 2]:
                             e.tensor_tensor(
                                 xb[:, t * D + f * 512:t * D + (f + 1) * 512],
                                 psf[b][:, :],
                                 xb[:, t * D + f * 512:t * D + (f + 1) * 512],
                                 ALU.add),
                             waits=[("pe", grp), ("sx", xres_load[st])],
                             incs=[("vv", 1)])
                        fbank.set(bank, ("vv", p.n("vv")))
                    wslot.set(slot, ("pe", p.n("pe")))
                res_vv = p.n("vv")
                dma_sync(
                    out_ext[st * 512:(st + 1) * 512, :].rearrange(
                        "(tc p) d -> p tc d", p=128),
                    x_buf[st % 2][:, :].rearrange("p (tc d) -> p tc d", tc=4),
                    waits=[("vv", res_vv)], sem="ds")
                st_store = p.n("ds")

                ln_tm(lambda tc, b=x_buf[st % 2]: b[:, tc * D:(tc + 1) * D], 4,
                      lambda tc, kc: fm_buf[:, kc * 512 + tc * 128:
                                            kc * 512 + (tc + 1) * 128],
                      [("vv", res_vv)],
                      ts_dst_fn=lambda tc: rt[tc][:, 0:D],
                      ts_waits_fn=lambda tc: rt_free.wait(tc))
                xh_done = p.n("ev")
                ln_vv = p.n("vv")
                xh_stored[st] = dma_sync(
                    xhsc[:, st * 512:(st + 1) * 512].rearrange(
                        "(kc p) t -> p kc t", p=128),
                    fm_buf[:, :].rearrange("p (kc t) -> p kc t", kc=NKC),
                    waits=[("ev", xh_done), ("vv", ln_vv),
                           ("dsx", p.n("dsx"))], sem="dsx")
                xslot.set(0, ("ds", st_store), ("vv", ln_vv),
                          ("ev", xh_done))
                if st + 1 < n_st:
                    load_xres(st + 1)

                for tc in range(4):
                    tg = st * 4 + tc
                    b_r = tc % 2
                    bw = fbank.wait(b_r)
                    for kc in range(NKC):
                        p.op("tensor",
                             lambda e, b=b_r, k=kc, t=tc: e.matmul(
                                 psf[b][:, 0:NR],
                                 fm_buf[:, k * 512 + t * 128:
                                        k * 512 + (t + 1) * 128],
                                 rw_bf[:, k * NR:(k + 1) * NR],
                                 start=(k == 0), stop=False),
                             waits=([("ev", xh_done), ("vv", ln_vv),
                                     ("dg", dg_all[0])] + bw)
                             if kc == 0 else [])
                    p.op("tensor", lambda e, b=b_r: e.matmul(
                        psf[b][:, 0:NR], ones_bf[0:1, 0:128], rbr[0:1, :],
                        start=False, stop=True), incs=[("pe", 1)])
                    rt_mm = p.n("pe")
                    p.op("vector", lambda e, b=b_r: e.reduce_max(
                        stat[:, 48:49], psf[b][:, 0:NR], axis=AX.X,
                        negate=True),
                         waits=[("pe", rt_mm)], incs=[("vv", 1)])
                    nm_ok = p.n("vv")
                    p.op("scalar", lambda e, b=b_r: e.activation(
                        e_buf[:, :], psf[b][:, 0:NR], AF.Exp,
                        bias=stat[:, 48:49]),
                         waits=[("vv", nm_ok)], incs=[("ev", 1)])
                    fbank.set(b_r, ("ev", p.n("ev")))
                    exp_ok = p.n("ev")
                    p.op("vector", lambda e: e.max(out=m8[:, :],
                                                   in_=e_buf[:, :]),
                         waits=[("ev", exp_ok)], incs=[("vv", 1)])
                    p.op("vector", lambda e: e.memset(m8[:, TOPK:8], 0.0),
                         incs=[("vv", 1)])
                    p.op("vector", lambda e: e.match_replace(
                        out=erem[:, :], in_to_replace=m8[:, :],
                        in_values=e_buf[:, :], imm_value=0.0),
                         incs=[("vv", 1)])
                    p.op("vector", lambda e: e.tensor_tensor(
                        gacc[:, :], e_buf[:, :], erem[:, :], ALU.subtract),
                         incs=[("vv", 1)])
                    p.op("vector", lambda e: e.reduce_sum(
                        stat[:, 49:50], gacc[:, :], axis=AX.X),
                         incs=[("vv", 1)])
                    p.op("vector", lambda e: e.reciprocal(
                        stat[:, 50:51], stat[:, 49:50]),
                         incs=[("vv", 1)])
                    p.op("vector", lambda e: e.tensor_scalar_mul(
                        gtmp[:, :], gacc[:, :], stat[:, 50:51]),
                         incs=[("vv", 1)])
                    p.op("vector", lambda e: e.tensor_copy(
                        gbf_tm[:, 0:NR], gtmp[:, :]), incs=[("vv", 1)])
                    g_ok = p.n("vv")
                    b = tr_par[0]; tr_par[0] ^= 1
                    p.op("tensor", lambda e, bb=b: e.transpose(
                        psb[bb][0:NR, 0:128], gbf_tm[:, :], ident_bf[:, :]),
                         waits=[("vv", g_ok)] + bbank.wait(b),
                         incs=[("pe", 1)])
                    p.op("scalar", lambda e, bb=b, t=tg: e.copy(
                        gates_fm[0:nr_red, t * 128:(t + 1) * 128],
                        psb[bb][0:nr_red, 0:128]),
                         waits=[("pe", p.n("pe")), ("gp", gfm_init)],
                         incs=[("ev", 1)])
                    bbank.set(b, ("ev", p.n("ev")))
                    p.op("scalar", lambda e, t=tc: e.copy(
                        rt[t][:, D:D + NR], gtmp[:, :]),
                         waits=[("vv", g_ok)], incs=[("ev", 1)])
                    p.op("vector", lambda e, t=tc: e.tensor_copy(
                        ghif[:, :], rt[t][:, D:D + NR]),
                         waits=[("ev", p.n("ev"))], incs=[("vv", 1)])
                    p.op("vector", lambda e, t=tc: e.tensor_tensor(
                        rt[t][:, D + NR:D + 2 * NR], gtmp[:, :], ghif[:, :],
                        ALU.subtract), incs=[("vv", 1)])
                    p.op("scalar", lambda e, t=tc, g=tg: e.copy(
                        rt[t][:, D + 2 * NR:D + 2 * NR + 1],
                        tokid[:, g:g + 1]),
                         waits=[("dc", dc_all[0])], incs=[("ev", 1)])
                    p.op("vector", lambda e, t=tc: e.tensor_copy(
                        tidf[:, :], rt[t][:, D + 2 * NR:D + 2 * NR + 1]),
                         waits=[("ev", p.n("ev"))], incs=[("vv", 1)])
                    p.op("vector", lambda e, t=tc, g=tg: e.tensor_tensor(
                        rt[t][:, D + 2 * NR + 1:D + 2 * NR + 2],
                        tokid[:, g:g + 1], tidf[:, :], ALU.subtract),
                         incs=[("vv", 1)])
                    row_ev = p.n("ev")
                    p.op("vector", lambda e: e.tensor_single_scalar(
                        mk_f[:, :], gacc[:, :], 0.0, ALU.is_gt),
                         incs=[("vv", 1)])
                    p.op("vector", lambda e: e.tensor_copy(
                        mk_b[:, :], mk_f[:, :]), incs=[("vv", 1)])
                    mk_ok = p.n("vv")
                    b_p = 2 + tc % 2
                    bwp = fbank.wait(b_p)
                    p.op("tensor", lambda e, b=b_p: e.matmul(
                        psf[b][:, 0:NR], triu_sb[:, :], mk_b[:, :],
                        start=True, stop=True),
                         waits=[("vv", mk_ok), ("dc", dc_all[0])] + bwp,
                         incs=[("pe", 1)])
                    pcum_mm = p.n("pe")
                    b_c = 4 + tc % 2
                    bwc = fbank.wait(b_c)
                    p.op("tensor", lambda e, b=b_c: e.matmul(
                        psf[b][0:1, 0:NR], onesc_bf[:, :], mk_b[:, :],
                        start=True, stop=True),
                         waits=[("vv", mk_ok), ("gp", setup_gp)] + bwc,
                         incs=[("pe", 1)])
                    cnt_mm = p.n("pe")
                    p.op("vector", lambda e, b=b_p: e.tensor_tensor(
                        pe_t[:, :], psf[b][:, 0:NR], offs_full[:, :],
                        ALU.add),
                         waits=[("pe", pcum_mm)], incs=[("vv", 1)])
                    fbank.set(b_p, ("vv", p.n("vv")))
                    p.op("vector", lambda e: e.tensor_single_scalar(
                        ov_t[:, :], pe_t[:, :], CAP - 0.5, ALU.is_ge),
                         incs=[("vv", 1)])
                    p.op("vector", lambda e: e.tensor_scalar(
                        ov_t[:, :], ov_t[:, :], -1.0, 1.0, ALU.mult, ALU.add),
                         incs=[("vv", 1)])
                    p.op("vector", lambda e: e.tensor_tensor(
                        vd_t[:, :], mk_f[:, :], ov_t[:, :], ALU.mult),
                         incs=[("vv", 1)])
                    p.op("vector", lambda e: e.tensor_tensor(
                        pe_t[:, :], pe_t[:, :], eCrow[:, :], ALU.add),
                         waits=[("dc", dc_all[0])], incs=[("vv", 1)])
                    p.op("vector", lambda e, g=tg: e.tensor_single_scalar(
                        tr_t[:, :], trashb[:, :], float(g * NR),
                        ALU.add), incs=[("vv", 1)])
                    p.op("vector", lambda e: e.tensor_tensor(
                        pe_t[:, :], vd_t[:, :], pe_t[:, :], ALU.mult),
                         incs=[("vv", 1)])
                    p.op("vector", lambda e: e.tensor_scalar(
                        vd_t[:, :], vd_t[:, :], -1.0, 1.0, ALU.mult, ALU.add),
                         incs=[("vv", 1)])
                    p.op("vector", lambda e: e.tensor_tensor(
                        tr_t[:, :], vd_t[:, :], tr_t[:, :], ALU.mult),
                         incs=[("vv", 1)])
                    p.op("vector", lambda e: e.tensor_tensor(
                        of_t[:, :], pe_t[:, :], tr_t[:, :], ALU.add),
                         incs=[("vv", 1)])
                    p.op("vector", lambda e, t=tc: e.tensor_copy(
                        oint[t][:, :], of_t[:, :]),
                         waits=oint_free.wait(tc), incs=[("vv", 1)])
                    oint_ok = p.n("vv")
                    p.op("scalar", lambda e, b=b_c: e.copy(
                        cntb[0:1, :], psf[b][0:1, 0:NR]),
                         waits=[("pe", cnt_mm)], incs=[("ev", 1)])
                    cnt_ev = p.n("ev")
                    fbank.set(b_c, ("ev", cnt_ev))
                    b_c2 = 4 + (tc + 1) % 2
                    bwc2 = fbank.wait(b_c2)
                    p.op("tensor", lambda e, b=b_c2: e.matmul(
                        psf[b][:, 0:NR], ones_bf[0:1, 0:128], cntb[0:1, :],
                        start=True, stop=True),
                         waits=[("ev", cnt_ev)] + bwc2, incs=[("pe", 1)])
                    cb_mm = p.n("pe")
                    p.op("vector", lambda e, b=b_c2: e.tensor_tensor(
                        offs_full[:, :], offs_full[:, :], psf[b][:, 0:NR],
                        ALU.add),
                         waits=[("pe", cb_mm)], incs=[("vv", 1)])
                    fbank.set(b_c2, ("vv", p.n("vv")))
                    sc_w = [("vv", oint_ok), ("ev", row_ev), ("dz", dz_all)]
                    for re_ in range(nr_red):
                        p.op("gpsimd",
                             lambda e, t=tc, ee=re_: e.indirect_dma_start(
                                 out=xg_dram[:, :],
                                 out_offset=IndirectOffsetOnAxis(
                                     oint[t][:, ee:ee + 1], 0),
                                 in_=rt[t][:, :], in_offset=None,
                                 oob_is_err=False),
                             waits=sc_w if re_ == 0 else [],
                             incs=[("dsc", 16)])
                for _tc in range(4):
                    rt_free.set(_tc, ("dsc", p.n("dsc")))
                    oint_free.set(_tc, ("dsc", p.n("dsc")))
            xmid_all = p.n("ds")
            gates_ev = p.n("ev")
            gates_vv = p.n("vv")
            dsc_all = p.n("dsc")
            xh_stored_all = dict(xh_stored)

            p.barrier()
            ph2_scope.close()

        with ExitStack() as fsco:
            def fsb(name, shape, dt=F32):
                return fsco.enter_context(nc.sbuf_tensor(name, shape, dt))

            from concourse.bass_core_rust import IndirectOffsetOnAxis
            b2g = fsb("b2g_sb", [n_exp, D], BF16)
            b1c = fsb("b1c", [128, n_exp * NHC])
            dma_gp(b2g[:, :], b2g_in[:, :])
            dg_all[0] = p.n("dg")
            b1c_ld = dma_sync(b1c[:, :], b1cols_in[:, :])
            xg_tm = [fsb(f"xg_tm{i}", [128, 4 * ROWW], BF16) for i in range(2)]
            xg_fm = fsb("xg_fm", [128, NKC * 512], BF16)
            h_sb = fsb("h_sb", [128, NHC * 512], BF16)
            w1_sb = fsb("w1_sb", [128, NKC * HID], BF16)
            w2_sb = fsb("w2_sb", [128, NHC * D], BF16)
            contrib = [fsb(f"contrib{i}", [128, D]) for i in range(2)]
            gcol = fsb("gcol", [128, 8])
            tid_i = fsb("tid_i", [128, 8], I32)
            xq_scope = ExitStack()
            xq_buf = xq_scope.enter_context(
                nc.sbuf_tensor("xq_buf", [128, NKC * 512], BF16))

            ctr = Slots(4)
            first_accum = [True]

            def accum_waits(wait_ev):
                w = [("ev", wait_ev), ("da", p.n("da"))]
                if first_accum[0]:
                    w.append(("ds", xmid_all))
                    first_accum[0] = False
                return w

            p.op("vector", lambda e: e.tensor_scalar_mul(
                b2g[:, :], b2g[:, :], 2.0),
                 waits=[("dg", dg_all[0])], incs=[("vv", 1)])
            b2x2 = p.n("vv")

            w1_loaded = [0]
            w2_loaded = [0]
            order = [nr_red, nr_red + 1] + list(range(nr_red))

            def load_w1(e_, wait_pe):
                dma_gp(w1_sb[:, :].rearrange("p (kc h) -> p kc h", kc=NKC),
                       w1_all[e_].rearrange("(kc p) h -> p kc h", p=128),
                       waits=[("pe", wait_pe)], sem="dw1")
                w1_loaded[0] = p.n("dw1")

            def load_w2(e_, wait_pe):
                dma_gp(w2_sb[:, :].rearrange("p (hc f) -> p hc f", hc=NHC),
                       w2_all[e_].rearrange("(hc p) f -> p hc f", p=128),
                       waits=[("pe", wait_pe)], sem="dw2")
                w2_loaded[0] = p.n("dw2")

            load_w1(order[0], 0)
            load_w2(order[0], 0)

            xq_load = [0]
            xq_free = [("pe", 0)]

            def load_xq(q_):
                xq_load[0] = dma_sync(
                    xq_buf[:, :].rearrange("p (kc t) -> p kc t", kc=NKC),
                    xhsc[:, q_ * 512:(q_ + 1) * 512].rearrange(
                        "(kc p) t -> p kc t", p=128),
                    waits=[xq_free[0], ("dsx", xh_stored_all[q_])],
                    sem="sxq0")

            for si, se in enumerate(order[:2]):
                w1_this = w1_loaded[0]
                w2_this = w2_loaded[0]
                for q in range(n_st):
                    load_xq(q)
                    xl = xq_load[0]
                    for hc in range(NHC):
                        bank = hc % 2
                        bw = fbank.wait(bank)
                        for kc in range(NKC):
                            p.op("tensor",
                                 lambda e, b=bank, hh=hc, k=kc: e.matmul(
                                     psf[b][:, :],
                                     w1_sb[:, k * HID + hh * 128:
                                           k * HID + (hh + 1) * 128],
                                     xq_buf[:, k * 512:(k + 1) * 512],
                                     start=(k == 0), stop=(k == NKC - 1)),
                                 waits=([("dw1", w1_this), ("sxq0", xl)] + bw)
                                 if kc == 0 else [],
                                 incs=([("pe", 1)] if kc == NKC - 1 else []))
                        grp = p.n("pe")
                        p.op("scalar",
                             lambda e, b=bank, hh=hc, ee=se: e.activation(
                                 h_sb[:, hh * 512:(hh + 1) * 512],
                                 psf[b][:, :], AF.Relu,
                                 bias=b1c[:, ee * NHC + hh:
                                          ee * NHC + hh + 1]),
                             waits=[("pe", grp), ("dc", b1c_ld)],
                             incs=[("ev", 1)])
                        fbank.set(bank, ("ev", p.n("ev")))
                    h_ev_end = p.n("ev")
                    xq_free[0] = ("pe", p.n("pe"))
                    for tc in range(4):
                        tg = q * 4 + tc
                        cslot = tg % 2
                        cw = ctr.wait(cslot)
                        for fs in range(4):
                            bank = 2 + (tc * 4 + fs) % 2
                            bw = fbank.wait(bank)
                            if si == 0:
                                p.op("tensor",
                                     lambda e, b=bank, t=tg, f=fs: e.matmul(
                                         psf[b][:, :],
                                         gates_fm[:, t * 128:(t + 1) * 128],
                                         b2g[:, f * 512:(f + 1) * 512],
                                         start=True, stop=False),
                                     waits=[("dw2", w2_this),
                                            ("ev", h_ev_end),
                                            ("vv", b2x2), ("ev", gates_ev),
                                            ("gp", gfm_init)] + bw)
                            for hc in range(NHC):
                                p.op("tensor",
                                     lambda e, b=bank, hh=hc, t=tc, f=fs,
                                     fst=(si != 0):
                                     e.matmul(
                                         psf[b][:, :],
                                         h_sb[:, hh * 512 + t * 128:
                                              hh * 512 + (t + 1) * 128],
                                         w2_sb[:, hh * D + f * 512:
                                               hh * D + (f + 1) * 512],
                                         start=(hh == 0 and fst),
                                         stop=(hh == NHC - 1)),
                                     waits=([("dw2", w2_this),
                                             ("ev", h_ev_end)] + bw)
                                     if (hc == 0 and si != 0) else [],
                                     incs=([("pe", 1)] if hc == NHC - 1
                                           else []))
                            grp = p.n("pe")
                            p.op("scalar", lambda e, b=bank, s=cslot, f=fs:
                                 e.activation(
                                     contrib[s][:, f * 512:(f + 1) * 512],
                                     psf[b][:, :], AF.Copy, scale=0.5),
                                 waits=[("pe", grp)] + (cw if fs == 0 else []),
                                 incs=[("ev", 1)])
                            fbank.set(bank, ("ev", p.n("ev")))
                        p.op("gpsimd",
                             lambda e, q2=q, t2=tc, s=cslot: e.dma_start(
                                 out=out_ext[q2 * 512 + t2 * 128:
                                             q2 * 512 + (t2 + 1) * 128, :],
                                 in_=contrib[s][:, :], accum_op=ALU.add),
                             waits=accum_waits(p.n("ev")), incs=[("da", 16)])
                        ctr.set(cslot, ("da", p.n("da")))
                    if q == n_st - 1:
                        load_w1(order[si + 1], p.n("pe"))
                z_pe_end = p.n("pe")
                load_w2(order[si + 1], z_pe_end) if si + 1 < len(order) else None

            xq_scope.close()
            contrib += [fsb(f"contrib{i}", [128, D]) for i in (2, 3)]
            xg_load = {0: 0, 1: 0}
            xg_free = Slots(2)
            xgfm_free = [0]
            slot_da = {0: 0, 1: 0}

            for ri, re_ in enumerate(order[2:]):
                slot = ri % 2
                gsem = "sgx0" if slot == 0 else "sgx1"
                xg_load[slot] = dma_sync(
                    xg_tm[slot][:, :].rearrange("p (tc f) -> p tc f", tc=4),
                    xg_dram[re_ * CAP:(re_ + 1) * CAP, :].rearrange(
                        "(tc p) f -> p tc f", p=128),
                    waits=[("dsc", dsc_all)] + xg_free.wait(slot), sem=gsem)
                gl = xg_load[slot]
                w1_this = w1_loaded[0]
                w2_this = w2_loaded[0]
                for tc in range(4):
                    p.op("vector", lambda e, s=slot, t=tc, ee=re_:
                         e.tensor_tensor(
                             gcol[:, s * 4 + t:s * 4 + t + 1],
                             xg_tm[s][:, t * ROWW + D + ee:
                                      t * ROWW + D + ee + 1],
                             xg_tm[s][:, t * ROWW + D + NR + ee:
                                      t * ROWW + D + NR + ee + 1],
                             ALU.add),
                         waits=([(gsem, gl), ("da", slot_da[slot])]
                                if tc == 0 else [(gsem, gl)]),
                         incs=[("vv", 1)])
                    p.op("vector", lambda e, s=slot, t=tc: e.tensor_tensor(
                        tidf[:, :],
                        xg_tm[s][:, t * ROWW + D + 2 * NR:
                                 t * ROWW + D + 2 * NR + 1],
                        xg_tm[s][:, t * ROWW + D + 2 * NR + 1:
                                 t * ROWW + D + 2 * NR + 2],
                        ALU.add), incs=[("vv", 1)])
                    p.op("vector", lambda e, s=slot, t=tc: e.tensor_copy(
                        tid_i[:, s * 4 + t:s * 4 + t + 1], tidf[:, :]),
                         incs=[("vv", 1)])
                gt_ok = p.n("vv")
                for tc in range(4):
                    for kc in range(NKC):
                        transpose_128(
                            xg_tm[slot][:, tc * ROWW + kc * 128:
                                        tc * ROWW + (kc + 1) * 128],
                            xg_fm[:, kc * 512 + tc * 128:
                                  kc * 512 + (tc + 1) * 128],
                            [(gsem, gl), ("pe", xgfm_free[0])] if
                            (tc == 0 and kc == 0) else [(gsem, gl)])
                tr_ev = p.n("ev")
                tr_vv = p.n("vv")
                for hc in range(NHC):
                    bank = hc % 2
                    bw = fbank.wait(bank)
                    for kc in range(NKC):
                        p.op("tensor",
                             lambda e, b=bank, hh=hc, k=kc: e.matmul(
                                 psf[b][:, :],
                                 w1_sb[:, k * HID + hh * 128:
                                       k * HID + (hh + 1) * 128],
                                 xg_fm[:, k * 512:(k + 1) * 512],
                                 start=(k == 0), stop=(k == NKC - 1)),
                             waits=([("dw1", w1_this), ("ev", tr_ev),
                                     ("vv", tr_vv)] + bw)
                             if kc == 0 else [],
                             incs=([("pe", 1)] if kc == NKC - 1 else []))
                    grp = p.n("pe")
                    p.op("scalar",
                         lambda e, b=bank, hh=hc, ee=re_: e.activation(
                             h_sb[:, hh * 512:(hh + 1) * 512],
                             psf[b][:, :], AF.Relu,
                             bias=b1c[:, ee * NHC + hh:ee * NHC + hh + 1]),
                         waits=[("pe", grp), ("dc", b1c_ld)],
                         incs=[("ev", 1)])
                    fbank.set(bank, ("ev", p.n("ev")))
                h_ev_end = p.n("ev")
                h_pe_end = p.n("pe")
                xgfm_free[0] = h_pe_end
                xg_free.set(slot, ("pe", h_pe_end))
                if ri + 1 < nr_red:
                    load_w1(order[2 + ri + 1], h_pe_end)
                for tc in range(4):
                    cslot = tc
                    cw = ctr.wait(cslot)
                    for fs in range(4):
                        bank = 2 + (tc * 4 + fs) % 2
                        bw = fbank.wait(bank)
                        for hc in range(NHC):
                            p.op("tensor",
                                 lambda e, b=bank, hh=hc, t=tc, f=fs:
                                 e.matmul(
                                     psf[b][:, :],
                                     h_sb[:, hh * 512 + t * 128:
                                          hh * 512 + (t + 1) * 128],
                                     w2_sb[:, hh * D + f * 512:
                                           hh * D + (f + 1) * 512],
                                     start=(hh == 0), stop=(hh == NHC - 1)),
                                 waits=([("dw2", w2_this),
                                         ("ev", h_ev_end)] + bw)
                                 if hc == 0 else [],
                                 incs=([("pe", 1)] if hc == NHC - 1 else []))
                        grp = p.n("pe")
                        p.op("scalar",
                             lambda e, b=bank, s=cslot, f=fs, sl=slot, t=tc:
                             e.activation(
                                 contrib[s][:, f * 512:(f + 1) * 512],
                                 psf[b][:, :], AF.Copy,
                                 scale=gcol[:, sl * 4 + t:sl * 4 + t + 1]),
                             waits=[("pe", grp), ("vv", gt_ok)] +
                             (cw if fs == 0 else []),
                             incs=[("ev", 1)])
                        fbank.set(bank, ("ev", p.n("ev")))
                    p.op("gpsimd",
                         lambda e, s=cslot, sl=slot, t=tc: e.indirect_dma_start(
                             out=out_ext[:, :],
                             out_offset=IndirectOffsetOnAxis(
                                 tid_i[:, sl * 4 + t:sl * 4 + t + 1], 0),
                             in_=contrib[s][:, :], in_offset=None,
                             oob_is_err=False, compute_op=ALU.add),
                         waits=accum_waits(p.n("ev")) + [("vv", gt_ok)],
                         incs=[("da", 16)])
                    ctr.set(cslot, ("da", p.n("da")))
                z_pe_end = p.n("pe")
                slot_da[slot] = p.n("da")
                if ri + 1 < nr_red:
                    load_w2(order[2 + ri + 1], z_pe_end)

            p.op("gpsimd", lambda e: e.nop(),
                 waits=[("da", p.n("da")), ("dw1", p.n("dw1")),
                        ("dw2", p.n("dw2")), ("dg", p.n("dg")),
                        ("dwa", p.n("dwa")), ("dwb", p.n("dwb")),
                        ("dsc", p.n("dsc"))])
            p.op("sync", lambda e: e.nop(),
                 waits=[("ds", p.n("ds")), ("dsx", p.n("dsx")),
                        ("sx", p.n("sx")), ("dz", p.n("dz")),
                        ("sxq0", p.n("sxq0")), ("sgx0", p.n("sgx0")),
                        ("sgx1", p.n("sgx1"))])

        p.emit(nc, sems)

    return nc


def _host_constants(cfg):
    n_st = cfg["n_st"]
    lam = np.exp(-(2.0 ** (-0.5)) ** (np.arange(1, NH + 1, dtype=np.float64)))
    i_idx = np.arange(128, dtype=np.float64)
    dmaskT = np.zeros((128, NH * 128))
    decayq = np.zeros((128, NH * 128))
    dkx = np.zeros((128, NH * 4))
    dk1 = np.zeros((128, NH))
    for h in range(NH):
        diff = i_idx[None, :] - i_idx[:, None]
        with np.errstate(over="ignore", under="ignore"):
            m = np.where(diff >= 0, lam[h] ** diff, 0.0)
            dmaskT[:, h * 128:(h + 1) * 128] = m
            decayq[:, h * 128:(h + 1) * 128] = (lam[h] ** (i_idx + 1.0))[None, :]
            dk1[:, h] = lam[h] ** (127.0 - i_idx)
            for cc in range(4):
                dkx[:, h * 4 + cc] = lam[h] ** (127.0 - i_idx + 128.0 * (3 - cc))
    NTC = 4 * n_st
    n_exp = cfg["n_exp"]
    nr_red = n_exp - 2
    consts = dict(
        dmaskT=dmaskT.astype(np.float32),
        decayq=decayq.astype(np.float32),
        dkx=dkx.astype(np.float32),
        dk1=dk1.astype(np.float32),
        ident_bf=np.eye(128).astype(ml_dtypes.bfloat16),
        triu=np.triu(np.ones((128, 128)), k=1).astype(ml_dtypes.bfloat16),
        eCrow=np.tile((np.arange(NR) * CAP).astype(np.float32), (128, 1)),
        trashb=np.tile((nr_red * CAP + np.arange(NR)).astype(np.float32),
                       (128, 1)),
        tokid=(np.arange(NTC)[None, :] * 128 +
               np.arange(128)[:, None]).astype(np.float32),
    )
    sinw = []
    span = 512 * n_st
    for c in range(NCORES):
        w = np.zeros((128, NCORES * NH))
        r0 = c % 4
        for g in range(NCORES):
            if g // 4 != c // 4:
                continue
            rgl = g % 4
            if rgl < r0:
                for h in range(NH):
                    with np.errstate(over="ignore", under="ignore"):
                        w[:, g * NH + h] = lam[h] ** (span * (r0 - 1 - rgl))
        sinw.append(w.astype(np.float32))
    return consts, sinw


def make_in_maps(inputs, cfg):
    n_st = cfg["n_st"]
    n_exp = cfg["n_exp"]
    T = 512 * n_st
    x = np.asarray(inputs["L0_attn_x"], dtype=np.float32).reshape(-1, D)
    wqkvT = np.ascontiguousarray(
        np.asarray(inputs["L0_attn_w_qkv"], dtype=np.float32).T)
    woutT = np.ascontiguousarray(
        np.asarray(inputs["L0_attn_w_out"], dtype=np.float32).T)
    b_qkv = np.asarray(inputs["L0_attn_b_qkv"], dtype=np.float32)
    b_out = np.asarray(inputs["L0_attn_b_out"], dtype=np.float32)
    sw1 = np.asarray(inputs["L0_ffn_sw1"], dtype=np.float32)
    sb1 = np.asarray(inputs["L0_ffn_sb1"], dtype=np.float32)
    sw2 = np.asarray(inputs["L0_ffn_sw2"], dtype=np.float32)
    sb2 = np.asarray(inputs["L0_ffn_sb2"], dtype=np.float32)
    rw1 = np.asarray(inputs["L0_ffn_rw1"], dtype=np.float32)
    rb1 = np.asarray(inputs["L0_ffn_rb1"], dtype=np.float32)
    rw2 = np.asarray(inputs["L0_ffn_rw2"], dtype=np.float32)
    rb2 = np.asarray(inputs["L0_ffn_rb2"], dtype=np.float32)
    router_w = np.asarray(inputs["L0_ffn_router_w"], dtype=np.float32)
    router_b = np.asarray(inputs["L0_ffn_router_b"], dtype=np.float32)

    nr_red = n_exp - 2
    w1_all = np.ascontiguousarray(np.concatenate([rw1[:nr_red], sw1], axis=0))
    w2_all = np.ascontiguousarray(np.concatenate([rw2[:nr_red], sw2], axis=0))
    b1_all = np.concatenate([rb1[:nr_red], sb1], axis=0)
    b1cols = np.ascontiguousarray(b1_all.reshape(n_exp * NHC, 128).T)
    b2stack = np.concatenate([rb2[:nr_red], sb2], axis=0).astype(
        ml_dtypes.bfloat16)

    consts, sinw = _host_constants(cfg)
    shared = dict(
        wqkvT=wqkvT, woutT=woutT, w1_all=w1_all, w2_all=w2_all,
        routerw=router_w,
        bq_row=b_qkv[None, :].astype(ml_dtypes.bfloat16),
        bqcol=np.ascontiguousarray(b_qkv[0:D].reshape(NKC, 128).T),
        bout_row=b_out[None, :].astype(ml_dtypes.bfloat16),
        routerb_row=router_b[None, :].astype(ml_dtypes.bfloat16),
        b2g=b2stack, b1cols=b1cols, **consts)
    in_maps = []
    for c in range(NCORES):
        m = dict(shared)
        m["x"] = np.ascontiguousarray(x[c * T:(c + 1) * T])
        m["sinw"] = sinw[c]
        in_maps.append(m)
    return in_maps


def kernel(**inputs):
    cfg = default_cfg()
    in_maps = make_in_maps(inputs, cfg)
    nc = build_nc(cfg)
    res = run_bass_kernel_spmd(nc, in_maps, core_ids=list(range(NCORES)))
    out = np.concatenate([np.asarray(r["out"]) for r in res.results], axis=0)
    return out.reshape(2, 8192, D).astype(np.float32)
